# version 01
# baseline (speedup 1.0000x reference)
import numpy as np
import ml_dtypes
from contextlib import ExitStack

import concourse.bass as bass
import concourse.mybir as mybir
from concourse.bass_types import AP
from concourse.bass_utils import run_bass_kernel_spmd

F32 = mybir.dt.float32
BF16 = mybir.dt.bfloat16
FP8 = mybir.dt.float8e4
I32 = mybir.dt.int32
I16 = mybir.dt.int16
ALU = mybir.AluOpType
ACTF = mybir.ActivationFunctionType
AX = mybir.AxisListType.X

L, D, KD, INNER, TT = 4096, 1024, 32, 4096, 128
ND = D // 128
NI = INNER // 512
NK = INNER // 128
EPS = 1e-5
STP, SKB = 288, 127


class Sem:
    def __init__(self, ctx, nc, name):
        self.h = ctx.enter_context(nc.semaphore(name))
        self.n = 0

    def inc(self, ins, k=1):
        ins.then_inc(self.h, k)
        self.n += k
        return self.n


def build_nc(nt=L // TT, dbg=False, phase_b=True, prep=True, astage=99, serial=False):
    nc = bass.Bass("TRN2", target_bir_lowering=False, detect_race_conditions=False)
    Ltot = nt * TT

    x_h = nc.dram_tensor("x", [Ltot, D], F32, kind="ExternalInput").ap()
    wt_h = nc.dram_tensor("Wt", [D, KD], F32, kind="ExternalInput").ap()
    bb_h = nc.dram_tensor("b_bc", [128, KD], F32, kind="ExternalInput").ap()
    wg_h = nc.dram_tensor("wgT", [D, INNER], F32, kind="ExternalInput").ap()
    wu_h = nc.dram_tensor("wuT", [D, INNER], F32, kind="ExternalInput").ap()
    wd_h = nc.dram_tensor("wdT", [INNER, D], F32, kind="ExternalInput").ap()
    id_h = nc.dram_tensor("ident", [128, 128], BF16, kind="ExternalInput").ap()
    out_h = nc.dram_tensor("out", [Ltot, D], F32, kind="ExternalOutput").ap()
    xp_h = nc.dram_tensor("xprime", [Ltot, D], F32,
                          kind="ExternalOutput" if dbg else "Internal").ap()
    stage_h = nc.dram_tensor("stage", [2, 128, STP], F32, kind="Internal").ap()

    ctx = ExitStack()
    with ctx:
        sbuf = lambda name, shape, dt: ctx.enter_context(nc.sbuf_tensor(name, shape, dt))
        wq_g = sbuf("wq_g", [128, ND * INNER], FP8)
        wq_u = sbuf("wq_u", [128, ND * INNER], FP8)
        wq_d = sbuf("wq_d", [128, NK * D], FP8)
        wbuf = [sbuf(f"wbuf{k}", [128, 2048], F32) for k in range(2)]
        tlt = [sbuf(f"tlt{k}", [128, 2048], BF16) for k in range(2)]
        rs_abs = sbuf("rs_abs", [128, 64], F32)
        thmw = [sbuf(f"thmw{m}", [128, 2], F32) for m in range(3)]
        negth = [sbuf(f"negth{m}", [128, 1], F32) for m in range(3)]
        vec2 = sbuf("vec2", [1, 2], F32)
        tot1 = sbuf("tot1", [1, 1], F32)
        ones_c = sbuf("ones_c", [128, 1], F32)
        ones_r = sbuf("ones_r", [1, 128], F32)
        identb = sbuf("identb", [128, 128], BF16)
        wtb = sbuf("wtb", [128, ND * KD], BF16)
        bbc = sbuf("bbc", [128, KD], F32)
        kap = sbuf("kap", [128, 1], F32)
        ceps = sbuf("ceps", [128, 1], F32)
        c15 = sbuf("c15", [128, 1], F32)
        ci32 = sbuf("ci32", [128, 1], I32)
        xb = [sbuf(f"xb{k}", [128, D], F32) for k in range(2)]
        xpr = [sbuf(f"xpr{k}", [128, D], F32) for k in range(2)]
        outb = xpr
        ybf = [sbuf(f"ybf{k}", [128, D], BF16) for k in range(2)]
        yT = [sbuf(f"yT{k}", [128, D], BF16) for k in range(2)]
        dump = sbuf("dump", [128, D], F32)
        lbs = [sbuf(f"lb{k}", [128, KD], F32) for k in range(2)]
        ews = [sbuf(f"ew{k}", [128, KD], F32) for k in range(2)]
        wns = [sbuf(f"wn{k}", [128, KD], F32) for k in range(2)]
        skbb = [sbuf(f"skb{k}", [128, 160], BF16) for k in range(2)]
        A1s = [sbuf(f"A1_{k}", [128, 128], BF16) for k in range(2)]
        A2s = [sbuf(f"A2_{k}", [32, 128], BF16) for k in range(2)]
        xq32 = sbuf("xq32", [128, D], I16)
        xqbf = [sbuf(f"xqbf{k}", [128, D], BF16) for k in range(2)]
        xqT = [sbuf(f"xqT{k}", [128, D], BF16) for k in range(2)]
        hbuf = sbuf("hbuf", [128, INNER], F32)
        hq32 = sbuf("hq32", [128, INNER], I16)
        hqbf = sbuf("hqbf", [128, INNER], BF16)
        hqT = sbuf("hqT", [128, INNER], BF16)
        gtmp = [sbuf(f"gtmp{k}", [128, 512], F32) for k in range(2)]
        mxh8 = [sbuf(f"mxh8{k}", [128, 8], F32) for k in range(2)]
        names = ["rs", "m", "var", "sd", "r", "nmr", "mx", "nmx", "ssq", "esum",
                 "rsum", "rmax", "rmin", "t1", "t2", "mxd", "mxy", "mxc", "sinv",
                 "s", "dqx", "scomb", "t3", "bcomb", "dqg", "dqu", "mxh", "shy",
                 "shinv", "sh", "dqh", "dqd", "ss2", "tred"]
        STbig = sbuf("STbig", [128, 2 * len(names)], F32)
        ST = {n: STbig[:, 2 * k:2 * k + 2] for k, n in enumerate(names)}

        def sv(n, i):
            k = names.index(n)
            return STbig[:, 2 * k + (i % 2):2 * k + (i % 2) + 1]

        ptb = [ctx.enter_context(nc.psum_tensor(f"ptb{k}", [128, 128], BF16))
               for k in range(2)]
        P512 = [ctx.enter_context(nc.psum_tensor(f"p512_{k}", [128, 512], F32))
                for k in range(4)]
        P1024 = ctx.enter_context(nc.psum_tensor("p1024", [128, 1024], F32))
        pprep = P512[3]
        pl = P512[0][:, 0:KD]
        pmix = P1024
        pd = P1024

        if dbg == 4:
            dbg4_A1 = nc.dram_tensor("dbg4_A1", [nt, 128, 128], BF16, kind="ExternalOutput").ap()
            dbg4_skb = nc.dram_tensor("dbg4_skb", [nt, 128, 160], BF16, kind="ExternalOutput").ap()
            dbg4_wn = nc.dram_tensor("dbg4_wn", [nt, 128, KD], F32, kind="ExternalOutput").ap()

        ld = Sem(ctx, nc, "ld")
        skw = Sem(ctx, nc, "skw")
        skr = Sem(ctx, nc, "skr")
        sst = Sem(ctx, nc, "sst")
        v = Sem(ctx, nc, "v")
        a = Sem(ctx, nc, "a")
        t = Sem(ctx, nc, "t")
        g = Sem(ctx, nc, "g")
        gld = Sem(ctx, nc, "gld")
        blk = ctx.enter_context(nc.Block())

        tile_done = [None] * (nt + 1)
        bank_mark = [(v, 0), (a, 0)]
        PEW = 4

        def pemm(tensor, ins):
            t.inc(ins)

        def vguard(vector, ins):
            v.inc(ins)
            vector.wait_ge(v.h, v.n)

        def pewait(tensor):
            if t.n >= PEW:
                tensor.wait_ge(t.h, t.n - (PEW - 1))

        def tr_group(n, src_fn, dst_fn):
            T0 = t.n
            V0, A0 = v.n, a.n
            nb0, nb1 = bank_mark[0][1], bank_mark[1][1]

            @blk.tensor
            def _(tensor):
                for k in range(n):
                    if k == 0 and nb0 > 0:
                        tensor.wait_ge(v.h, nb0)
                    if k == 1 and nb1 > 0:
                        tensor.wait_ge(a.h, nb1)
                    if k >= 2:
                        if k % 2 == 0:
                            tensor.wait_ge(v.h, V0 + k // 2)
                        else:
                            tensor.wait_ge(a.h, A0 + k // 2)
                    pewait(tensor)
                    t.inc(tensor.transpose(ptb[k % 2][0:src_fn(k).shape[1], :],
                                           src_fn(k), identb[:]))

            @blk.vector
            def _(vector):
                for k in range(0, n, 2):
                    vector.wait_ge(t.h, T0 + k + 1)
                    v.inc(vector.tensor_copy(dst_fn(k),
                                             ptb[0][0:src_fn(k).shape[1], :]))

            @blk.scalar
            def _(scalar):
                for k in range(1, n, 2):
                    scalar.wait_ge(t.h, T0 + k + 1)
                    a.inc(scalar.activation(dst_fn(k),
                                            ptb[1][0:src_fn(k).shape[1], :],
                                            ACTF.Copy, bias=0.0, scale=1.0))
            bank_mark[0] = (v, v.n)
            bank_mark[1] = (a, a.n)
            return v.n, a.n

        @blk.sync
        def _(sync):
            sync.dma_start(out=identb[:], in_=id_h[:]).then_inc(ld.h, 16)
            sync.dma_start(out=bbc[:], in_=bb_h[:]).then_inc(ld.h, 16)
            ld.n += 32

        @blk.gpsimd
        def _(gpsimd):
            for d in range(ND):
                gpsimd.dma_start(out=wtb[:, KD * d:KD * (d + 1)],
                                 in_=wt_h[128 * d:128 * (d + 1), :]).then_inc(gld.h, 16)
            gld.n += 16 * ND
        ld_const = ld.n
        gld_const = gld.n

        @blk.vector
        def _(vector):
            vector.memset(c15[:], 1.5)
            vector.memset(ceps[:], EPS)
            vector.memset(ones_c[:], 1.0)
            vector.memset(ones_r[:], 1.0)
            vguard(vector, vector.memset(hbuf[:, 0:STP], 0.0))
            vguard(vector, vector.tensor_copy(ci32[:], c15[:]))
            vguard(vector, vector.tensor_scalar(kap[:], ci32[:], -0.5, 129.0,
                                                ALU.mult, ALU.add))
            vector.wait_ge(ld.h, ld_const)
            vector.wait_ge(gld.h, gld_const)
            v.inc(vector.engine_nop())
        v_const = v.n

        @blk.sync
        def _(sync):
            sync.wait_ge(v.h, v_const)
            sync.dma_start(out=stage_h[0], in_=hbuf[:, 0:STP]).then_inc(skw.h, 16)
            sync.dma_start(out=stage_h[1], in_=hbuf[:, 0:STP]).then_inc(skw.h, 16)
            skw.n += 32
        skw_zero = skw.n

        def _chunks_gu():
            return [(128 * (c // 2), 2048 * (c % 2), 2048,
                     INNER * (c // 2) + 2048 * (c % 2)) for c in range(16)]

        def _chunks_d():
            return [(128 * c, 0, D, D * c) for c in range(NK)]

        WMAT = [(wg_h, wq_g, 16, _chunks_gu(), 0), (wu_h, wq_u, 16, _chunks_gu(), 16),
                (wd_h, wq_d, NK, _chunks_d(), 32)]
        sched = [[] for _ in range(nt)]
        if nt >= 18:
            for c in range(16):
                sched[c // 4].append(("p1", 0, c))
            for c in range(16):
                sched[4 + c // 4].append(("p1", 1, c))
            for c in range(32):
                sched[8 + c // 8].append(("p1", 2, c))
            sched[4].append(("th", 0, 0))
            for c in range(16):
                sched[5 + c // 4].append(("p2", 0, c))
            sched[9].append(("th", 1, 0))
            for c in range(16):
                sched[10 + c // 4].append(("p2", 1, c))
            sched[12].append(("th", 2, 0))
            for c in range(32):
                sched[13 + c // 8].append(("p2", 2, c))
        else:
            for m in range(3):
                nch = WMAT[m][2]
                sched[0] += [("p1", m, c) for c in range(nch)]
                sched[0].append(("th", m, 0))
                sched[0] += [("p2", m, c) for c in range(nch)]

        a_abs = [[0] * 32 for _ in range(3)]
        ld_w = [[0] * 32 for _ in range(3)], [[0] * 32 for _ in range(3)]
        v_lt = [[0] * 32 for _ in range(3)]
        g_stt = [[0] * 32 for _ in range(3)]
        wslot_i = [0]
        wslot_free = [[], []]

        def emit_prep(entry):
            kind, m, c = entry
            wh, wq, nch, chunks, rsoff = WMAT[m]
            if kind in ("p1", "p2"):
                row0, col0, cols, qbase = chunks[c]
                sl = wslot_i[0] % 2
                wslot_i[0] += 1
                p2 = kind == "p2"

                @blk.sync
                def _(sync, m=m, c=c, sl=sl, p2=p2, marks=list(wslot_free[sl]),
                      row0=row0, col0=col0, cols=cols):
                    for s_, n_ in marks:
                        if n_ > 0:
                            sync.wait_ge(s_.h, n_)
                    ld.inc(sync.dma_start(out=wbuf[sl][:, 0:cols],
                                          in_=wh[row0:row0 + 128, col0:col0 + cols]),
                           16)
                    ld_w[1 if p2 else 0][m][c] = ld.n
                if not p2:
                    @blk.scalar
                    def _(scalar, m=m, c=c, sl=sl, cols=cols):
                        scalar.wait_ge(ld.h, ld_w[0][m][c])
                        if m == 0 and c == 0:
                            scalar.wait_ge(skw.h, skw_zero)
                        a.inc(scalar.activation(
                            hbuf[:, 0:cols], wbuf[sl][:, 0:cols], ACTF.Abs,
                            bias=0.0, scale=1.0,
                            accum_out=rs_abs[:, rsoff + c:rsoff + c + 1]))
                        a_abs[m][c] = a.n
                    wslot_free[sl] = [(a, a_abs[m][c])]
                else:
                    @blk.vector
                    def _(vector, m=m, c=c, sl=sl, cols=cols):
                        vector.wait_ge(ld.h, ld_w[1][m][c])
                        if c >= 2:
                            vector.wait_ge(v.h, g_stt[m][c - 2])
                        v.inc(vector.tensor_scalar(tlt[sl][:, 0:cols],
                                                   wbuf[sl][:, 0:cols], negth[m][:],
                                                   None, ALU.is_lt))
                        v_lt[m][c] = v.n

                    @blk.vector
                    def _(vector, m=m, c=c, sl=sl, cols=cols, qbase=qbase):
                        vector.wait_ge(v.h, v_lt[m][c])
                        v.inc(vector.scalar_tensor_tensor(
                            wq[:, qbase:qbase + cols], wbuf[sl][:, 0:cols],
                            thmw[m][:, 0:1], tlt[sl][:, 0:cols],
                            ALU.is_gt, ALU.subtract))
                        g_stt[m][c] = v.n
                    wslot_free[sl] = [(v, g_stt[m][c])]
            else:
                @blk.vector
                def _(vector, m=m):
                    vector.wait_ge(a.h, a_abs[m][nch - 1])
                    v.inc(vector.tensor_reduce(ST["tred"][:, 0:1],
                                               rs_abs[:, rsoff:rsoff + nch], AX,
                                               ALU.add))

                @blk.tensor
                def _(tensor, m=m, vn=v.n):
                    tensor.wait_ge(v.h, vn)
                    pewait(tensor)
                    t.inc(tensor.matmul(pprep[0:1, 0:1], ones_c[:],
                                        ST["tred"][:, 0:1], start=True, stop=True))

                @blk.vector
                def _(vector, m=m, tn=t.n):
                    vector.wait_ge(t.h, tn)
                    vguard(vector, vector.tensor_copy(tot1[:], pprep[0:1, 0:1]))
                    vguard(vector, vector.tensor_scalar(tot1[:], tot1[:],
                                                        1.0 / (D * INNER), 1e-5,
                                                        ALU.mult, ALU.max))
                    vector.tensor_copy(vec2[:, 1:2], tot1[:])
                    v.inc(vector.tensor_scalar(vec2[:, 0:1], tot1[:], 0.5, None,
                                               ALU.mult))

                @blk.tensor
                def _(tensor, m=m, vn=v.n):
                    tensor.wait_ge(v.h, vn)
                    pewait(tensor)
                    t.inc(tensor.matmul(pprep[:, 0:2], ones_r[:], vec2[:],
                                        start=True, stop=True))

                @blk.vector
                def _(vector, m=m, tn=t.n):
                    vector.wait_ge(t.h, tn)
                    vguard(vector, vector.tensor_copy(thmw[m][:], pprep[:, 0:2]))
                    v.inc(vector.tensor_scalar(negth[m][:], thmw[m][:, 0:1], -1.0,
                                               None, ALU.mult))

        Z = lambda: [0] * (nt + 1)
        ld_x, a_sq, v_var, a_sd, v_ln, a_ybf = Z(), Z(), Z(), Z(), Z(), Z()
        t_ytr, v_yev, a_yev, t_log, v_lb = Z(), Z(), Z(), Z(), Z()
        v_nmx, a_exp, v_rsum, a_wn, skw_n, skr_n = Z(), Z(), Z(), Z(), Z(), Z()
        t_atr, v_aev, a_aev, t_mix, v_mst, st_xp = Z(), Z(), Z(), Z(), Z(), Z()
        ld2, a_sq2, v_var2, a_sd2, g_rmin, v_xq = Z(), Z(), Z(), Z(), Z(), Z()
        a_xqbf, t_xtr, v_xev, a_xev, v_hq, a_hqbf = Z(), Z(), Z(), Z(), Z(), Z()
        t_down, v_dst, st_out, g_amax = Z(), Z(), Z(), Z()
        t_gu = [[0] * NI for _ in range(nt)]
        a_silu = [[0] * NI for _ in range(nt)]
        v_hstt = [[0] * NI for _ in range(nt)]
        t_hqtr = [[0] * 4 for _ in range(nt)]
        v_hqev = [[0] * 4 for _ in range(nt)]
        a_hqev = [[0] * 4 for _ in range(nt)]

        @blk.sync
        def _(sync):
            for j in (0, 1):
                if j < nt:
                    sync.wait_ge(v.h, v_const)
                    ld.inc(sync.dma_start(out=xb[j][:],
                                          in_=x_h[TT * j:TT * (j + 1), :]), 16)
                    ld_x[j] = ld.n

        for i in range(nt):

            @blk.vector
            def _(vector, i=i):
                vector.wait_ge(ld.h, ld_x[i])
                vguard(vector, vector.tensor_reduce(sv("rs", i), xb[i % 2][:], AX,
                                                    ALU.add))
                vguard(vector, vector.tensor_scalar(sv("m", i), sv("rs", i),
                                                    1.0 / D, None, ALU.mult))

            @blk.scalar
            def _(scalar, i=i):
                scalar.wait_ge(ld.h, ld_x[i])
                a.inc(scalar.activation(dump[:], xb[i % 2][:], ACTF.Square,
                                        bias=0.0, scale=1.0, accum_out=sv("ssq", i)))
                a_sq[i] = a.n

            @blk.vector
            def _(vector, i=i):
                vector.wait_ge(a.h, a_sq[i])
                vector.tensor_scalar(sv("var", i), sv("ssq", i), 1.0 / D, None,
                                     ALU.mult)
                vguard(vector, vector.tensor_scalar(sv("t1", i), sv("m", i),
                                                    sv("m", i), None, ALU.mult))
                v.inc(vector.tensor_tensor(sv("var", i), sv("var", i), sv("t1", i),
                                           ALU.subtract))
                v_var[i] = v.n

            @blk.scalar
            def _(scalar, i=i):
                scalar.wait_ge(v.h, v_var[i])
                a.inc(scalar.activation(sv("sd", i), sv("var", i), ACTF.Sqrt,
                                        bias=ceps[:], scale=1.0))
                a_sd[i] = a.n

            @blk.vector
            def _(vector, i=i):
                vector.wait_ge(a.h, a_sd[i])
                vguard(vector, vector.reciprocal(sv("r", i), sv("sd", i)))
                v.inc(vector.tensor_scalar(sv("nmr", i), sv("m", i), sv("r", i),
                                           -1.0, ALU.mult, ALU.mult))
                v_ln[i] = v.n

            @blk.scalar
            def _(scalar, i=i):
                scalar.wait_ge(v.h, v_ln[i])
                if i >= 2:
                    scalar.wait_ge(t.h, t_mix[i - 2])
                a.inc(scalar.activation(ybf[i % 2][:], xb[i % 2][:], ACTF.Identity,
                                        bias=sv("nmr", i), scale=sv("r", i)))
                a_ybf[i] = a.n

            tile_done[i] = (a, a_ybf[i])
            if astage < 2:
                continue

            @blk.tensor
            def _(tensor, i=i):
                tensor.wait_ge(a.h, a_ybf[i])
                if serial and i >= 2 and v_mst[i - 2]:
                    tensor.wait_ge(v.h, v_mst[i - 2])
                if i >= 2:
                    tensor.wait_ge(t.h, t_log[i - 2])

            v_yev[i], a_yev[i] = tr_group(
                8, lambda k, i=i: ybf[i % 2][:, 128 * k:128 * (k + 1)],
                lambda k, i=i: yT[i % 2][:, 128 * k:128 * (k + 1)])
            tile_done[i] = (a, a_yev[i])
            if astage < 3:
                continue

            @blk.tensor
            def _(tensor, i=i):
                tensor.wait_ge(v.h, v_yev[i])
                tensor.wait_ge(a.h, a_yev[i])
                if i >= 1:
                    tensor.wait_ge(v.h, v_lb[i - 1])
                for d in range(ND):
                    pewait(tensor)
                    t.inc(tensor.matmul(pl, yT[i % 2][:, 128 * d:128 * (d + 1)],
                                        wtb[:, KD * d:KD * (d + 1)],
                                        start=(d == 0), stop=(d == ND - 1)))
                t_log[i] = t.n

            @blk.vector
            def _(vector, i=i):
                vector.wait_ge(t.h, t_log[i])
                if i >= 2:
                    vector.wait_ge(a.h, a_exp[i - 2])
                vguard(vector, vector.tensor_tensor(lbs[i % 2][:], pl, bbc[:],
                                                    ALU.add))
                v_lb[i] = v.n
                vguard(vector, vector.tensor_reduce(sv("mx", i), lbs[i % 2][:], AX,
                                                    ALU.max))
                v.inc(vector.tensor_scalar(sv("nmx", i), sv("mx", i), -1.0, None,
                                           ALU.mult))
                v_nmx[i] = v.n

            tile_done[i] = (v, v_nmx[i])
            if astage < 4:
                continue

            @blk.scalar
            def _(scalar, i=i):
                scalar.wait_ge(v.h, v_nmx[i])
                a.inc(scalar.activation(ews[i % 2][:], lbs[i % 2][:], ACTF.Exp,
                                        bias=sv("nmx", i), scale=1.0,
                                        accum_out=sv("esum", i)))
                a_exp[i] = a.n

            @blk.vector
            def _(vector, i=i):
                vector.wait_ge(a.h, a_exp[i])
                v.inc(vector.reciprocal(sv("rsum", i), sv("esum", i)))
                v_rsum[i] = v.n

            @blk.scalar
            def _(scalar, i=i):
                scalar.wait_ge(v.h, v_rsum[i])
                if i >= 2:
                    scalar.wait_ge(skw.h, skw_n[i - 2])
                a.inc(scalar.activation(wns[i % 2][:], ews[i % 2][:], ACTF.Copy,
                                        bias=0.0, scale=sv("rsum", i)))
                a_wn[i] = a.n

            tile_done[i] = (a, a_wn[i])
            if astage < 5:
                continue

            @blk.sync
            def _(sync, i=i):
                sync.wait_ge(a.h, a_wn[i])
                if i >= 2:
                    sync.wait_ge(skr.h, skr_n[i - 2])
                skw.inc(sync.dma_start(out=stage_h[i % 2, :, SKB:SKB + KD],
                                       in_=wns[i % 2][:]), 16)
                skw_n[i] = skw.n

            @blk.gpsimd
            def _(gpsimd, i=i):
                gpsimd.wait_ge(skw.h, skw_n[i])
                if i >= 2:
                    gpsimd.wait_ge(t.h, t_atr[i - 2])
                src = AP(stage_h.tensor, (i % 2) * 128 * STP + SKB,
                         [[STP - 1, 128], [1, 160]])
                skr.inc(gpsimd.dma_start(out=skbb[i % 2][:], in_=src), 16)
                skr_n[i] = skr.n

            tile_done[i] = (skr, skr_n[i])
            if astage < 6:
                continue

            @blk.tensor
            def _(tensor, i=i):
                tensor.wait_ge(skr.h, skr_n[i])
                if i >= 2:
                    tensor.wait_ge(t.h, t_mix[i - 2])

            def _asrc(k, i=i):
                return (skbb[i % 2][:, 0:128] if k == 0
                        else skbb[i % 2][:, 128:160])

            def _adst(k, i=i):
                return A1s[i % 2][:] if k == 0 else A2s[i % 2][:]
            v_aev[i], a_aev_i = tr_group(2, _asrc, _adst)
            a_aev[i] = a_aev_i
            t_atr[i] = t.n
            if dbg == 4:
                @blk.scalar
                def _(scalar, i=i):
                    scalar.wait_ge(v.h, v_aev[i])
                    scalar.wait_ge(a.h, a_aev[i])
                    scalar.dma_start(out=dbg4_A1[i], in_=A1s[i % 2][:]).then_inc(sst.h, 16)
                    scalar.dma_start(out=dbg4_skb[i], in_=skbb[i % 2][:]).then_inc(sst.h, 16)
                    scalar.dma_start(out=dbg4_wn[i], in_=wns[i % 2][:]).then_inc(sst.h, 16)
                    sst.n += 48
            tile_done[i] = (v, v_aev[i])
            if astage < 7:
                continue

            if i >= 1:
                @blk.tensor
                def _(tensor, i=i):
                    tensor.wait_ge(v.h, v_aev[i - 1])
                    tensor.wait_ge(a.h, a_aev[i - 1])
                    tensor.wait_ge(a.h, a_ybf[i])
                    if i >= 2:
                        tensor.wait_ge(v.h, v_mst[i - 2])
                    for half in range(2):
                        cs = slice(512 * half, 512 * (half + 1))
                        pewait(tensor)
                        t.inc(tensor.matmul(pmix[:, cs], A1s[(i - 1) % 2][:],
                                            ybf[(i - 1) % 2][:, cs], start=True,
                                            stop=False))
                        pewait(tensor)
                        t.inc(tensor.matmul(pmix[:, cs], A2s[(i - 1) % 2][:],
                                            ybf[i % 2][0:32, cs], start=False,
                                            stop=True))
                    t_mix[i - 1] = t.n

                @blk.vector
                def _(vector, i=i):
                    vector.wait_ge(t.h, t_mix[i - 1])
                    if i >= 3 and st_xp[i - 3]:
                        vector.wait_ge(sst.h, st_xp[i - 3])
                    v.inc(vector.scalar_tensor_tensor(
                        xpr[(i - 1) % 2][:], pmix[:], 1.0, xb[(i - 1) % 2][:],
                        ALU.mult, ALU.add))
                    v_mst[i - 1] = v.n

                @blk.scalar
                def _(scalar, i=i):
                    scalar.wait_ge(v.h, v_mst[i - 1])
                    sst.inc(scalar.dma_start(out=xp_h[TT * (i - 1):TT * i, :],
                                             in_=xpr[(i - 1) % 2][:]), 16)
                    st_xp[i - 1] = sst.n

            j = i + 1
            if 2 <= j < nt:
                @blk.sync
                def _(sync, i=i, j=j):
                    if astage >= 7:
                        sync.wait_ge(v.h, v_mst[j - 2])
                    elif tile_done[j - 2] is not None:
                        s_, n_ = tile_done[j - 2]
                        sync.wait_ge(s_.h, n_)
                    ld.inc(sync.dma_start(out=xb[j % 2][:],
                                          in_=x_h[TT * j:TT * (j + 1), :]), 16)
                    ld_x[j] = ld.n

            if prep:
                for entry in sched[i]:
                    emit_prep(entry)

        _ = None
        if astage < 7:
            @blk.vector
            def _(vector):
                s_, n_ = tile_done[nt - 1]
                vector.wait_ge(s_.h, n_)
            return nc

        @blk.tensor
        def _(tensor):
            i = nt - 1
            tensor.wait_ge(v.h, v_aev[i])
            tensor.wait_ge(a.h, a_aev[i])
            if i >= 1:
                tensor.wait_ge(v.h, v_mst[i - 1])
            for half in range(2):
                cs = slice(512 * half, 512 * (half + 1))
                pewait(tensor)
                t.inc(tensor.matmul(pmix[:, cs], A1s[i % 2][:], ybf[i % 2][:, cs],
                                    start=True, stop=True))
            t_mix[nt - 1] = t.n

        @blk.vector
        def _(vector):
            i = nt - 1
            vector.wait_ge(t.h, t_mix[i])
            if i >= 3 and st_xp[i - 3]:
                vector.wait_ge(sst.h, st_xp[i - 3])
            v.inc(vector.scalar_tensor_tensor(xpr[i % 2][:], pmix[:], 1.0,
                                              xb[i % 2][:], ALU.mult, ALU.add))
            v_mst[i] = v.n

        @blk.scalar
        def _(scalar):
            i = nt - 1
            scalar.wait_ge(v.h, v_mst[i])
            sst.inc(scalar.dma_start(out=xp_h[TT * i:TT * (i + 1), :],
                                     in_=xpr[i % 2][:]), 16)
            st_xp[i] = sst.n

        a_abs_all = a.n

        if dbg == 2:
            dbgs = [("ybf", ybf[0], BF16), ("yT", yT[0], BF16),
                    ("lb", lbs[0], F32), ("ew", ews[0], F32), ("wn", wns[0], F32),
                    ("skb", skbb[0], BF16), ("A1", A1s[0], BF16),
                    ("A2", A2s[0], BF16), ("stats", STbig, F32)]
        elif dbg == 3:
            dbgs = [("thmw0", thmw[0], F32), ("thmw1", thmw[1], F32),
                    ("thmw2", thmw[2], F32), ("kap", kap, F32),
                    ("wqg", wq_g[:, 0:INNER], FP8), ("wqd", wq_d[:, 0:D], FP8),
                    ("xqbf", xqbf[0], BF16), ("xqT", xqT[0], BF16),
                    ("hqbf", hqbf, BF16), ("stats", STbig, F32)]
            dbg_h = {nm: nc.dram_tensor("dbg_" + nm, list(tt.shape), dt,
                                        kind="ExternalOutput").ap()
                     for nm, tt, dt in dbgs}

            if dbg == 2:
                @blk.sync
                def _(sync):
                    sync.wait_ge(sst.h, sst.n)
                    nst = 0
                    for nm, tt, dt in dbgs:
                        sync.dma_start(out=dbg_h[nm], in_=tt[:]).then_inc(skw.h, 16)
                        nst += 16
                    sync.wait_ge(skw.h, skw.n + nst)
                    skw.n += nst

        if not phase_b:
            @blk.scalar
            def _(scalar):
                scalar.wait_ge(sst.h, sst.n)
            return nc
        def hbq_src(grp, k):
            kk = 8 * grp + k
            return hqbf[:, 128 * kk:128 * (kk + 1)]

        def hbq_dst(grp, k):
            kk = 8 * grp + k
            return hqT[:, 128 * kk:128 * (kk + 1)]

        def emit_hq_down(iq):
            for grp in range(4):
                @blk.tensor
                def _(tensor, iq=iq, grp=grp):
                    tensor.wait_ge(a.h, a_hqbf[iq])
                    if grp == 0 and iq >= 1:
                        tensor.wait_ge(t.h, t_down[iq - 1])

                v_hqev[iq][grp], a_hqev[iq][grp] = tr_group(
                    8, lambda k, grp=grp: hbq_src(grp, k),
                    lambda k, grp=grp: hbq_dst(grp, k))

                @blk.tensor
                def _(tensor, iq=iq, grp=grp):
                    tensor.wait_ge(v.h, v_hqev[iq][grp])
                    tensor.wait_ge(a.h, a_hqev[iq][grp])
                    if grp == 0:
                        if iq == 0:
                            tensor.wait_ge(v.h, g_stt[2][NK - 1])
                            tensor.wait_ge(v.h, v_mst[nt - 1])
                        elif v_dst[iq - 1]:
                            tensor.wait_ge(v.h, v_dst[iq - 1])
                    for kk in range(8):
                        k = 8 * grp + kk
                        pewait(tensor)
                        t.inc(tensor.matmul(pd[:, 0:512],
                                            hqT[:, 128 * k:128 * (k + 1)],
                                            wq_d[:, D * k:D * k + 512],
                                            start=(k == 0), stop=(k == NK - 1)))
                        pewait(tensor)
                        t.inc(tensor.matmul(pd[:, 512:1024],
                                            hqT[:, 128 * k:128 * (k + 1)],
                                            wq_d[:, D * k + 512:D * (k + 1)],
                                            start=(k == 0), stop=(k == NK - 1)))
                    if grp == 3:
                        t_down[iq] = t.n

            @blk.vector
            def _(vector, iq=iq):
                vector.wait_ge(t.h, t_down[iq])
                if iq >= 2 and st_out[iq - 2]:
                    vector.wait_ge(sst.h, st_out[iq - 2])
                elif iq < 2:
                    vector.wait_ge(sst.h, st_xp[nt - 1])
                v.inc(vector.scalar_tensor_tensor(outb[iq % 2][:], pd[:],
                                                  sv("dqd", iq), xb[iq % 2][:],
                                                  ALU.mult, ALU.add))
                v_dst[iq] = v.n

            @blk.scalar
            def _(scalar, iq=iq):
                scalar.wait_ge(v.h, v_dst[iq])
                sst.inc(scalar.dma_start(out=out_h[TT * iq:TT * (iq + 1), :],
                                         in_=outb[iq % 2][:]), 16)
                st_out[iq] = sst.n

        @blk.sync
        def _(sync):
            for j in (0, 1):
                if j < nt:
                    sync.wait_ge(sst.h, st_xp[j])
                    sync.wait_ge(v.h, v_mst[nt - 1])
                    ld.inc(sync.dma_start(out=xb[j % 2][:],
                                          in_=xp_h[TT * j:TT * (j + 1), :]), 16)
                    ld2[j] = ld.n

        for i in range(nt):

            @blk.vector
            def _(vector, i=i):
                vector.wait_ge(ld.h, ld2[i])
                vector.tensor_reduce(sv("rs", i), xb[i % 2][:], AX, ALU.add)
                vector.tensor_reduce(sv("rmax", i), xb[i % 2][:], AX, ALU.max)
                vguard(vector, vector.tensor_reduce(sv("rmin", i), xb[i % 2][:], AX,
                                                    ALU.min))
                vguard(vector, vector.tensor_scalar(sv("m", i), sv("rs", i),
                                                    1.0 / D, None, ALU.mult))

            @blk.scalar
            def _(scalar, i=i):
                if i >= 1:
                    scalar.wait_ge(v.h, v_hq[i - 1])
                    if i >= 2:
                        scalar.wait_ge(t.h, t_hqtr[i - 2][3])
                    a.inc(scalar.activation(hqbf[:], hq32[:], ACTF.Copy,
                                            bias=-128.0, scale=1.0))
                    a_hqbf[i - 1] = a.n
                scalar.wait_ge(ld.h, ld2[i])
                a.inc(scalar.activation(dump[:], xb[i % 2][:], ACTF.Square,
                                        bias=0.0, scale=1.0, accum_out=sv("ss2", i)))
                a_sq2[i] = a.n

            @blk.vector
            def _(vector, i=i):
                vector.wait_ge(a.h, a_sq2[i])
                vector.tensor_scalar(sv("var", i), sv("ss2", i), 1.0 / D, None,
                                     ALU.mult)
                vguard(vector, vector.tensor_scalar(sv("t1", i), sv("m", i),
                                                    sv("m", i), None, ALU.mult))
                v.inc(vector.tensor_tensor(sv("var", i), sv("var", i), sv("t1", i),
                                           ALU.subtract))
                v_var2[i] = v.n

            @blk.scalar
            def _(scalar, i=i):
                scalar.wait_ge(v.h, v_var2[i])
                a.inc(scalar.activation(sv("sd", i), sv("var", i), ACTF.Sqrt,
                                        bias=ceps[:], scale=1.0))
                a_sd2[i] = a.n

            @blk.vector
            def _(vector, i=i):
                vector.wait_ge(a.h, a_sd2[i])
                if i >= 1:
                    vector.wait_ge(a.h, a_xqbf[i - 1])
                vguard(vector, vector.reciprocal(sv("r", i), sv("sd", i)))
                vector.tensor_tensor(sv("t1", i), sv("rmax", i), sv("m", i),
                                     ALU.subtract)
                vguard(vector, vector.tensor_tensor(sv("t2", i), sv("m", i),
                                                    sv("rmin", i), ALU.subtract))
                vguard(vector, vector.tensor_tensor(sv("mxd", i), sv("t1", i),
                                                    sv("t2", i), ALU.max))
                vguard(vector, vector.tensor_scalar(sv("mxy", i), sv("mxd", i),
                                                    sv("r", i), None, ALU.mult))
                vguard(vector, vector.tensor_scalar(sv("mxc", i), sv("mxy", i),
                                                    1e-5, None, ALU.max))
                vguard(vector, vector.reciprocal(sv("sinv", i), sv("mxc", i)))
                vector.tensor_scalar(sv("dqx", i), sv("mxc", i), 1.0 / 127, None,
                                     ALU.mult)
                vguard(vector, vector.tensor_scalar(sv("s", i), sv("sinv", i),
                                                    127.0, None, ALU.mult))
                vguard(vector, vector.tensor_scalar(sv("scomb", i), sv("r", i),
                                                    sv("s", i), None, ALU.mult))
                vguard(vector, vector.tensor_scalar(sv("t3", i), sv("m", i),
                                                    sv("scomb", i), -1.0,
                                                    ALU.mult, ALU.mult))
                vguard(vector, vector.tensor_tensor(sv("bcomb", i), sv("t3", i),
                                                    kap[:], ALU.add))
                vector.tensor_scalar(sv("dqg", i), sv("dqx", i), thmw[0][:, 1:2],
                                     None, ALU.mult)
                vector.tensor_scalar(sv("dqu", i), sv("dqx", i), thmw[1][:, 1:2],
                                     None, ALU.mult)
                v.inc(vector.tensor_scalar(xq32[:], xb[i % 2][:], sv("scomb", i),
                                           sv("bcomb", i), ALU.mult, ALU.add))
                v_xq[i] = v.n

            @blk.scalar
            def _(scalar, i=i):
                scalar.wait_ge(v.h, v_xq[i])
                if i >= 2:
                    scalar.wait_ge(t.h, t_xtr[i - 2])
                a.inc(scalar.activation(xqbf[i % 2][:], xq32[:], ACTF.Copy,
                                        bias=-128.0, scale=1.0))
                a_xqbf[i] = a.n

            @blk.tensor
            def _(tensor, i=i):
                tensor.wait_ge(a.h, a_xqbf[i])
                if i >= 2 and t_gu[i - 2][NI - 1]:
                    tensor.wait_ge(t.h, t_gu[i - 2][NI - 1])

            v_xev[i], a_xev[i] = tr_group(
                8, lambda k, i=i: xqbf[i % 2][:, 128 * k:128 * (k + 1)],
                lambda k, i=i: xqT[i % 2][:, 128 * k:128 * (k + 1)])
            t_xtr[i] = t.n

            for j in range(NI):
                @blk.tensor
                def _(tensor, i=i, j=j):
                    tensor.wait_ge(v.h, v_xev[i])
                    tensor.wait_ge(a.h, a_xev[i])
                    if i == 0 and j == 0:
                        tensor.wait_ge(v.h, g_stt[1][WMAT[1][2] - 1])
                    pj, qi = (j - 2, i) if j >= 2 else (NI - 2 + j, i - 1)
                    if qi >= 0 and a_silu[qi][pj]:
                        tensor.wait_ge(a.h, a_silu[qi][pj])
                        tensor.wait_ge(v.h, v_hstt[qi][pj])
                    for d in range(8):
                        base = INNER * d + 512 * j
                        pewait(tensor)
                        t.inc(tensor.matmul(P512[j % 2][:],
                                            xqT[i % 2][:, 128 * d:128 * (d + 1)],
                                            wq_g[:, base:base + 512],
                                            start=(d == 0), stop=(d == 7)))
                        pewait(tensor)
                        t.inc(tensor.matmul(P512[2 + j % 2][:],
                                            xqT[i % 2][:, 128 * d:128 * (d + 1)],
                                            wq_u[:, base:base + 512],
                                            start=(d == 0), stop=(d == 7)))
                    t_gu[i][j] = t.n

                @blk.scalar
                def _(scalar, i=i, j=j):
                    scalar.wait_ge(t.h, t_gu[i][j])
                    if j >= 2:
                        scalar.wait_ge(v.h, v_hstt[i][j - 2])
                    elif i >= 1:
                        scalar.wait_ge(v.h, v_hstt[i - 1][NI - 2 + j])
                    a.inc(scalar.activation(gtmp[j % 2][:], P512[j % 2][:],
                                            ACTF.Silu, bias=0.0, scale=sv("dqg", i)))
                    a_silu[i][j] = a.n

                @blk.vector
                def _(vector, i=i, j=j):
                    vector.wait_ge(a.h, a_silu[i][j])
                    if i == 0 and j == 0:
                        vector.wait_ge(a.h, a_abs_all)
                    v.inc(vector.scalar_tensor_tensor(
                        hbuf[:, 512 * j:512 * (j + 1)], P512[2 + j % 2][:],
                        sv("dqu", i), gtmp[j % 2][:], ALU.mult, ALU.mult))
                    v_hstt[i][j] = v.n

            @blk.vector
            def _(vector, i=i):
                vector.wait_ge(v.h, v_hstt[i][NI - 1])
                for j in range(NI):
                    ins = vector.tensor_reduce(mxh8[i % 2][:, j:j + 1],
                                               hbuf[:, 512 * j:512 * (j + 1)], AX,
                                               ALU.max, apply_absolute_value=True)
                    if j == NI - 1:
                        vguard(vector, ins)
                vguard(vector, vector.tensor_reduce(sv("mxh", i), mxh8[i % 2][:], AX,
                                                    ALU.max))
                vguard(vector, vector.tensor_scalar(sv("shy", i), sv("mxh", i),
                                                    1e-5, None, ALU.max))
                vguard(vector, vector.reciprocal(sv("shinv", i), sv("shy", i)))
                vguard(vector, vector.tensor_scalar(sv("sh", i), sv("shinv", i),
                                                    127.0, None, ALU.mult))
                vguard(vector, vector.tensor_scalar(sv("dqh", i), sv("shy", i),
                                                    1.0 / 127, None, ALU.mult))
                vector.tensor_scalar(sv("dqd", i), sv("dqh", i), thmw[2][:, 1:2],
                                     None, ALU.mult)
                if i >= 1:
                    vector.wait_ge(a.h, a_hqbf[i - 1])
                v.inc(vector.tensor_scalar(hq32[:], hbuf[:], sv("sh", i), kap[:],
                                           ALU.mult, ALU.add))
                v_hq[i] = v.n

            if i >= 1:
                emit_hq_down(i - 1)

            j = i + 1
            if 2 <= j < nt:
                @blk.sync
                def _(sync, j=j):
                    sync.wait_ge(sst.h, st_xp[j])
                    sync.wait_ge(v.h, v_dst[j - 2])
                    ld.inc(sync.dma_start(out=xb[j % 2][:],
                                          in_=xp_h[TT * j:TT * (j + 1), :]), 16)
                    ld2[j] = ld.n

        @blk.scalar
        def _(scalar):
            scalar.wait_ge(v.h, v_hq[nt - 1])
            a.inc(scalar.activation(hqbf[:], hq32[:], ACTF.Copy, bias=-128.0,
                                    scale=1.0))
            a_hqbf[nt - 1] = a.n
        emit_hq_down(nt - 1)

        if dbg == 3:
            @blk.sync
            def _(sync):
                sync.wait_ge(sst.h, sst.n)
                nst = 0
                for nm, tt, dt in dbgs:
                    sync.dma_start(out=dbg_h[nm], in_=tt[:]).then_inc(skw.h, 16)
                    nst += 16
                sync.wait_ge(skw.h, skw.n + nst)
                skw.n += nst

        @blk.scalar
        def _(scalar):
            scalar.wait_ge(sst.h, sst.n)

    return nc


def kernel(L0_attn_x, L0_attn_W, L0_attn_b, L0_ffn_w_gate, L0_ffn_w_up,
           L0_ffn_w_down):
    x = np.ascontiguousarray(np.asarray(L0_attn_x, dtype=np.float32))
    Wt = np.ascontiguousarray(np.asarray(L0_attn_W, dtype=np.float32).T)
    b_bc = np.ascontiguousarray(
        np.tile(np.asarray(L0_attn_b, dtype=np.float32)[None, :], (128, 1)))
    wgT = np.ascontiguousarray(np.asarray(L0_ffn_w_gate, dtype=np.float32).T)
    wuT = np.ascontiguousarray(np.asarray(L0_ffn_w_up, dtype=np.float32).T)
    wdT = np.ascontiguousarray(np.asarray(L0_ffn_w_down, dtype=np.float32).T)
    ident = np.eye(128, dtype=ml_dtypes.bfloat16)
    nc = build_nc()
    in_maps = [{"x": x[c], "Wt": Wt, "b_bc": b_bc, "wgT": wgT, "wuT": wuT,
                "wdT": wdT, "ident": ident} for c in range(8)]
    res = run_bass_kernel_spmd(nc, in_maps, core_ids=list(range(8)))
    return np.stack([res.results[c]["out"] for c in range(8)], axis=0)


# version 03
# speedup vs baseline: 1.4290x; 1.4290x over previous
import numpy as np
import ml_dtypes
from contextlib import ExitStack

import concourse.bass as bass
import concourse.mybir as mybir
from concourse.bass_types import AP
from concourse.bass_utils import run_bass_kernel_spmd

F32 = mybir.dt.float32
BF16 = mybir.dt.bfloat16
FP8 = mybir.dt.float8e4
I32 = mybir.dt.int32
I16 = mybir.dt.int16
ALU = mybir.AluOpType
ACTF = mybir.ActivationFunctionType
AX = mybir.AxisListType.X

L, D, KD, INNER, TT = 4096, 1024, 32, 4096, 128
ND = D // 128
NI = INNER // 512
NK = INNER // 128
EPS = 1e-5
STP, SKB = 288, 127


class Sem:
    def __init__(self, ctx, nc, name):
        self.h = ctx.enter_context(nc.semaphore(name))
        self.n = 0

    def inc(self, ins, k=1):
        ins.then_inc(self.h, k)
        self.n += k
        return self.n


def build_nc(nt=L // TT, dbg=False, phase_b=True, prep=True, astage=99, serial=False):
    nc = bass.Bass("TRN2", target_bir_lowering=False, detect_race_conditions=False)
    Ltot = nt * TT

    x_h = nc.dram_tensor("x", [Ltot, D], F32, kind="ExternalInput").ap()
    wt_h = nc.dram_tensor("Wt", [D, KD], F32, kind="ExternalInput").ap()
    bb_h = nc.dram_tensor("b_bc", [128, KD], F32, kind="ExternalInput").ap()
    wg_h = nc.dram_tensor("wgT", [D, INNER], F32, kind="ExternalInput").ap()
    wu_h = nc.dram_tensor("wuT", [D, INNER], F32, kind="ExternalInput").ap()
    wd_h = nc.dram_tensor("wdT", [INNER, D], F32, kind="ExternalInput").ap()
    id_h = nc.dram_tensor("ident", [128, 128], BF16, kind="ExternalInput").ap()
    out_h = nc.dram_tensor("out", [Ltot, D], F32, kind="ExternalOutput").ap()
    xp_h = nc.dram_tensor("xprime", [Ltot, D], F32,
                          kind="ExternalOutput" if dbg else "Internal").ap()
    stage_h = nc.dram_tensor("stage", [2, 128, STP], F32, kind="Internal").ap()

    ctx = ExitStack()
    with ctx:
        sbuf = lambda name, shape, dt: ctx.enter_context(nc.sbuf_tensor(name, shape, dt))
        wq_g = sbuf("wq_g", [128, ND * INNER], FP8)
        wq_u = sbuf("wq_u", [128, ND * INNER], FP8)
        wq_d = sbuf("wq_d", [128, NK * D], FP8)
        wbuf = [sbuf(f"wbuf{k}", [128, 2048], F32) for k in range(2)]
        tlt = [sbuf(f"tlt{k}", [128, 2048], BF16) for k in range(2)]
        rs_abs = sbuf("rs_abs", [128, 64], F32)
        thmw = [sbuf(f"thmw{m}", [128, 2], F32) for m in range(3)]
        negth = [sbuf(f"negth{m}", [128, 1], F32) for m in range(3)]
        vec2 = sbuf("vec2", [1, 2], F32)
        tot1 = sbuf("tot1", [1, 1], F32)
        ones_c = sbuf("ones_c", [128, 1], F32)
        ones_r = sbuf("ones_r", [1, 128], F32)
        identb = sbuf("identb", [128, 128], BF16)
        wtb = sbuf("wtb", [128, ND * KD], BF16)
        bbc = sbuf("bbc", [128, KD], F32)
        kap = sbuf("kap", [128, 1], F32)
        ceps = sbuf("ceps", [128, 1], F32)
        c15 = sbuf("c15", [128, 1], F32)
        ci32 = sbuf("ci32", [128, 1], I32)
        bn6 = sbuf("bn6", [128, 12], F32)
        bnmv = sbuf("bnmv", [128, 4], F32)
        bmv = lambda i: bnmv[:, 2 * (i % 2):2 * (i % 2) + 2]
        bmean = lambda i: bnmv[:, 2 * (i % 2):2 * (i % 2) + 1]
        bvar = lambda i: bnmv[:, 2 * (i % 2) + 1:2 * (i % 2) + 2]
        xb = [sbuf(f"xb{k}", [128, D], F32) for k in range(2)]
        xpr = [sbuf(f"xpr{k}", [128, D], F32) for k in range(2)]
        outb = xpr
        ybf = [sbuf(f"ybf{k}", [128, D], BF16) for k in range(2)]
        yT = [sbuf(f"yT{k}", [128, D], BF16) for k in range(2)]
        lbs = [sbuf(f"lb{k}", [128, KD], F32) for k in range(2)]
        ews = [sbuf(f"ew{k}", [128, KD], F32) for k in range(2)]
        wns = [sbuf(f"wn{k}", [128, KD], F32) for k in range(2)]
        skbb = [sbuf(f"skb{k}", [128, 160], BF16) for k in range(2)]
        A1s = [sbuf(f"A1_{k}", [128, 128], BF16) for k in range(2)]
        A2s = [sbuf(f"A2_{k}", [32, 128], BF16) for k in range(2)]
        xq32 = sbuf("xq32", [128, D], I16)
        xqbf = [sbuf(f"xqbf{k}", [128, D], BF16) for k in range(2)]
        xqT = [sbuf(f"xqT{k}", [128, D], BF16) for k in range(2)]
        hbuf = sbuf("hbuf", [128, INNER], F32)
        hq32 = sbuf("hq32", [128, INNER], I16)
        hqbf = sbuf("hqbf", [128, INNER], BF16)
        hqT = sbuf("hqT", [128, INNER], BF16)
        gtmp = [sbuf(f"gtmp{k}", [128, 512], F32) for k in range(2)]
        mxh8 = [sbuf(f"mxh8{k}", [128, 8], F32) for k in range(2)]
        names = ["rs", "m", "var", "sd", "r", "nmr", "mx", "nmx", "ssq", "esum",
                 "rsum", "rmax", "rmin", "t1", "t2", "mxd", "mxy", "mxc", "sinv",
                 "s", "dqx", "scomb", "t3", "bcomb", "dqg", "dqu", "mxh", "shy",
                 "shinv", "sh", "dqh", "dqd", "ss2", "tred"]
        STbig = sbuf("STbig", [128, 2 * len(names)], F32)
        ST = {n: STbig[:, 2 * k:2 * k + 2] for k, n in enumerate(names)}

        def sv(n, i):
            k = names.index(n)
            return STbig[:, 2 * k + (i % 2):2 * k + (i % 2) + 1]

        ptb = [ctx.enter_context(nc.psum_tensor(f"ptb{k}", [128, 128], BF16))
               for k in range(2)]
        P512 = [ctx.enter_context(nc.psum_tensor(f"p512_{k}", [128, 512], F32))
                for k in range(4)]
        P1024 = ctx.enter_context(nc.psum_tensor("p1024", [128, 1024], F32))
        pprep = P512[3]
        pl = P512[0][:, 0:KD]
        pmix = P1024
        pd = P1024

        if dbg == 4:
            dbg4_A1 = nc.dram_tensor("dbg4_A1", [nt, 128, 128], BF16, kind="ExternalOutput").ap()
            dbg4_skb = nc.dram_tensor("dbg4_skb", [nt, 128, 160], BF16, kind="ExternalOutput").ap()
            dbg4_wn = nc.dram_tensor("dbg4_wn", [nt, 128, KD], F32, kind="ExternalOutput").ap()

        ld = Sem(ctx, nc, "ld")
        skw = Sem(ctx, nc, "skw")
        skr = Sem(ctx, nc, "skr")
        sst = Sem(ctx, nc, "sst")
        v = Sem(ctx, nc, "v")
        a = Sem(ctx, nc, "a")
        t = Sem(ctx, nc, "t")
        g = Sem(ctx, nc, "g")
        gld = Sem(ctx, nc, "gld")
        trp = Sem(ctx, nc, "trp")
        blk = ctx.enter_context(nc.Block())

        tile_done = [None] * (nt + 1)
        bank_mark = [(v, 0), (a, 0)]
        PEW = 4

        def pemm(tensor, ins):
            t.inc(ins)

        def vguard(vector, ins):
            v.inc(ins)
            vector.wait_ge(v.h, v.n)

        def pewait(tensor):
            if t.n >= PEW and t.n % 2 == 0:
                tensor.wait_ge(t.h, t.n - 2)

        def tr_group(n, src_fn, dst_fn):
            T0 = t.n
            V0, A0 = v.n, a.n
            nb0, nb1 = bank_mark[0][1], bank_mark[1][1]

            @blk.tensor
            def _(tensor):
                for k in range(n):
                    if k == 0 and nb0 > 0:
                        tensor.wait_ge(v.h, nb0)
                    if k == 1 and nb1 > 0:
                        tensor.wait_ge(a.h, nb1)
                    if k >= 2:
                        if k % 2 == 0:
                            tensor.wait_ge(v.h, V0 + k // 2)
                        else:
                            tensor.wait_ge(a.h, A0 + k // 2)
                    pewait(tensor)
                    t.inc(tensor.transpose(ptb[k % 2][0:src_fn(k).shape[1], :],
                                           src_fn(k), identb[:]))

            @blk.vector
            def _(vector):
                for k in range(0, n, 2):
                    vector.wait_ge(t.h, T0 + k + 1)
                    v.inc(vector.tensor_copy(dst_fn(k),
                                             ptb[0][0:src_fn(k).shape[1], :]))

            @blk.scalar
            def _(scalar):
                for k in range(1, n, 2):
                    scalar.wait_ge(t.h, T0 + k + 1)
                    a.inc(scalar.activation(dst_fn(k),
                                            ptb[1][0:src_fn(k).shape[1], :],
                                            ACTF.Copy, bias=0.0, scale=1.0))
            bank_mark[0] = (v, v.n)
            bank_mark[1] = (a, a.n)
            return v.n, a.n

        @blk.sync
        def _(sync):
            sync.dma_start(out=identb[:], in_=id_h[:]).then_inc(ld.h, 16)
            sync.dma_start(out=bbc[:], in_=bb_h[:]).then_inc(ld.h, 16)
            ld.n += 32

        @blk.gpsimd
        def _(gpsimd):
            for d in range(ND):
                gpsimd.dma_start(out=wtb[:, KD * d:KD * (d + 1)],
                                 in_=wt_h[128 * d:128 * (d + 1), :]).then_inc(gld.h, 16)
            gld.n += 16 * ND
        ld_const = ld.n
        gld_const = gld.n

        @blk.vector
        def _(vector):
            vector.memset(c15[:], 1.5)
            vector.memset(ceps[:], EPS)
            vector.memset(ones_c[:], 1.0)
            vector.memset(ones_r[:], 1.0)
            vguard(vector, vector.memset(hbuf[:, 0:STP], 0.0))
            vguard(vector, vector.tensor_copy(ci32[:], c15[:]))
            vguard(vector, vector.tensor_scalar(kap[:], ci32[:], -0.5, 129.0,
                                                ALU.mult, ALU.add))
            vector.wait_ge(ld.h, ld_const)
            vector.wait_ge(gld.h, gld_const)
            v.inc(vector.engine_nop())
        v_const = v.n

        @blk.sync
        def _(sync):
            sync.wait_ge(v.h, v_const)
            sync.dma_start(out=stage_h[0], in_=hbuf[:, 0:STP]).then_inc(skw.h, 16)
            sync.dma_start(out=stage_h[1], in_=hbuf[:, 0:STP]).then_inc(skw.h, 16)
            skw.n += 32
        skw_zero = skw.n

        def _chunks_gu():
            return [(128 * (c // 2), 2048 * (c % 2), 2048,
                     INNER * (c // 2) + 2048 * (c % 2)) for c in range(16)]

        def _chunks_d():
            return [(128 * c, 0, D, D * c) for c in range(NK)]

        WMAT = [(wg_h, wq_g, 16, _chunks_gu(), 0), (wu_h, wq_u, 16, _chunks_gu(), 16),
                (wd_h, wq_d, NK, _chunks_d(), 32)]
        sched = [[] for _ in range(nt)]
        if nt >= 18:
            for c in range(16):
                sched[c // 4].append(("p1", 0, c))
            for c in range(16):
                sched[4 + c // 4].append(("p1", 1, c))
            for c in range(32):
                sched[8 + c // 8].append(("p1", 2, c))
            sched[4].append(("th", 0, 0))
            for c in range(16):
                sched[5 + c // 4].append(("p2", 0, c))
            sched[9].append(("th", 1, 0))
            for c in range(16):
                sched[10 + c // 4].append(("p2", 1, c))
            sched[12].append(("th", 2, 0))
            for c in range(32):
                sched[13 + c // 8].append(("p2", 2, c))
        else:
            for m in range(3):
                nch = WMAT[m][2]
                sched[0] += [("p1", m, c) for c in range(nch)]
                sched[0].append(("th", m, 0))
                sched[0] += [("p2", m, c) for c in range(nch)]

        a_abs = [[0] * 32 for _ in range(3)]
        v_abs = [[0] * 32 for _ in range(3)]
        ld_w = [[0] * 32 for _ in range(3)], [[0] * 32 for _ in range(3)]
        v_lt = [[0] * 32 for _ in range(3)]
        g_stt = [[0] * 32 for _ in range(3)]
        wslot_i = [0]
        wslot_free = [[], []]

        def emit_prep(entry):
            kind, m, c = entry
            wh, wq, nch, chunks, rsoff = WMAT[m]
            if kind in ("p1", "p2"):
                row0, col0, cols, qbase = chunks[c]
                sl = wslot_i[0] % 2
                wslot_i[0] += 1
                p2 = kind == "p2"

                @blk.sync
                def _(sync, m=m, c=c, sl=sl, p2=p2, marks=list(wslot_free[sl]),
                      row0=row0, col0=col0, cols=cols):
                    for s_, n_ in marks:
                        if n_ > 0:
                            sync.wait_ge(s_.h, n_)
                    ld.inc(sync.dma_start(out=wbuf[sl][:, 0:cols],
                                          in_=wh[row0:row0 + 128, col0:col0 + cols]),
                           16)
                    ld_w[1 if p2 else 0][m][c] = ld.n
                if not p2:
                    if c % 2 == 0:
                        @blk.scalar
                        def _(scalar, m=m, c=c, sl=sl, cols=cols):
                            scalar.wait_ge(ld.h, ld_w[0][m][c])
                            if m == 0 and c == 0:
                                scalar.wait_ge(skw.h, skw_zero)
                            a.inc(scalar.activation(
                                hbuf[:, 0:cols], wbuf[sl][:, 0:cols], ACTF.Abs,
                                bias=0.0, scale=1.0,
                                accum_out=rs_abs[:, rsoff + c:rsoff + c + 1]))
                            a_abs[m][c] = a.n
                        wslot_free[sl] = [(a, a_abs[m][c])]
                    else:
                        @blk.vector
                        def _(vector, m=m, c=c, sl=sl, cols=cols):
                            vector.wait_ge(ld.h, ld_w[0][m][c])
                            v.inc(vector.tensor_reduce(
                                rs_abs[:, rsoff + c:rsoff + c + 1],
                                wbuf[sl][:, 0:cols], AX, ALU.add,
                                apply_absolute_value=True))
                            v_abs[m][c] = v.n
                        wslot_free[sl] = [(v, v_abs[m][c])]
                else:
                    @blk.vector
                    def _(vector, m=m, c=c, sl=sl, cols=cols):
                        vector.wait_ge(ld.h, ld_w[1][m][c])
                        if c >= 2:
                            vector.wait_ge(v.h, g_stt[m][c - 2])
                        v.inc(vector.tensor_scalar(tlt[sl][:, 0:cols],
                                                   wbuf[sl][:, 0:cols], negth[m][:],
                                                   None, ALU.is_lt))
                        v_lt[m][c] = v.n

                    @blk.vector
                    def _(vector, m=m, c=c, sl=sl, cols=cols, qbase=qbase):
                        vector.wait_ge(v.h, v_lt[m][c])
                        v.inc(vector.scalar_tensor_tensor(
                            wq[:, qbase:qbase + cols], wbuf[sl][:, 0:cols],
                            thmw[m][:, 0:1], tlt[sl][:, 0:cols],
                            ALU.is_gt, ALU.subtract))
                        g_stt[m][c] = v.n
                    wslot_free[sl] = [(v, g_stt[m][c])]
            else:
                @blk.vector
                def _(vector, m=m):
                    vector.wait_ge(a.h, max(a_abs[m][:nch]))
                    vector.wait_ge(v.h, max(v_abs[m][:nch]))
                    v.inc(vector.tensor_reduce(ST["tred"][:, 0:1],
                                               rs_abs[:, rsoff:rsoff + nch], AX,
                                               ALU.add))

                @blk.tensor
                def _(tensor, m=m, vn=v.n):
                    tensor.wait_ge(v.h, vn)
                    pewait(tensor)
                    t.inc(tensor.matmul(pprep[0:1, 0:1], ones_c[:],
                                        ST["tred"][:, 0:1], start=True, stop=True))

                @blk.vector
                def _(vector, m=m, tn=t.n):
                    vector.wait_ge(t.h, tn)
                    vguard(vector, vector.tensor_copy(tot1[:], pprep[0:1, 0:1]))
                    vguard(vector, vector.tensor_scalar(tot1[:], tot1[:],
                                                        1.0 / (D * INNER), 1e-5,
                                                        ALU.mult, ALU.max))
                    vector.tensor_copy(vec2[:, 1:2], tot1[:])
                    v.inc(vector.tensor_scalar(vec2[:, 0:1], tot1[:], 0.5, None,
                                               ALU.mult))

                @blk.tensor
                def _(tensor, m=m, vn=v.n):
                    tensor.wait_ge(v.h, vn)
                    pewait(tensor)
                    t.inc(tensor.matmul(pprep[:, 0:2], ones_r[:], vec2[:],
                                        start=True, stop=True))

                @blk.vector
                def _(vector, m=m, tn=t.n):
                    vector.wait_ge(t.h, tn)
                    vguard(vector, vector.tensor_copy(thmw[m][:], pprep[:, 0:2]))
                    v.inc(vector.tensor_scalar(negth[m][:], thmw[m][:, 0:1], -1.0,
                                               None, ALU.mult))

        Z = lambda: [0] * (nt + 1)
        ld_x, a_sq, v_var, a_sd, v_ln, a_ybf = Z(), Z(), Z(), Z(), Z(), Z()
        t_ytr, v_yev, a_yev, t_log, v_lb = Z(), Z(), Z(), Z(), Z()
        v_nmx, a_exp, v_rsum, a_wn, skw_n, skr_n = Z(), Z(), Z(), Z(), Z(), Z()
        t_atr, v_aev, a_aev, t_mix, v_mst, st_xp = Z(), Z(), Z(), Z(), Z(), Z()
        ld2, a_sq2, v_var2, a_sd2, g_rmin, v_xq = Z(), Z(), Z(), Z(), Z(), Z()
        a_xqbf, t_xtr, v_xev, a_xev, v_hq, a_hqbf = Z(), Z(), Z(), Z(), Z(), Z()
        v_hqbf2 = Z()
        t_down, v_dst, st_out, g_amax = Z(), Z(), Z(), Z()
        trp_y, trp_xq, trp_hq = Z(), Z(), Z()
        t_gu = [[0] * NI for _ in range(nt)]
        a_silu = [[0] * NI for _ in range(nt)]
        v_hstt = [[0] * NI for _ in range(nt)]
        t_hqtr = [[0] * 4 for _ in range(nt)]
        v_hqev = [[0] * 4 for _ in range(nt)]
        a_hqev = [[0] * 4 for _ in range(nt)]

        @blk.sync
        def _(sync):
            for j in (0, 1):
                if j < nt:
                    sync.wait_ge(v.h, v_const)
                    ld.inc(sync.dma_start(out=xb[j][:],
                                          in_=x_h[TT * j:TT * (j + 1), :]), 16)
                    ld_x[j] = ld.n

        for i in range(nt):

            @blk.vector
            def _(vector, i=i):
                vector.wait_ge(ld.h, ld_x[i])
                vector.bn_stats(bn6[:, 0:6], xb[i % 2][:, 0:512])
                vguard(vector, vector.bn_stats(bn6[:, 6:12], xb[i % 2][:, 512:D]))
                v.inc(vector.bn_aggr(bmv(i), bn6[:]))
                v_var[i] = v.n

            @blk.scalar
            def _(scalar, i=i):
                scalar.wait_ge(v.h, v_var[i])
                a.inc(scalar.activation(sv("sd", i), bvar(i), ACTF.Sqrt,
                                        bias=ceps[:], scale=1.0))
                a_sd[i] = a.n

            @blk.vector
            def _(vector, i=i):
                vector.wait_ge(a.h, a_sd[i])
                vguard(vector, vector.reciprocal(sv("r", i), sv("sd", i)))
                v.inc(vector.tensor_scalar(sv("nmr", i), bmean(i), sv("r", i),
                                           -1.0, ALU.mult, ALU.mult))
                v_ln[i] = v.n

            @blk.scalar
            def _(scalar, i=i):
                scalar.wait_ge(v.h, v_ln[i])
                if i >= 2:
                    scalar.wait_ge(t.h, t_mix[i - 2])
                a.inc(scalar.activation(ybf[i % 2][:], xb[i % 2][:], ACTF.Identity,
                                        bias=sv("nmr", i), scale=sv("r", i)))
                a_ybf[i] = a.n

            tile_done[i] = (a, a_ybf[i])
            if astage < 2:
                continue

            @blk.sync
            def _(sync, i=i):
                sync.wait_ge(a.h, a_ybf[i])
                if i >= 2:
                    sync.wait_ge(t.h, t_log[i - 2])
                sync.wait_ge(trp.h, trp.n)
                trp.inc(sync.dma_start_transpose(
                    yT[i % 2][:].rearrange("p (k f) -> p k f", k=8),
                    ybf[i % 2][:].rearrange("p (k f) -> p k f", k=8)), 16)
                trp_y[i] = trp.n
            tile_done[i] = (a, a_yev[i])
            if astage < 3:
                continue

            @blk.tensor
            def _(tensor, i=i):
                tensor.wait_ge(trp.h, trp_y[i])
                if i >= 1:
                    tensor.wait_ge(v.h, v_lb[i - 1])
                for d in range(ND):
                    pewait(tensor)
                    t.inc(tensor.matmul(pl, yT[i % 2][:, 128 * d:128 * (d + 1)],
                                        wtb[:, KD * d:KD * (d + 1)],
                                        start=(d == 0), stop=(d == ND - 1)))
                t_log[i] = t.n

            @blk.vector
            def _(vector, i=i):
                vector.wait_ge(t.h, t_log[i])
                if i >= 2:
                    vector.wait_ge(a.h, a_exp[i - 2])
                vguard(vector, vector.tensor_tensor(lbs[i % 2][:], pl, bbc[:],
                                                    ALU.add))
                v_lb[i] = v.n
                vguard(vector, vector.tensor_reduce(sv("mx", i), lbs[i % 2][:], AX,
                                                    ALU.max))
                v.inc(vector.tensor_scalar(sv("nmx", i), sv("mx", i), -1.0, None,
                                           ALU.mult))
                v_nmx[i] = v.n

            tile_done[i] = (v, v_nmx[i])
            if astage < 4:
                continue

            @blk.scalar
            def _(scalar, i=i):
                scalar.wait_ge(v.h, v_nmx[i])
                a.inc(scalar.activation(ews[i % 2][:], lbs[i % 2][:], ACTF.Exp,
                                        bias=sv("nmx", i), scale=1.0,
                                        accum_out=sv("esum", i)))
                a_exp[i] = a.n

            @blk.vector
            def _(vector, i=i):
                vector.wait_ge(a.h, a_exp[i])
                v.inc(vector.reciprocal(sv("rsum", i), sv("esum", i)))
                v_rsum[i] = v.n

            @blk.scalar
            def _(scalar, i=i):
                scalar.wait_ge(v.h, v_rsum[i])
                if i >= 2:
                    scalar.wait_ge(skw.h, skw_n[i - 2])
                a.inc(scalar.activation(wns[i % 2][:], ews[i % 2][:], ACTF.Copy,
                                        bias=0.0, scale=sv("rsum", i)))
                a_wn[i] = a.n

            tile_done[i] = (a, a_wn[i])
            if astage < 5:
                continue

            @blk.sync
            def _(sync, i=i):
                sync.wait_ge(a.h, a_wn[i])
                if i >= 2:
                    sync.wait_ge(skr.h, skr_n[i - 2])
                skw.inc(sync.dma_start(out=stage_h[i % 2, :, SKB:SKB + KD],
                                       in_=wns[i % 2][:]), 16)
                skw_n[i] = skw.n

            @blk.gpsimd
            def _(gpsimd, i=i):
                gpsimd.wait_ge(skw.h, skw_n[i])
                if i >= 2:
                    gpsimd.wait_ge(t.h, t_atr[i - 2])
                src = AP(stage_h.tensor, (i % 2) * 128 * STP + SKB,
                         [[STP - 1, 128], [1, 160]])
                skr.inc(gpsimd.dma_start(out=skbb[i % 2][:], in_=src), 16)
                skr_n[i] = skr.n

            tile_done[i] = (skr, skr_n[i])
            if astage < 6:
                continue

            @blk.tensor
            def _(tensor, i=i):
                tensor.wait_ge(skr.h, skr_n[i])
                if i >= 2:
                    tensor.wait_ge(t.h, t_mix[i - 2])

            def _asrc(k, i=i):
                return (skbb[i % 2][:, 0:128] if k == 0
                        else skbb[i % 2][:, 128:160])

            def _adst(k, i=i):
                return A1s[i % 2][:] if k == 0 else A2s[i % 2][:]
            v_aev[i], a_aev_i = tr_group(2, _asrc, _adst)
            a_aev[i] = a_aev_i
            t_atr[i] = t.n
            if dbg == 4:
                @blk.scalar
                def _(scalar, i=i):
                    scalar.wait_ge(v.h, v_aev[i])
                    scalar.wait_ge(a.h, a_aev[i])
                    scalar.dma_start(out=dbg4_A1[i], in_=A1s[i % 2][:]).then_inc(sst.h, 16)
                    scalar.dma_start(out=dbg4_skb[i], in_=skbb[i % 2][:]).then_inc(sst.h, 16)
                    scalar.dma_start(out=dbg4_wn[i], in_=wns[i % 2][:]).then_inc(sst.h, 16)
                    sst.n += 48
            tile_done[i] = (v, v_aev[i])
            if astage < 7:
                continue

            if i >= 1:
                @blk.tensor
                def _(tensor, i=i):
                    tensor.wait_ge(v.h, v_aev[i - 1])
                    tensor.wait_ge(a.h, a_aev[i - 1])
                    tensor.wait_ge(a.h, a_ybf[i])
                    if i >= 2:
                        tensor.wait_ge(v.h, v_mst[i - 2])
                    for half in range(2):
                        cs = slice(512 * half, 512 * (half + 1))
                        pewait(tensor)
                        t.inc(tensor.matmul(pmix[:, cs], A1s[(i - 1) % 2][:],
                                            ybf[(i - 1) % 2][:, cs], start=True,
                                            stop=False))
                        pewait(tensor)
                        t.inc(tensor.matmul(pmix[:, cs], A2s[(i - 1) % 2][:],
                                            ybf[i % 2][0:32, cs], start=False,
                                            stop=True))
                    t_mix[i - 1] = t.n

                @blk.vector
                def _(vector, i=i):
                    vector.wait_ge(t.h, t_mix[i - 1])
                    if i >= 3 and st_xp[i - 3]:
                        vector.wait_ge(sst.h, st_xp[i - 3])
                    v.inc(vector.scalar_tensor_tensor(
                        xpr[(i - 1) % 2][:], pmix[:], 1.0, xb[(i - 1) % 2][:],
                        ALU.mult, ALU.add))
                    v_mst[i - 1] = v.n

                @blk.scalar
                def _(scalar, i=i):
                    scalar.wait_ge(v.h, v_mst[i - 1])
                    sst.inc(scalar.dma_start(out=xp_h[TT * (i - 1):TT * i, :],
                                             in_=xpr[(i - 1) % 2][:]), 16)
                    st_xp[i - 1] = sst.n

            j = i + 1
            if 2 <= j < nt:
                @blk.sync
                def _(sync, i=i, j=j):
                    if astage >= 7:
                        sync.wait_ge(v.h, v_mst[j - 2])
                    elif tile_done[j - 2] is not None:
                        s_, n_ = tile_done[j - 2]
                        sync.wait_ge(s_.h, n_)
                    ld.inc(sync.dma_start(out=xb[j % 2][:],
                                          in_=x_h[TT * j:TT * (j + 1), :]), 16)
                    ld_x[j] = ld.n

            if prep:
                for entry in sched[i]:
                    emit_prep(entry)

        _ = None
        if astage < 7:
            @blk.vector
            def _(vector):
                s_, n_ = tile_done[nt - 1]
                vector.wait_ge(s_.h, n_)
            return nc

        @blk.tensor
        def _(tensor):
            i = nt - 1
            tensor.wait_ge(v.h, v_aev[i])
            tensor.wait_ge(a.h, a_aev[i])
            if i >= 1:
                tensor.wait_ge(v.h, v_mst[i - 1])
            for half in range(2):
                cs = slice(512 * half, 512 * (half + 1))
                pewait(tensor)
                t.inc(tensor.matmul(pmix[:, cs], A1s[i % 2][:], ybf[i % 2][:, cs],
                                    start=True, stop=True))
            t_mix[nt - 1] = t.n

        @blk.vector
        def _(vector):
            i = nt - 1
            vector.wait_ge(t.h, t_mix[i])
            if i >= 3 and st_xp[i - 3]:
                vector.wait_ge(sst.h, st_xp[i - 3])
            v.inc(vector.scalar_tensor_tensor(xpr[i % 2][:], pmix[:], 1.0,
                                              xb[i % 2][:], ALU.mult, ALU.add))
            v_mst[i] = v.n

        @blk.scalar
        def _(scalar):
            i = nt - 1
            scalar.wait_ge(v.h, v_mst[i])
            sst.inc(scalar.dma_start(out=xp_h[TT * i:TT * (i + 1), :],
                                     in_=xpr[i % 2][:]), 16)
            st_xp[i] = sst.n

        a_abs_all = a.n

        if dbg == 2:
            dbgs = [("ybf", ybf[0], BF16), ("yT", yT[0], BF16),
                    ("lb", lbs[0], F32), ("ew", ews[0], F32), ("wn", wns[0], F32),
                    ("skb", skbb[0], BF16), ("A1", A1s[0], BF16),
                    ("A2", A2s[0], BF16), ("stats", STbig, F32)]
        elif dbg == 3:
            dbgs = [("thmw0", thmw[0], F32), ("thmw1", thmw[1], F32),
                    ("thmw2", thmw[2], F32), ("kap", kap, F32),
                    ("wqg", wq_g[:, 0:INNER], FP8), ("wqd", wq_d[:, 0:D], FP8),
                    ("xqbf", xqbf[0], BF16), ("xqT", xqT[0], BF16),
                    ("hqbf", hqbf, BF16), ("stats", STbig, F32)]
            dbg_h = {nm: nc.dram_tensor("dbg_" + nm, list(tt.shape), dt,
                                        kind="ExternalOutput").ap()
                     for nm, tt, dt in dbgs}

            if dbg == 2:
                @blk.sync
                def _(sync):
                    sync.wait_ge(sst.h, sst.n)
                    nst = 0
                    for nm, tt, dt in dbgs:
                        sync.dma_start(out=dbg_h[nm], in_=tt[:]).then_inc(skw.h, 16)
                        nst += 16
                    sync.wait_ge(skw.h, skw.n + nst)
                    skw.n += nst

        if not phase_b:
            @blk.scalar
            def _(scalar):
                scalar.wait_ge(sst.h, sst.n)
            return nc
        def hbq_src(grp, k):
            kk = 8 * grp + k
            return hqbf[:, 128 * kk:128 * (kk + 1)]

        def hbq_dst(grp, k):
            kk = 8 * grp + k
            return hqT[:, 128 * kk:128 * (kk + 1)]

        def emit_hq_down(iq):
            @blk.sync
            def _(sync, iq=iq):
                sync.wait_ge(a.h, a_hqbf[iq])
                sync.wait_ge(v.h, v_hqbf2[iq])
                if iq >= 1:
                    sync.wait_ge(t.h, t_down[iq - 1])
                sync.wait_ge(trp.h, trp.n)
                trp.inc(sync.dma_start_transpose(
                    hqT[:].rearrange("p (k f) -> p k f", k=NK),
                    hqbf[:].rearrange("p (k f) -> p k f", k=NK)), 16)
                trp_hq[iq] = trp.n

            for grp in range(4):
                @blk.tensor
                def _(tensor, iq=iq, grp=grp):
                    tensor.wait_ge(trp.h, trp_hq[iq])
                    if grp == 0:
                        if iq == 0:
                            tensor.wait_ge(v.h, g_stt[2][NK - 1])
                            tensor.wait_ge(v.h, v_mst[nt - 1])
                        elif v_dst[iq - 1]:
                            tensor.wait_ge(v.h, v_dst[iq - 1])
                    for kk in range(8):
                        k = 8 * grp + kk
                        pewait(tensor)
                        t.inc(tensor.matmul(pd[:, 0:512],
                                            hqT[:, 128 * k:128 * (k + 1)],
                                            wq_d[:, D * k:D * k + 512],
                                            start=(k == 0), stop=(k == NK - 1)))
                        pewait(tensor)
                        t.inc(tensor.matmul(pd[:, 512:1024],
                                            hqT[:, 128 * k:128 * (k + 1)],
                                            wq_d[:, D * k + 512:D * (k + 1)],
                                            start=(k == 0), stop=(k == NK - 1)))
                    if grp == 3:
                        t_down[iq] = t.n

            @blk.vector
            def _(vector, iq=iq):
                vector.wait_ge(t.h, t_down[iq])
                if iq >= 2 and st_out[iq - 2]:
                    vector.wait_ge(sst.h, st_out[iq - 2])
                elif iq < 2:
                    vector.wait_ge(sst.h, st_xp[nt - 1])
                v.inc(vector.scalar_tensor_tensor(outb[iq % 2][:], pd[:],
                                                  sv("dqd", iq), xb[iq % 2][:],
                                                  ALU.mult, ALU.add))
                v_dst[iq] = v.n

            @blk.scalar
            def _(scalar, iq=iq):
                scalar.wait_ge(v.h, v_dst[iq])
                sst.inc(scalar.dma_start(out=out_h[TT * iq:TT * (iq + 1), :],
                                         in_=outb[iq % 2][:]), 16)
                st_out[iq] = sst.n

        @blk.sync
        def _(sync):
            for j in (0, 1):
                if j < nt:
                    sync.wait_ge(sst.h, st_xp[j])
                    sync.wait_ge(v.h, v_mst[nt - 1])
                    ld.inc(sync.dma_start(out=xb[j % 2][:],
                                          in_=xp_h[TT * j:TT * (j + 1), :]), 16)
                    ld2[j] = ld.n

        for i in range(nt):

            @blk.vector
            def _(vector, i=i):
                if i >= 1:
                    if i >= 2:
                        vector.wait_ge(trp.h, trp_hq[i - 2])
                    v.inc(vector.tensor_scalar(hqbf[:, 2048:INNER],
                                               hq32[:, 2048:INNER], -128.0, None,
                                               ALU.add))
                    v_hqbf2[i - 1] = v.n
                vector.wait_ge(ld.h, ld2[i])
                vector.bn_stats(bn6[:, 0:6], xb[i % 2][:, 0:512])
                vector.bn_stats(bn6[:, 6:12], xb[i % 2][:, 512:D])
                vector.tensor_reduce(sv("rmax", i), xb[i % 2][:], AX, ALU.max)
                vguard(vector, vector.tensor_reduce(sv("rmin", i), xb[i % 2][:], AX,
                                                    ALU.min))
                v.inc(vector.bn_aggr(bmv(i), bn6[:]))
                v_var2[i] = v.n

            @blk.scalar
            def _(scalar, i=i):
                if i >= 1:
                    scalar.wait_ge(v.h, v_hq[i - 1])
                    if i >= 2:
                        scalar.wait_ge(trp.h, trp_hq[i - 2])
                    a.inc(scalar.activation(hqbf[:, 0:2048], hq32[:, 0:2048],
                                            ACTF.Copy, bias=-128.0, scale=1.0))
                    a_hqbf[i - 1] = a.n
                scalar.wait_ge(v.h, v_var2[i])
                a.inc(scalar.activation(sv("sd", i), bvar(i), ACTF.Sqrt,
                                        bias=ceps[:], scale=1.0))
                a_sd2[i] = a.n

            @blk.vector
            def _(vector, i=i):
                vector.wait_ge(a.h, a_sd2[i])
                if i >= 1:
                    vector.wait_ge(a.h, a_xqbf[i - 1])
                vguard(vector, vector.reciprocal(sv("r", i), sv("sd", i)))
                vector.tensor_tensor(sv("t1", i), sv("rmax", i), bmean(i),
                                     ALU.subtract)
                vguard(vector, vector.tensor_tensor(sv("t2", i), bmean(i),
                                                    sv("rmin", i), ALU.subtract))
                vguard(vector, vector.tensor_tensor(sv("mxd", i), sv("t1", i),
                                                    sv("t2", i), ALU.max))
                vguard(vector, vector.tensor_scalar(sv("mxy", i), sv("mxd", i),
                                                    sv("r", i), None, ALU.mult))
                vguard(vector, vector.tensor_scalar(sv("mxc", i), sv("mxy", i),
                                                    1e-5, None, ALU.max))
                vguard(vector, vector.reciprocal(sv("sinv", i), sv("mxc", i)))
                vector.tensor_scalar(sv("dqx", i), sv("mxc", i), 1.0 / 127, None,
                                     ALU.mult)
                vguard(vector, vector.tensor_scalar(sv("s", i), sv("sinv", i),
                                                    127.0, None, ALU.mult))
                vguard(vector, vector.tensor_scalar(sv("scomb", i), sv("r", i),
                                                    sv("s", i), None, ALU.mult))
                vguard(vector, vector.tensor_scalar(sv("t3", i), bmean(i),
                                                    sv("scomb", i), -1.0,
                                                    ALU.mult, ALU.mult))
                vguard(vector, vector.tensor_tensor(sv("bcomb", i), sv("t3", i),
                                                    kap[:], ALU.add))
                vector.tensor_scalar(sv("dqg", i), sv("dqx", i), thmw[0][:, 1:2],
                                     None, ALU.mult)
                vector.tensor_scalar(sv("dqu", i), sv("dqx", i), thmw[1][:, 1:2],
                                     None, ALU.mult)
                v.inc(vector.tensor_scalar(xq32[:], xb[i % 2][:], sv("scomb", i),
                                           sv("bcomb", i), ALU.mult, ALU.add))
                v_xq[i] = v.n

            @blk.scalar
            def _(scalar, i=i):
                scalar.wait_ge(v.h, v_xq[i])
                if i >= 2:
                    scalar.wait_ge(trp.h, trp_xq[i - 2])
                a.inc(scalar.activation(xqbf[i % 2][:], xq32[:], ACTF.Copy,
                                        bias=-128.0, scale=1.0))
                a_xqbf[i] = a.n

            @blk.sync
            def _(sync, i=i):
                sync.wait_ge(a.h, a_xqbf[i])
                if i >= 2 and t_gu[i - 2][NI - 1]:
                    sync.wait_ge(t.h, t_gu[i - 2][NI - 1])
                sync.wait_ge(trp.h, trp.n)
                trp.inc(sync.dma_start_transpose(
                    xqT[i % 2][:].rearrange("p (k f) -> p k f", k=8),
                    xqbf[i % 2][:].rearrange("p (k f) -> p k f", k=8)), 16)
                trp_xq[i] = trp.n
            t_xtr[i] = t.n

            for j in range(NI):
                @blk.tensor
                def _(tensor, i=i, j=j):
                    tensor.wait_ge(trp.h, trp_xq[i])
                    if i == 0 and j == 0:
                        tensor.wait_ge(v.h, g_stt[1][WMAT[1][2] - 1])
                    pj, qi = (j - 2, i) if j >= 2 else (NI - 2 + j, i - 1)
                    if qi >= 0 and a_silu[qi][pj]:
                        tensor.wait_ge(a.h, a_silu[qi][pj])
                        tensor.wait_ge(v.h, v_hstt[qi][pj])
                    for d in range(8):
                        base = INNER * d + 512 * j
                        pewait(tensor)
                        t.inc(tensor.matmul(P512[j % 2][:],
                                            xqT[i % 2][:, 128 * d:128 * (d + 1)],
                                            wq_g[:, base:base + 512],
                                            start=(d == 0), stop=(d == 7)))
                        pewait(tensor)
                        t.inc(tensor.matmul(P512[2 + j % 2][:],
                                            xqT[i % 2][:, 128 * d:128 * (d + 1)],
                                            wq_u[:, base:base + 512],
                                            start=(d == 0), stop=(d == 7)))
                    t_gu[i][j] = t.n

                @blk.scalar
                def _(scalar, i=i, j=j):
                    scalar.wait_ge(t.h, t_gu[i][j])
                    if j >= 2:
                        scalar.wait_ge(v.h, v_hstt[i][j - 2])
                    elif i >= 1:
                        scalar.wait_ge(v.h, v_hstt[i - 1][NI - 2 + j])
                    a.inc(scalar.activation(gtmp[j % 2][:], P512[j % 2][:],
                                            ACTF.Silu, bias=0.0, scale=sv("dqg", i)))
                    a_silu[i][j] = a.n

                @blk.vector
                def _(vector, i=i, j=j):
                    vector.wait_ge(a.h, a_silu[i][j])
                    if i == 0 and j == 0:
                        vector.wait_ge(a.h, a_abs_all)
                    v.inc(vector.scalar_tensor_tensor(
                        hbuf[:, 512 * j:512 * (j + 1)], P512[2 + j % 2][:],
                        sv("dqu", i), gtmp[j % 2][:], ALU.mult, ALU.mult))
                    v_hstt[i][j] = v.n

            @blk.vector
            def _(vector, i=i):
                vector.wait_ge(v.h, v_hstt[i][NI - 1])
                for j in range(NI):
                    ins = vector.tensor_reduce(mxh8[i % 2][:, j:j + 1],
                                               hbuf[:, 512 * j:512 * (j + 1)], AX,
                                               ALU.max, apply_absolute_value=True)
                    if j == NI - 1:
                        vguard(vector, ins)
                vguard(vector, vector.tensor_reduce(sv("mxh", i), mxh8[i % 2][:], AX,
                                                    ALU.max))
                vguard(vector, vector.tensor_scalar(sv("shy", i), sv("mxh", i),
                                                    1e-5, None, ALU.max))
                vguard(vector, vector.reciprocal(sv("shinv", i), sv("shy", i)))
                vguard(vector, vector.tensor_scalar(sv("sh", i), sv("shinv", i),
                                                    127.0, None, ALU.mult))
                vguard(vector, vector.tensor_scalar(sv("dqh", i), sv("shy", i),
                                                    1.0 / 127, None, ALU.mult))
                vector.tensor_scalar(sv("dqd", i), sv("dqh", i), thmw[2][:, 1:2],
                                     None, ALU.mult)
                if i >= 1:
                    vector.wait_ge(a.h, a_hqbf[i - 1])
                v.inc(vector.tensor_scalar(hq32[:], hbuf[:], sv("sh", i), kap[:],
                                           ALU.mult, ALU.add))
                v_hq[i] = v.n

            if i >= 1:
                emit_hq_down(i - 1)

            j = i + 1
            if 2 <= j < nt:
                @blk.sync
                def _(sync, j=j):
                    sync.wait_ge(sst.h, st_xp[j])
                    sync.wait_ge(v.h, v_dst[j - 2])
                    ld.inc(sync.dma_start(out=xb[j % 2][:],
                                          in_=xp_h[TT * j:TT * (j + 1), :]), 16)
                    ld2[j] = ld.n

        @blk.scalar
        def _(scalar):
            scalar.wait_ge(v.h, v_hq[nt - 1])
            if nt >= 2:
                scalar.wait_ge(trp.h, trp_hq[nt - 2])
            a.inc(scalar.activation(hqbf[:, 0:2048], hq32[:, 0:2048], ACTF.Copy,
                                    bias=-128.0, scale=1.0))
            a_hqbf[nt - 1] = a.n

        @blk.vector
        def _(vector):
            if nt >= 2:
                vector.wait_ge(trp.h, trp_hq[nt - 2])
            v.inc(vector.tensor_scalar(hqbf[:, 2048:INNER], hq32[:, 2048:INNER],
                                       -128.0, None, ALU.add))
            v_hqbf2[nt - 1] = v.n
        emit_hq_down(nt - 1)

        if dbg == 3:
            @blk.sync
            def _(sync):
                sync.wait_ge(sst.h, sst.n)
                nst = 0
                for nm, tt, dt in dbgs:
                    sync.dma_start(out=dbg_h[nm], in_=tt[:]).then_inc(skw.h, 16)
                    nst += 16
                sync.wait_ge(skw.h, skw.n + nst)
                skw.n += nst

        @blk.scalar
        def _(scalar):
            scalar.wait_ge(sst.h, sst.n)

    return nc


def kernel(L0_attn_x, L0_attn_W, L0_attn_b, L0_ffn_w_gate, L0_ffn_w_up,
           L0_ffn_w_down):
    x = np.ascontiguousarray(np.asarray(L0_attn_x, dtype=np.float32))
    Wt = np.ascontiguousarray(np.asarray(L0_attn_W, dtype=np.float32).T)
    b_bc = np.ascontiguousarray(
        np.tile(np.asarray(L0_attn_b, dtype=np.float32)[None, :], (128, 1)))
    wgT = np.ascontiguousarray(np.asarray(L0_ffn_w_gate, dtype=np.float32).T)
    wuT = np.ascontiguousarray(np.asarray(L0_ffn_w_up, dtype=np.float32).T)
    wdT = np.ascontiguousarray(np.asarray(L0_ffn_w_down, dtype=np.float32).T)
    ident = np.eye(128, dtype=ml_dtypes.bfloat16)
    nc = build_nc()
    in_maps = [{"x": x[c], "Wt": Wt, "b_bc": b_bc, "wgT": wgT, "wuT": wuT,
                "wdT": wdT, "ident": ident} for c in range(8)]
    res = run_bass_kernel_spmd(nc, in_maps, core_ids=list(range(8)))
    return np.stack([res.results[c]["out"] for c in range(8)], axis=0)


# version 04
# speedup vs baseline: 1.4491x; 1.0141x over previous
import numpy as np
import ml_dtypes
from contextlib import ExitStack

import concourse.bass as bass
import concourse.mybir as mybir
from concourse.bass_types import AP
from concourse.bass_utils import run_bass_kernel_spmd

F32 = mybir.dt.float32
BF16 = mybir.dt.bfloat16
FP8 = mybir.dt.float8e4
I32 = mybir.dt.int32
I16 = mybir.dt.int16
ALU = mybir.AluOpType
ACTF = mybir.ActivationFunctionType
AX = mybir.AxisListType.X

L, D, KD, INNER, TT = 4096, 1024, 32, 4096, 128
ND = D // 128
NI = INNER // 512
NK = INNER // 128
EPS = 1e-5
STP, SKB = 288, 127


class Sem:
    def __init__(self, ctx, nc, name):
        self.h = ctx.enter_context(nc.semaphore(name))
        self.n = 0

    def inc(self, ins, k=1):
        ins.then_inc(self.h, k)
        self.n += k
        return self.n


def build_nc(nt=L // TT, dbg=False, phase_b=True, prep=True, astage=99, serial=False):
    nc = bass.Bass("TRN2", target_bir_lowering=False, detect_race_conditions=False)
    Ltot = nt * TT

    x_h = nc.dram_tensor("x", [Ltot, D], F32, kind="ExternalInput").ap()
    wt_h = nc.dram_tensor("Wt", [D, KD], F32, kind="ExternalInput").ap()
    bb_h = nc.dram_tensor("b_bc", [128, KD], F32, kind="ExternalInput").ap()
    wg_h = nc.dram_tensor("wgT", [D, INNER], F32, kind="ExternalInput").ap()
    wu_h = nc.dram_tensor("wuT", [D, INNER], F32, kind="ExternalInput").ap()
    wd_h = nc.dram_tensor("wdT", [INNER, D], F32, kind="ExternalInput").ap()
    id_h = nc.dram_tensor("ident", [128, 128], BF16, kind="ExternalInput").ap()
    out_h = nc.dram_tensor("out", [Ltot, D], F32, kind="ExternalOutput").ap()
    xp_h = nc.dram_tensor("xprime", [Ltot, D], F32,
                          kind="ExternalOutput" if dbg else "Internal").ap()
    stage_h = nc.dram_tensor("stage", [2, 128, STP], F32, kind="Internal").ap()

    ctx = ExitStack()
    with ctx:
        sbuf = lambda name, shape, dt: ctx.enter_context(nc.sbuf_tensor(name, shape, dt))
        wq_g = sbuf("wq_g", [128, ND * INNER], FP8)
        wq_u = sbuf("wq_u", [128, ND * INNER], FP8)
        wq_d = sbuf("wq_d", [128, NK * D], FP8)
        wbuf = [sbuf(f"wbuf{k}", [128, 2048], F32) for k in range(2)]
        tlt = [sbuf(f"tlt{k}", [128, 2048], BF16) for k in range(2)]
        rs_abs = sbuf("rs_abs", [128, 64], F32)
        thmw = [sbuf(f"thmw{m}", [128, 2], F32) for m in range(3)]
        negth = [sbuf(f"negth{m}", [128, 1], F32) for m in range(3)]
        vec2 = sbuf("vec2", [1, 2], F32)
        tot1 = sbuf("tot1", [1, 1], F32)
        ones_c = sbuf("ones_c", [128, 1], F32)
        ones_r = sbuf("ones_r", [1, 128], F32)
        identb = sbuf("identb", [128, 128], BF16)
        wtb = sbuf("wtb", [128, ND * KD], BF16)
        bbc = sbuf("bbc", [128, KD], F32)
        kap = sbuf("kap", [128, 1], F32)
        ceps = sbuf("ceps", [128, 1], F32)
        c15 = sbuf("c15", [128, 1], F32)
        ci32 = sbuf("ci32", [128, 1], I32)
        bn6 = sbuf("bn6", [128, 12], F32)
        bnmv = sbuf("bnmv", [128, 4], F32)
        bmv = lambda i: bnmv[:, 2 * (i % 2):2 * (i % 2) + 2]
        bmean = lambda i: bnmv[:, 2 * (i % 2):2 * (i % 2) + 1]
        bvar = lambda i: bnmv[:, 2 * (i % 2) + 1:2 * (i % 2) + 2]
        xb = [sbuf(f"xb{k}", [128, D], F32) for k in range(3)]
        xpr = [sbuf(f"xpr{k}", [128, D], F32) for k in range(2)]
        outb = xpr
        ybf = [sbuf(f"ybf{k}", [128, D], BF16) for k in range(2)]
        yT = [sbuf(f"yT{k}", [128, D], BF16) for k in range(2)]
        lbs = [sbuf(f"lb{k}", [128, KD], F32) for k in range(2)]
        ews = [sbuf(f"ew{k}", [128, KD], F32) for k in range(2)]
        wns = [sbuf(f"wn{k}", [128, KD], F32) for k in range(2)]
        skbb = [sbuf(f"skb{k}", [128, 160], BF16) for k in range(2)]
        A1s = [sbuf(f"A1_{k}", [128, 128], BF16) for k in range(2)]
        A2s = [sbuf(f"A2_{k}", [32, 128], BF16) for k in range(2)]
        xq32 = sbuf("xq32", [128, D], I16)
        xqbf = [sbuf(f"xqbf{k}", [128, D], BF16) for k in range(2)]
        xqT = [sbuf(f"xqT{k}", [128, D], BF16) for k in range(2)]
        hbuf = sbuf("hbuf", [128, INNER], F32)
        hq32 = sbuf("hq32", [128, INNER], I16)
        hqbf = sbuf("hqbf", [128, INNER], BF16)
        hqT = sbuf("hqT", [128, INNER], BF16)
        gtmp = [sbuf(f"gtmp{k}", [128, 512], F32) for k in range(2)]
        mxh8 = [sbuf(f"mxh8{k}", [128, 8], F32) for k in range(2)]
        names = ["rs", "m", "var", "sd", "r", "nmr", "mx", "nmx", "ssq", "esum",
                 "rsum", "rmax", "rmin", "t1", "t2", "mxd", "mxy", "mxc", "sinv",
                 "s", "dqx", "scomb", "t3", "bcomb", "dqg", "dqu", "mxh", "shy",
                 "shinv", "sh", "dqh", "dqd", "ss2", "tred"]
        STbig = sbuf("STbig", [128, 2 * len(names)], F32)
        ST = {n: STbig[:, 2 * k:2 * k + 2] for k, n in enumerate(names)}

        def sv(n, i):
            k = names.index(n)
            return STbig[:, 2 * k + (i % 2):2 * k + (i % 2) + 1]

        ptb = [ctx.enter_context(nc.psum_tensor(f"ptb{k}", [128, 128], BF16))
               for k in range(2)]
        P512 = [ctx.enter_context(nc.psum_tensor(f"p512_{k}", [128, 512], F32))
                for k in range(4)]
        P1024 = ctx.enter_context(nc.psum_tensor("p1024", [128, 1024], F32))
        pprep = P512[3]
        pl = P512[0][:, 0:KD]
        pmix = P1024
        pd = P1024

        if dbg == 4:
            dbg4_A1 = nc.dram_tensor("dbg4_A1", [nt, 128, 128], BF16, kind="ExternalOutput").ap()
            dbg4_skb = nc.dram_tensor("dbg4_skb", [nt, 128, 160], BF16, kind="ExternalOutput").ap()
            dbg4_wn = nc.dram_tensor("dbg4_wn", [nt, 128, KD], F32, kind="ExternalOutput").ap()

        ld = Sem(ctx, nc, "ld")
        skw = Sem(ctx, nc, "skw")
        skr = Sem(ctx, nc, "skr")
        sst = Sem(ctx, nc, "sst")
        v = Sem(ctx, nc, "v")
        a = Sem(ctx, nc, "a")
        t = Sem(ctx, nc, "t")
        g = Sem(ctx, nc, "g")
        gld = Sem(ctx, nc, "gld")
        trp = Sem(ctx, nc, "trp")
        blk = ctx.enter_context(nc.Block())

        tile_done = [None] * (nt + 1)
        bank_mark = [(v, 0), (a, 0)]
        PEW = 4

        def pemm(tensor, ins):
            t.inc(ins)

        def vguard(vector, ins):
            v.inc(ins)
            vector.wait_ge(v.h, v.n)

        def pewait(tensor):
            if t.n >= PEW and t.n % 2 == 0:
                tensor.wait_ge(t.h, t.n - 2)

        def tr_group(n, src_fn, dst_fn):
            T0 = t.n
            V0, A0 = v.n, a.n
            nb0, nb1 = bank_mark[0][1], bank_mark[1][1]

            @blk.tensor
            def _(tensor):
                for k in range(n):
                    if k == 0 and nb0 > 0:
                        tensor.wait_ge(v.h, nb0)
                    if k == 1 and nb1 > 0:
                        tensor.wait_ge(a.h, nb1)
                    if k >= 2:
                        if k % 2 == 0:
                            tensor.wait_ge(v.h, V0 + k // 2)
                        else:
                            tensor.wait_ge(a.h, A0 + k // 2)
                    pewait(tensor)
                    t.inc(tensor.transpose(ptb[k % 2][0:src_fn(k).shape[1], :],
                                           src_fn(k), identb[:]))

            @blk.vector
            def _(vector):
                for k in range(0, n, 2):
                    vector.wait_ge(t.h, T0 + k + 1)
                    v.inc(vector.tensor_copy(dst_fn(k),
                                             ptb[0][0:src_fn(k).shape[1], :]))

            @blk.scalar
            def _(scalar):
                for k in range(1, n, 2):
                    scalar.wait_ge(t.h, T0 + k + 1)
                    a.inc(scalar.activation(dst_fn(k),
                                            ptb[1][0:src_fn(k).shape[1], :],
                                            ACTF.Copy, bias=0.0, scale=1.0))
            bank_mark[0] = (v, v.n)
            bank_mark[1] = (a, a.n)
            return v.n, a.n

        @blk.sync
        def _(sync):
            sync.dma_start(out=identb[:], in_=id_h[:]).then_inc(ld.h, 16)
            sync.dma_start(out=bbc[:], in_=bb_h[:]).then_inc(ld.h, 16)
            ld.n += 32

        @blk.gpsimd
        def _(gpsimd):
            for d in range(ND):
                gpsimd.dma_start(out=wtb[:, KD * d:KD * (d + 1)],
                                 in_=wt_h[128 * d:128 * (d + 1), :]).then_inc(gld.h, 16)
            gld.n += 16 * ND
        ld_const = ld.n
        gld_const = gld.n

        @blk.vector
        def _(vector):
            vector.memset(c15[:], 1.5)
            vector.memset(ceps[:], EPS)
            vector.memset(ones_c[:], 1.0)
            vector.memset(ones_r[:], 1.0)
            vguard(vector, vector.memset(hbuf[:, 0:STP], 0.0))
            vguard(vector, vector.tensor_copy(ci32[:], c15[:]))
            vguard(vector, vector.tensor_scalar(kap[:], ci32[:], -0.5, 129.0,
                                                ALU.mult, ALU.add))
            vector.wait_ge(ld.h, ld_const)
            vector.wait_ge(gld.h, gld_const)
            v.inc(vector.engine_nop())
        v_const = v.n

        @blk.sync
        def _(sync):
            sync.wait_ge(v.h, v_const)
            sync.dma_start(out=stage_h[0], in_=hbuf[:, 0:STP]).then_inc(skw.h, 16)
            sync.dma_start(out=stage_h[1], in_=hbuf[:, 0:STP]).then_inc(skw.h, 16)
            skw.n += 32
        skw_zero = skw.n

        def _chunks_gu():
            return [(128 * (c // 2), 2048 * (c % 2), 2048,
                     INNER * (c // 2) + 2048 * (c % 2)) for c in range(16)]

        def _chunks_d():
            return [(128 * c, 0, D, D * c) for c in range(NK)]

        WMAT = [(wg_h, wq_g, 16, _chunks_gu(), 0), (wu_h, wq_u, 16, _chunks_gu(), 16),
                (wd_h, wq_d, NK, _chunks_d(), 32)]
        sched = [[] for _ in range(nt)]
        if nt >= 18:
            for c in range(16):
                sched[c // 4].append(("p1", 0, c))
            for c in range(16):
                sched[4 + c // 4].append(("p1", 1, c))
            for c in range(32):
                sched[8 + c // 8].append(("p1", 2, c))
            sched[4].append(("th", 0, 0))
            for c in range(16):
                sched[5 + c // 4].append(("p2", 0, c))
            sched[9].append(("th", 1, 0))
            for c in range(16):
                sched[10 + c // 4].append(("p2", 1, c))
            sched[12].append(("th", 2, 0))
            for c in range(32):
                sched[13 + c // 8].append(("p2", 2, c))
        else:
            for m in range(3):
                nch = WMAT[m][2]
                sched[0] += [("p1", m, c) for c in range(nch)]
                sched[0].append(("th", m, 0))
                sched[0] += [("p2", m, c) for c in range(nch)]

        a_abs = [[0] * 32 for _ in range(3)]
        v_abs = [[0] * 32 for _ in range(3)]
        ld_w = [[0] * 32 for _ in range(3)], [[0] * 32 for _ in range(3)]
        v_lt = [[0] * 32 for _ in range(3)]
        g_stt = [[0] * 32 for _ in range(3)]
        wslot_i = [0]
        wslot_free = [[], []]

        def emit_prep(entry):
            kind, m, c = entry
            wh, wq, nch, chunks, rsoff = WMAT[m]
            if kind in ("p1", "p2"):
                row0, col0, cols, qbase = chunks[c]
                sl = wslot_i[0] % 2
                wslot_i[0] += 1
                p2 = kind == "p2"

                @blk.sync
                def _(sync, m=m, c=c, sl=sl, p2=p2, marks=list(wslot_free[sl]),
                      row0=row0, col0=col0, cols=cols):
                    for s_, n_ in marks:
                        if n_ > 0:
                            sync.wait_ge(s_.h, n_)
                    ld.inc(sync.dma_start(out=wbuf[sl][:, 0:cols],
                                          in_=wh[row0:row0 + 128, col0:col0 + cols]),
                           16)
                    ld_w[1 if p2 else 0][m][c] = ld.n
                if not p2:
                    if c % 2 == 0:
                        @blk.scalar
                        def _(scalar, m=m, c=c, sl=sl, cols=cols):
                            scalar.wait_ge(ld.h, ld_w[0][m][c])
                            if m == 0 and c == 0:
                                scalar.wait_ge(skw.h, skw_zero)
                            a.inc(scalar.activation(
                                hbuf[:, 0:cols], wbuf[sl][:, 0:cols], ACTF.Abs,
                                bias=0.0, scale=1.0,
                                accum_out=rs_abs[:, rsoff + c:rsoff + c + 1]))
                            a_abs[m][c] = a.n
                        wslot_free[sl] = [(a, a_abs[m][c])]
                    else:
                        @blk.vector
                        def _(vector, m=m, c=c, sl=sl, cols=cols):
                            vector.wait_ge(ld.h, ld_w[0][m][c])
                            v.inc(vector.tensor_reduce(
                                rs_abs[:, rsoff + c:rsoff + c + 1],
                                wbuf[sl][:, 0:cols], AX, ALU.add,
                                apply_absolute_value=True))
                            v_abs[m][c] = v.n
                        wslot_free[sl] = [(v, v_abs[m][c])]
                else:
                    @blk.vector
                    def _(vector, m=m, c=c, sl=sl, cols=cols):
                        vector.wait_ge(ld.h, ld_w[1][m][c])
                        if c >= 2:
                            vector.wait_ge(v.h, g_stt[m][c - 2])
                        v.inc(vector.tensor_scalar(tlt[sl][:, 0:cols],
                                                   wbuf[sl][:, 0:cols], negth[m][:],
                                                   None, ALU.is_lt))
                        v_lt[m][c] = v.n

                    @blk.vector
                    def _(vector, m=m, c=c, sl=sl, cols=cols, qbase=qbase):
                        vector.wait_ge(v.h, v_lt[m][c])
                        v.inc(vector.scalar_tensor_tensor(
                            wq[:, qbase:qbase + cols], wbuf[sl][:, 0:cols],
                            thmw[m][:, 0:1], tlt[sl][:, 0:cols],
                            ALU.is_gt, ALU.subtract))
                        g_stt[m][c] = v.n
                    wslot_free[sl] = [(v, g_stt[m][c])]
            else:
                @blk.vector
                def _(vector, m=m):
                    vector.wait_ge(a.h, max(a_abs[m][:nch]))
                    vector.wait_ge(v.h, max(v_abs[m][:nch]))
                    v.inc(vector.tensor_reduce(ST["tred"][:, 0:1],
                                               rs_abs[:, rsoff:rsoff + nch], AX,
                                               ALU.add))

                @blk.tensor
                def _(tensor, m=m, vn=v.n):
                    tensor.wait_ge(v.h, vn)
                    pewait(tensor)
                    t.inc(tensor.matmul(pprep[0:1, 0:1], ones_c[:],
                                        ST["tred"][:, 0:1], start=True, stop=True))

                @blk.vector
                def _(vector, m=m, tn=t.n):
                    vector.wait_ge(t.h, tn)
                    vguard(vector, vector.tensor_copy(tot1[:], pprep[0:1, 0:1]))
                    vguard(vector, vector.tensor_scalar(tot1[:], tot1[:],
                                                        1.0 / (D * INNER), 1e-5,
                                                        ALU.mult, ALU.max))
                    vector.tensor_copy(vec2[:, 1:2], tot1[:])
                    v.inc(vector.tensor_scalar(vec2[:, 0:1], tot1[:], 0.5, None,
                                               ALU.mult))

                @blk.tensor
                def _(tensor, m=m, vn=v.n):
                    tensor.wait_ge(v.h, vn)
                    pewait(tensor)
                    t.inc(tensor.matmul(pprep[:, 0:2], ones_r[:], vec2[:],
                                        start=True, stop=True))

                @blk.vector
                def _(vector, m=m, tn=t.n):
                    vector.wait_ge(t.h, tn)
                    vguard(vector, vector.tensor_copy(thmw[m][:], pprep[:, 0:2]))
                    v.inc(vector.tensor_scalar(negth[m][:], thmw[m][:, 0:1], -1.0,
                                               None, ALU.mult))

        Z = lambda: [0] * (nt + 1)
        ld_x, a_sq, v_var, a_sd, v_ln, a_ybf = Z(), Z(), Z(), Z(), Z(), Z()
        t_ytr, v_yev, a_yev, t_log, v_lb = Z(), Z(), Z(), Z(), Z()
        v_nmx, a_exp, v_rsum, a_wn, skw_n, skr_n = Z(), Z(), Z(), Z(), Z(), Z()
        t_atr, v_aev, a_aev, t_mix, v_mst, st_xp = Z(), Z(), Z(), Z(), Z(), Z()
        ld2, a_sq2, v_var2, a_sd2, g_rmin, v_xq = Z(), Z(), Z(), Z(), Z(), Z()
        a_xqbf, t_xtr, v_xev, a_xev, v_hq, a_hqbf = Z(), Z(), Z(), Z(), Z(), Z()
        v_hqbf2 = Z()
        t_down, v_dst, st_out, g_amax = Z(), Z(), Z(), Z()
        trp_y, trp_xq, trp_hq = Z(), Z(), Z()
        t_gu = [[0] * NI for _ in range(nt)]
        a_silu = [[0] * NI for _ in range(nt)]
        v_hstt = [[0] * NI for _ in range(nt)]
        t_hqtr = [[0] * 4 for _ in range(nt)]
        v_hqev = [[0] * 4 for _ in range(nt)]
        a_hqev = [[0] * 4 for _ in range(nt)]

        @blk.sync
        def _(sync):
            for j in (0, 1):
                if j < nt:
                    sync.wait_ge(v.h, v_const)
                    ld.inc(sync.dma_start(out=xb[j][:],
                                          in_=x_h[TT * j:TT * (j + 1), :]), 16)
                    ld_x[j] = ld.n

        for i in range(nt):

            @blk.vector
            def _(vector, i=i):
                vector.wait_ge(ld.h, ld_x[i])
                vector.bn_stats(bn6[:, 0:6], xb[i % 2][:, 0:512])
                vguard(vector, vector.bn_stats(bn6[:, 6:12], xb[i % 2][:, 512:D]))
                v.inc(vector.bn_aggr(bmv(i), bn6[:]))
                v_var[i] = v.n

            @blk.scalar
            def _(scalar, i=i):
                scalar.wait_ge(v.h, v_var[i])
                a.inc(scalar.activation(sv("sd", i), bvar(i), ACTF.Sqrt,
                                        bias=ceps[:], scale=1.0))
                a_sd[i] = a.n

            @blk.vector
            def _(vector, i=i):
                vector.wait_ge(a.h, a_sd[i])
                vguard(vector, vector.reciprocal(sv("r", i), sv("sd", i)))
                v.inc(vector.tensor_scalar(sv("nmr", i), bmean(i), sv("r", i),
                                           -1.0, ALU.mult, ALU.mult))
                v_ln[i] = v.n

            @blk.scalar
            def _(scalar, i=i):
                scalar.wait_ge(v.h, v_ln[i])
                if i >= 2:
                    scalar.wait_ge(t.h, t_mix[i - 2])
                a.inc(scalar.activation(ybf[i % 2][:], xb[i % 2][:], ACTF.Identity,
                                        bias=sv("nmr", i), scale=sv("r", i)))
                a_ybf[i] = a.n

            tile_done[i] = (a, a_ybf[i])
            if astage < 2:
                continue

            @blk.sync
            def _(sync, i=i):
                sync.wait_ge(a.h, a_ybf[i])
                if i >= 2:
                    sync.wait_ge(t.h, t_log[i - 2])
                sync.wait_ge(trp.h, trp.n)
                trp.inc(sync.dma_start_transpose(
                    yT[i % 2][:].rearrange("p (k f) -> p k f", k=8),
                    ybf[i % 2][:].rearrange("p (k f) -> p k f", k=8)), 16)
                trp_y[i] = trp.n
            tile_done[i] = (a, a_yev[i])
            if astage < 3:
                continue

            @blk.tensor
            def _(tensor, i=i):
                tensor.wait_ge(trp.h, trp_y[i])
                if i >= 1:
                    tensor.wait_ge(v.h, v_lb[i - 1])
                for d in range(ND):
                    pewait(tensor)
                    t.inc(tensor.matmul(pl, yT[i % 2][:, 128 * d:128 * (d + 1)],
                                        wtb[:, KD * d:KD * (d + 1)],
                                        start=(d == 0), stop=(d == ND - 1)))
                t_log[i] = t.n

            @blk.vector
            def _(vector, i=i):
                vector.wait_ge(t.h, t_log[i])
                if i >= 2:
                    vector.wait_ge(a.h, a_exp[i - 2])
                vguard(vector, vector.tensor_tensor(lbs[i % 2][:], pl, bbc[:],
                                                    ALU.add))
                v_lb[i] = v.n
                vguard(vector, vector.tensor_reduce(sv("mx", i), lbs[i % 2][:], AX,
                                                    ALU.max))
                v.inc(vector.tensor_scalar(sv("nmx", i), sv("mx", i), -1.0, None,
                                           ALU.mult))
                v_nmx[i] = v.n

            tile_done[i] = (v, v_nmx[i])
            if astage < 4:
                continue

            @blk.scalar
            def _(scalar, i=i):
                scalar.wait_ge(v.h, v_nmx[i])
                a.inc(scalar.activation(ews[i % 2][:], lbs[i % 2][:], ACTF.Exp,
                                        bias=sv("nmx", i), scale=1.0,
                                        accum_out=sv("esum", i)))
                a_exp[i] = a.n

            @blk.vector
            def _(vector, i=i):
                vector.wait_ge(a.h, a_exp[i])
                v.inc(vector.reciprocal(sv("rsum", i), sv("esum", i)))
                v_rsum[i] = v.n

            @blk.scalar
            def _(scalar, i=i):
                scalar.wait_ge(v.h, v_rsum[i])
                if i >= 2:
                    scalar.wait_ge(skw.h, skw_n[i - 2])
                a.inc(scalar.activation(wns[i % 2][:], ews[i % 2][:], ACTF.Copy,
                                        bias=0.0, scale=sv("rsum", i)))
                a_wn[i] = a.n

            tile_done[i] = (a, a_wn[i])
            if astage < 5:
                continue

            @blk.sync
            def _(sync, i=i):
                sync.wait_ge(a.h, a_wn[i])
                if i >= 2:
                    sync.wait_ge(skr.h, skr_n[i - 2])
                skw.inc(sync.dma_start(out=stage_h[i % 2, :, SKB:SKB + KD],
                                       in_=wns[i % 2][:]), 16)
                skw_n[i] = skw.n

            @blk.gpsimd
            def _(gpsimd, i=i):
                gpsimd.wait_ge(skw.h, skw_n[i])
                if i >= 2:
                    gpsimd.wait_ge(t.h, t_atr[i - 2])
                src = AP(stage_h.tensor, (i % 2) * 128 * STP + SKB,
                         [[STP - 1, 128], [1, 160]])
                skr.inc(gpsimd.dma_start(out=skbb[i % 2][:], in_=src), 16)
                skr_n[i] = skr.n

            tile_done[i] = (skr, skr_n[i])
            if astage < 6:
                continue

            @blk.tensor
            def _(tensor, i=i):
                tensor.wait_ge(skr.h, skr_n[i])
                if i >= 2:
                    tensor.wait_ge(t.h, t_mix[i - 2])

            def _asrc(k, i=i):
                return (skbb[i % 2][:, 0:128] if k == 0
                        else skbb[i % 2][:, 128:160])

            def _adst(k, i=i):
                return A1s[i % 2][:] if k == 0 else A2s[i % 2][:]
            v_aev[i], a_aev_i = tr_group(2, _asrc, _adst)
            a_aev[i] = a_aev_i
            t_atr[i] = t.n
            if dbg == 4:
                @blk.scalar
                def _(scalar, i=i):
                    scalar.wait_ge(v.h, v_aev[i])
                    scalar.wait_ge(a.h, a_aev[i])
                    scalar.dma_start(out=dbg4_A1[i], in_=A1s[i % 2][:]).then_inc(sst.h, 16)
                    scalar.dma_start(out=dbg4_skb[i], in_=skbb[i % 2][:]).then_inc(sst.h, 16)
                    scalar.dma_start(out=dbg4_wn[i], in_=wns[i % 2][:]).then_inc(sst.h, 16)
                    sst.n += 48
            tile_done[i] = (v, v_aev[i])
            if astage < 7:
                continue

            if i >= 1:
                @blk.tensor
                def _(tensor, i=i):
                    tensor.wait_ge(v.h, v_aev[i - 1])
                    tensor.wait_ge(a.h, a_aev[i - 1])
                    tensor.wait_ge(a.h, a_ybf[i])
                    if i >= 2:
                        tensor.wait_ge(v.h, v_mst[i - 2])
                    for half in range(2):
                        cs = slice(512 * half, 512 * (half + 1))
                        pewait(tensor)
                        t.inc(tensor.matmul(pmix[:, cs], A1s[(i - 1) % 2][:],
                                            ybf[(i - 1) % 2][:, cs], start=True,
                                            stop=False))
                        pewait(tensor)
                        t.inc(tensor.matmul(pmix[:, cs], A2s[(i - 1) % 2][:],
                                            ybf[i % 2][0:32, cs], start=False,
                                            stop=True))
                    t_mix[i - 1] = t.n

                @blk.vector
                def _(vector, i=i):
                    vector.wait_ge(t.h, t_mix[i - 1])
                    if i >= 3 and st_xp[i - 3]:
                        vector.wait_ge(sst.h, st_xp[i - 3])
                    v.inc(vector.scalar_tensor_tensor(
                        xpr[(i - 1) % 2][:], pmix[:], 1.0, xb[(i - 1) % 2][:],
                        ALU.mult, ALU.add))
                    v_mst[i - 1] = v.n

                @blk.scalar
                def _(scalar, i=i):
                    scalar.wait_ge(v.h, v_mst[i - 1])
                    sst.inc(scalar.dma_start(out=xp_h[TT * (i - 1):TT * i, :],
                                             in_=xpr[(i - 1) % 2][:]), 16)
                    st_xp[i - 1] = sst.n

            j = i + 1
            if 2 <= j < nt:
                @blk.sync
                def _(sync, i=i, j=j):
                    if astage >= 7:
                        sync.wait_ge(v.h, v_mst[j - 2])
                    elif tile_done[j - 2] is not None:
                        s_, n_ = tile_done[j - 2]
                        sync.wait_ge(s_.h, n_)
                    ld.inc(sync.dma_start(out=xb[j % 2][:],
                                          in_=x_h[TT * j:TT * (j + 1), :]), 16)
                    ld_x[j] = ld.n

            if prep:
                for entry in sched[i]:
                    emit_prep(entry)

        _ = None
        if astage < 7:
            @blk.vector
            def _(vector):
                s_, n_ = tile_done[nt - 1]
                vector.wait_ge(s_.h, n_)
            return nc

        @blk.tensor
        def _(tensor):
            i = nt - 1
            tensor.wait_ge(v.h, v_aev[i])
            tensor.wait_ge(a.h, a_aev[i])
            if i >= 1:
                tensor.wait_ge(v.h, v_mst[i - 1])
            for half in range(2):
                cs = slice(512 * half, 512 * (half + 1))
                pewait(tensor)
                t.inc(tensor.matmul(pmix[:, cs], A1s[i % 2][:], ybf[i % 2][:, cs],
                                    start=True, stop=True))
            t_mix[nt - 1] = t.n

        @blk.vector
        def _(vector):
            i = nt - 1
            vector.wait_ge(t.h, t_mix[i])
            if i >= 3 and st_xp[i - 3]:
                vector.wait_ge(sst.h, st_xp[i - 3])
            v.inc(vector.scalar_tensor_tensor(xpr[i % 2][:], pmix[:], 1.0,
                                              xb[i % 2][:], ALU.mult, ALU.add))
            v_mst[i] = v.n

        @blk.scalar
        def _(scalar):
            i = nt - 1
            scalar.wait_ge(v.h, v_mst[i])
            sst.inc(scalar.dma_start(out=xp_h[TT * i:TT * (i + 1), :],
                                     in_=xpr[i % 2][:]), 16)
            st_xp[i] = sst.n

        a_abs_all = a.n

        if dbg == 2:
            dbgs = [("ybf", ybf[0], BF16), ("yT", yT[0], BF16),
                    ("lb", lbs[0], F32), ("ew", ews[0], F32), ("wn", wns[0], F32),
                    ("skb", skbb[0], BF16), ("A1", A1s[0], BF16),
                    ("A2", A2s[0], BF16), ("stats", STbig, F32)]
        elif dbg == 3:
            dbgs = [("thmw0", thmw[0], F32), ("thmw1", thmw[1], F32),
                    ("thmw2", thmw[2], F32), ("kap", kap, F32),
                    ("wqg", wq_g[:, 0:INNER], FP8), ("wqd", wq_d[:, 0:D], FP8),
                    ("xqbf", xqbf[0], BF16), ("xqT", xqT[0], BF16),
                    ("hqbf", hqbf, BF16), ("stats", STbig, F32)]
            dbg_h = {nm: nc.dram_tensor("dbg_" + nm, list(tt.shape), dt,
                                        kind="ExternalOutput").ap()
                     for nm, tt, dt in dbgs}

            if dbg == 2:
                @blk.sync
                def _(sync):
                    sync.wait_ge(sst.h, sst.n)
                    nst = 0
                    for nm, tt, dt in dbgs:
                        sync.dma_start(out=dbg_h[nm], in_=tt[:]).then_inc(skw.h, 16)
                        nst += 16
                    sync.wait_ge(skw.h, skw.n + nst)
                    skw.n += nst

        if not phase_b:
            @blk.scalar
            def _(scalar):
                scalar.wait_ge(sst.h, sst.n)
            return nc
        def hbq_src(grp, k):
            kk = 8 * grp + k
            return hqbf[:, 128 * kk:128 * (kk + 1)]

        def hbq_dst(grp, k):
            kk = 8 * grp + k
            return hqT[:, 128 * kk:128 * (kk + 1)]

        def emit_hq_down(iq):
            @blk.sync
            def _(sync, iq=iq):
                sync.wait_ge(a.h, a_hqbf[iq])
                sync.wait_ge(v.h, v_hqbf2[iq])
                if iq >= 1:
                    sync.wait_ge(t.h, t_down[iq - 1])
                sync.wait_ge(trp.h, trp.n)
                trp.inc(sync.dma_start_transpose(
                    hqT[:].rearrange("p (k f) -> p k f", k=NK),
                    hqbf[:].rearrange("p (k f) -> p k f", k=NK)), 16)
                trp_hq[iq] = trp.n

            for grp in range(4):
                @blk.tensor
                def _(tensor, iq=iq, grp=grp):
                    tensor.wait_ge(trp.h, trp_hq[iq])
                    if grp == 0:
                        if iq == 0:
                            tensor.wait_ge(v.h, g_stt[2][NK - 1])
                            tensor.wait_ge(v.h, v_mst[nt - 1])
                        elif v_dst[iq - 1]:
                            tensor.wait_ge(v.h, v_dst[iq - 1])
                    for kk in range(8):
                        k = 8 * grp + kk
                        pewait(tensor)
                        t.inc(tensor.matmul(pd[:, 0:512],
                                            hqT[:, 128 * k:128 * (k + 1)],
                                            wq_d[:, D * k:D * k + 512],
                                            start=(k == 0), stop=(k == NK - 1)))
                        pewait(tensor)
                        t.inc(tensor.matmul(pd[:, 512:1024],
                                            hqT[:, 128 * k:128 * (k + 1)],
                                            wq_d[:, D * k + 512:D * (k + 1)],
                                            start=(k == 0), stop=(k == NK - 1)))
                    if grp == 3:
                        t_down[iq] = t.n

            @blk.vector
            def _(vector, iq=iq):
                vector.wait_ge(t.h, t_down[iq])
                if iq >= 2 and st_out[iq - 2]:
                    vector.wait_ge(sst.h, st_out[iq - 2])
                elif iq < 2:
                    vector.wait_ge(sst.h, st_xp[nt - 1])
                v.inc(vector.scalar_tensor_tensor(outb[iq % 2][:], pd[:],
                                                  sv("dqd", iq), xb[iq % 3][:],
                                                  ALU.mult, ALU.add))
                v_dst[iq] = v.n

            @blk.scalar
            def _(scalar, iq=iq):
                scalar.wait_ge(v.h, v_dst[iq])
                sst.inc(scalar.dma_start(out=out_h[TT * iq:TT * (iq + 1), :],
                                         in_=outb[iq % 2][:]), 16)
                st_out[iq] = sst.n

        def front(f):
            @blk.vector
            def _(vector, f=f):
                vector.wait_ge(ld.h, ld2[f])
                vector.bn_stats(bn6[:, 0:6], xb[f % 3][:, 0:512])
                vector.bn_stats(bn6[:, 6:12], xb[f % 3][:, 512:D])
                vector.tensor_reduce(sv("rmax", f), xb[f % 3][:], AX, ALU.max)
                vguard(vector, vector.tensor_reduce(sv("rmin", f), xb[f % 3][:], AX,
                                                    ALU.min))
                v.inc(vector.bn_aggr(bmv(f), bn6[:]))
                v_var2[f] = v.n

            @blk.scalar
            def _(scalar, f=f):
                scalar.wait_ge(v.h, v_var2[f])
                a.inc(scalar.activation(sv("sd", f), bvar(f), ACTF.Sqrt,
                                        bias=ceps[:], scale=1.0))
                a_sd2[f] = a.n

            @blk.vector
            def _(vector, f=f):
                vector.wait_ge(a.h, a_sd2[f])
                if f >= 1:
                    vector.wait_ge(a.h, a_xqbf[f - 1])
                vguard(vector, vector.reciprocal(sv("r", f), sv("sd", f)))
                vector.tensor_tensor(sv("t1", f), sv("rmax", f), bmean(f),
                                     ALU.subtract)
                vguard(vector, vector.tensor_tensor(sv("t2", f), bmean(f),
                                                    sv("rmin", f), ALU.subtract))
                vguard(vector, vector.tensor_tensor(sv("mxd", f), sv("t1", f),
                                                    sv("t2", f), ALU.max))
                vguard(vector, vector.tensor_scalar(sv("mxy", f), sv("mxd", f),
                                                    sv("r", f), None, ALU.mult))
                vguard(vector, vector.tensor_scalar(sv("mxc", f), sv("mxy", f),
                                                    1e-5, None, ALU.max))
                vguard(vector, vector.reciprocal(sv("sinv", f), sv("mxc", f)))
                vector.tensor_scalar(sv("dqx", f), sv("mxc", f), 1.0 / 127, None,
                                     ALU.mult)
                vguard(vector, vector.tensor_scalar(sv("s", f), sv("sinv", f),
                                                    127.0, None, ALU.mult))
                vguard(vector, vector.tensor_scalar(sv("scomb", f), sv("r", f),
                                                    sv("s", f), None, ALU.mult))
                vguard(vector, vector.tensor_scalar(sv("t3", f), bmean(f),
                                                    sv("scomb", f), -1.0,
                                                    ALU.mult, ALU.mult))
                vguard(vector, vector.tensor_tensor(sv("bcomb", f), sv("t3", f),
                                                    kap[:], ALU.add))
                vector.tensor_scalar(sv("dqg", f), sv("dqx", f), thmw[0][:, 1:2],
                                     None, ALU.mult)
                vector.tensor_scalar(sv("dqu", f), sv("dqx", f), thmw[1][:, 1:2],
                                     None, ALU.mult)
                v.inc(vector.tensor_scalar(xq32[:], xb[f % 3][:], sv("scomb", f),
                                           sv("bcomb", f), ALU.mult, ALU.add))
                v_xq[f] = v.n

            @blk.scalar
            def _(scalar, f=f):
                scalar.wait_ge(v.h, v_xq[f])
                if f >= 2:
                    scalar.wait_ge(trp.h, trp_xq[f - 2])
                a.inc(scalar.activation(xqbf[f % 2][:], xq32[:], ACTF.Copy,
                                        bias=-128.0, scale=1.0))
                a_xqbf[f] = a.n

            @blk.sync
            def _(sync, f=f):
                sync.wait_ge(a.h, a_xqbf[f])
                if f >= 2 and t_gu[f - 2][NI - 1]:
                    sync.wait_ge(t.h, t_gu[f - 2][NI - 1])
                sync.wait_ge(trp.h, trp.n)
                trp.inc(sync.dma_start_transpose(
                    xqT[f % 2][:].rearrange("p (k f) -> p k f", k=8),
                    xqbf[f % 2][:].rearrange("p (k f) -> p k f", k=8)), 16)
                trp_xq[f] = trp.n

        def emit_hqbf(i):
            @blk.scalar
            def _(scalar, i=i):
                scalar.wait_ge(v.h, v_hq[i])
                if i >= 1:
                    scalar.wait_ge(trp.h, trp_hq[i - 1])
                a.inc(scalar.activation(hqbf[:, 0:2048], hq32[:, 0:2048],
                                        ACTF.Copy, bias=-128.0, scale=1.0))
                a_hqbf[i] = a.n

            @blk.vector
            def _(vector, i=i):
                if i >= 1:
                    vector.wait_ge(trp.h, trp_hq[i - 1])
                v.inc(vector.tensor_scalar(hqbf[:, 2048:INNER],
                                           hq32[:, 2048:INNER], -128.0, None,
                                           ALU.add))
                v_hqbf2[i] = v.n

        @blk.sync
        def _(sync):
            for j in (0, 1):
                if j < nt:
                    sync.wait_ge(sst.h, st_xp[j])
                    sync.wait_ge(v.h, v_mst[nt - 1])
                    ld.inc(sync.dma_start(out=xb[j % 3][:],
                                          in_=xp_h[TT * j:TT * (j + 1), :]), 16)
                    ld2[j] = ld.n

        front(0)
        if nt > 1:
            front(1)

        for i in range(nt):

            for j in range(NI):
                @blk.tensor
                def _(tensor, i=i, j=j):
                    tensor.wait_ge(trp.h, trp_xq[i])
                    if i == 0 and j == 0:
                        tensor.wait_ge(v.h, g_stt[1][WMAT[1][2] - 1])
                    pj, qi = (j - 2, i) if j >= 2 else (NI - 2 + j, i - 1)
                    if qi >= 0 and a_silu[qi][pj]:
                        tensor.wait_ge(a.h, a_silu[qi][pj])
                        tensor.wait_ge(v.h, v_hstt[qi][pj])
                    for d in range(8):
                        base = INNER * d + 512 * j
                        pewait(tensor)
                        t.inc(tensor.matmul(P512[j % 2][:],
                                            xqT[i % 2][:, 128 * d:128 * (d + 1)],
                                            wq_g[:, base:base + 512],
                                            start=(d == 0), stop=(d == 7)))
                        pewait(tensor)
                        t.inc(tensor.matmul(P512[2 + j % 2][:],
                                            xqT[i % 2][:, 128 * d:128 * (d + 1)],
                                            wq_u[:, base:base + 512],
                                            start=(d == 0), stop=(d == 7)))
                    t_gu[i][j] = t.n

                @blk.scalar
                def _(scalar, i=i, j=j):
                    scalar.wait_ge(t.h, t_gu[i][j])
                    if j >= 2:
                        scalar.wait_ge(v.h, v_hstt[i][j - 2])
                    elif i >= 1:
                        scalar.wait_ge(v.h, v_hstt[i - 1][NI - 2 + j])
                    a.inc(scalar.activation(gtmp[j % 2][:], P512[j % 2][:],
                                            ACTF.Silu, bias=0.0, scale=sv("dqg", i)))
                    a_silu[i][j] = a.n

                @blk.vector
                def _(vector, i=i, j=j):
                    vector.wait_ge(a.h, a_silu[i][j])
                    if i == 0 and j == 0:
                        vector.wait_ge(a.h, a_abs_all)
                    v.inc(vector.scalar_tensor_tensor(
                        hbuf[:, 512 * j:512 * (j + 1)], P512[2 + j % 2][:],
                        sv("dqu", i), gtmp[j % 2][:], ALU.mult, ALU.mult))
                    v_hstt[i][j] = v.n

            @blk.vector
            def _(vector, i=i):
                vector.wait_ge(v.h, v_hstt[i][NI - 1])
                for j in range(NI):
                    ins = vector.tensor_reduce(mxh8[i % 2][:, j:j + 1],
                                               hbuf[:, 512 * j:512 * (j + 1)], AX,
                                               ALU.max, apply_absolute_value=True)
                    if j == NI - 1:
                        vguard(vector, ins)
                vguard(vector, vector.tensor_reduce(sv("mxh", i), mxh8[i % 2][:], AX,
                                                    ALU.max))
                vguard(vector, vector.tensor_scalar(sv("shy", i), sv("mxh", i),
                                                    1e-5, None, ALU.max))
                vguard(vector, vector.reciprocal(sv("shinv", i), sv("shy", i)))
                vguard(vector, vector.tensor_scalar(sv("sh", i), sv("shinv", i),
                                                    127.0, None, ALU.mult))
                vguard(vector, vector.tensor_scalar(sv("dqh", i), sv("shy", i),
                                                    1.0 / 127, None, ALU.mult))
                vector.tensor_scalar(sv("dqd", i), sv("dqh", i), thmw[2][:, 1:2],
                                     None, ALU.mult)
                if i >= 1:
                    vector.wait_ge(a.h, a_hqbf[i - 1])
                v.inc(vector.tensor_scalar(hq32[:], hbuf[:], sv("sh", i), kap[:],
                                           ALU.mult, ALU.add))
                v_hq[i] = v.n

            if i >= 1:
                emit_hq_down(i - 1)
            emit_hqbf(i)

            j = i + 2
            if 2 <= j < nt:
                @blk.sync
                def _(sync, i=i, j=j):
                    sync.wait_ge(sst.h, st_xp[j])
                    if i >= 1:
                        sync.wait_ge(v.h, v_dst[i - 1])
                    ld.inc(sync.dma_start(out=xb[j % 3][:],
                                          in_=xp_h[TT * j:TT * (j + 1), :]), 16)
                    ld2[j] = ld.n
            if i + 1 < nt:
                front(i + 1)

        emit_hq_down(nt - 1)

        if dbg == 3:
            @blk.sync
            def _(sync):
                sync.wait_ge(sst.h, sst.n)
                nst = 0
                for nm, tt, dt in dbgs:
                    sync.dma_start(out=dbg_h[nm], in_=tt[:]).then_inc(skw.h, 16)
                    nst += 16
                sync.wait_ge(skw.h, skw.n + nst)
                skw.n += nst

        @blk.scalar
        def _(scalar):
            scalar.wait_ge(sst.h, sst.n)

    return nc


def kernel(L0_attn_x, L0_attn_W, L0_attn_b, L0_ffn_w_gate, L0_ffn_w_up,
           L0_ffn_w_down):
    x = np.ascontiguousarray(np.asarray(L0_attn_x, dtype=np.float32))
    Wt = np.ascontiguousarray(np.asarray(L0_attn_W, dtype=np.float32).T)
    b_bc = np.ascontiguousarray(
        np.tile(np.asarray(L0_attn_b, dtype=np.float32)[None, :], (128, 1)))
    wgT = np.ascontiguousarray(np.asarray(L0_ffn_w_gate, dtype=np.float32).T)
    wuT = np.ascontiguousarray(np.asarray(L0_ffn_w_up, dtype=np.float32).T)
    wdT = np.ascontiguousarray(np.asarray(L0_ffn_w_down, dtype=np.float32).T)
    ident = np.eye(128, dtype=ml_dtypes.bfloat16)
    nc = build_nc()
    in_maps = [{"x": x[c], "Wt": Wt, "b_bc": b_bc, "wgT": wgT, "wuT": wuT,
                "wdT": wdT, "ident": ident} for c in range(8)]
    res = run_bass_kernel_spmd(nc, in_maps, core_ids=list(range(8)))
    return np.stack([res.results[c]["out"] for c in range(8)], axis=0)


# version 05
# speedup vs baseline: 1.5276x; 1.0541x over previous
import numpy as np
import ml_dtypes
from contextlib import ExitStack

import concourse.bass as bass
import concourse.mybir as mybir
from concourse.bass_types import AP
from concourse.bass_utils import run_bass_kernel_spmd

F32 = mybir.dt.float32
BF16 = mybir.dt.bfloat16
FP8 = mybir.dt.float8e4
I32 = mybir.dt.int32
I16 = mybir.dt.int16
ALU = mybir.AluOpType
ACTF = mybir.ActivationFunctionType
AX = mybir.AxisListType.X

L, D, KD, INNER, TT = 4096, 1024, 32, 4096, 128
ND = D // 128
NI = INNER // 512
NK = INNER // 128
EPS = 1e-5
STP, SKB = 288, 127


class Sem:
    def __init__(self, ctx, nc, name):
        self.h = ctx.enter_context(nc.semaphore(name))
        self.n = 0

    def inc(self, ins, k=1):
        ins.then_inc(self.h, k)
        self.n += k
        return self.n


def build_nc(nt=L // TT, dbg=False, phase_b=True, prep=True, astage=99, serial=False):
    nc = bass.Bass("TRN2", target_bir_lowering=False, detect_race_conditions=False)
    Ltot = nt * TT

    x_h = nc.dram_tensor("x", [Ltot, D], F32, kind="ExternalInput").ap()
    wt_h = nc.dram_tensor("Wt", [D, KD], F32, kind="ExternalInput").ap()
    bb_h = nc.dram_tensor("b_bc", [128, KD], F32, kind="ExternalInput").ap()
    wg_h = nc.dram_tensor("wgT", [D, INNER], F32, kind="ExternalInput").ap()
    wu_h = nc.dram_tensor("wuT", [D, INNER], F32, kind="ExternalInput").ap()
    wd_h = nc.dram_tensor("wdT", [INNER, D], F32, kind="ExternalInput").ap()
    id_h = nc.dram_tensor("ident", [128, 128], BF16, kind="ExternalInput").ap()
    out_h = nc.dram_tensor("out", [Ltot, D], F32, kind="ExternalOutput").ap()
    xp_h = nc.dram_tensor("xprime", [Ltot, D], F32,
                          kind="ExternalOutput" if dbg else "Internal").ap()
    stage_h = nc.dram_tensor("stage", [2, 128, STP], F32, kind="Internal").ap()

    ctx = ExitStack()
    with ctx:
        sbuf = lambda name, shape, dt: ctx.enter_context(nc.sbuf_tensor(name, shape, dt))
        wq_g = sbuf("wq_g", [128, ND * INNER], FP8)
        wq_u = sbuf("wq_u", [128, ND * INNER], FP8)
        wq_d = sbuf("wq_d", [128, NK * D], FP8)
        wbuf = [sbuf(f"wbuf{k}", [128, 2048], F32) for k in range(2)]
        tlt = [sbuf(f"tlt{k}", [128, 2048], BF16) for k in range(2)]
        rs_abs = sbuf("rs_abs", [128, 64], F32)
        thmw = [sbuf(f"thmw{m}", [128, 2], F32) for m in range(3)]
        negth = [sbuf(f"negth{m}", [128, 1], F32) for m in range(3)]
        vec2 = sbuf("vec2", [1, 2], F32)
        tot1 = sbuf("tot1", [1, 1], F32)
        ones_c = sbuf("ones_c", [128, 1], F32)
        ones_r = sbuf("ones_r", [1, 128], F32)
        identb = sbuf("identb", [128, 128], BF16)
        wtb = sbuf("wtb", [128, ND * KD], BF16)
        bbc = sbuf("bbc", [128, KD], F32)
        kap = sbuf("kap", [128, 1], F32)
        ceps = sbuf("ceps", [128, 1], F32)
        c15 = sbuf("c15", [128, 1], F32)
        ci32 = sbuf("ci32", [128, 1], I32)
        bn6 = sbuf("bn6", [128, 12], F32)
        bnmv = sbuf("bnmv", [128, 4], F32)
        bmv = lambda i: bnmv[:, 2 * (i % 2):2 * (i % 2) + 2]
        bmean = lambda i: bnmv[:, 2 * (i % 2):2 * (i % 2) + 1]
        bvar = lambda i: bnmv[:, 2 * (i % 2) + 1:2 * (i % 2) + 2]
        xb = [sbuf(f"xb{k}", [128, D], F32) for k in range(3)]
        xpr = [sbuf(f"xpr{k}", [128, D], F32) for k in range(2)]
        outb = xpr
        ybf = [sbuf(f"ybf{k}", [128, D], BF16) for k in range(2)]
        yT = [sbuf(f"yT{k}", [128, D], BF16) for k in range(2)]
        lbs = [sbuf(f"lb{k}", [128, KD], F32) for k in range(2)]
        ews = [sbuf(f"ew{k}", [128, KD], F32) for k in range(2)]
        wns = [sbuf(f"wn{k}", [128, KD], F32) for k in range(2)]
        skbb = [sbuf(f"skb{k}", [128, 256], BF16) for k in range(2)]
        ATs = [sbuf(f"AT{k}", [128, 256], BF16) for k in range(2)]
        A1 = lambda q: ATs[q % 2][:, 0:128]
        A2 = lambda q: ATs[q % 2][0:32, 128:256]
        xq32 = sbuf("xq32", [128, D], I16)
        xqbf = [sbuf(f"xqbf{k}", [128, D], BF16) for k in range(2)]
        xqT = [sbuf(f"xqT{k}", [128, D], BF16) for k in range(2)]
        hbuf = sbuf("hbuf", [128, INNER], F32)
        hq32 = sbuf("hq32", [128, INNER], I16)
        hqbf = sbuf("hqbf", [128, INNER], BF16)
        hqT = sbuf("hqT", [128, INNER], BF16)
        gtmp = [sbuf(f"gtmp{k}", [128, 512], F32) for k in range(2)]
        mxh8 = [sbuf(f"mxh8{k}", [128, 8], F32) for k in range(2)]
        names = ["rs", "m", "var", "sd", "r", "nmr", "mx", "nmx", "ssq", "esum",
                 "rsum", "rmax", "rmin", "t1", "t2", "mxd", "mxy", "mxc", "sinv",
                 "s", "dqx", "scomb", "t3", "bcomb", "dqg", "dqu", "mxh", "shy",
                 "shinv", "sh", "dqh", "dqd", "ss2", "tred"]
        STbig = sbuf("STbig", [128, 2 * len(names)], F32)
        ST = {n: STbig[:, 2 * k:2 * k + 2] for k, n in enumerate(names)}

        def sv(n, i):
            k = names.index(n)
            return STbig[:, 2 * k + (i % 2):2 * k + (i % 2) + 1]

        P512 = [ctx.enter_context(nc.psum_tensor(f"p512_{k}", [128, 512], F32))
                for k in range(4)]
        P1024 = ctx.enter_context(nc.psum_tensor("p1024", [128, 1024], F32))
        PMIXH = ctx.enter_context(nc.psum_tensor("pmixh", [128, 512], F32))
        P32 = ctx.enter_context(nc.psum_tensor("p32", [128, KD], F32))
        pl = P32[:]
        pd = P1024
        pprep1 = P1024[0:1, 992:993]
        pprep2 = P1024[:, 992:994]

        if dbg == 4:
            dbg4_A1 = nc.dram_tensor("dbg4_A1", [nt, 128, 128], BF16, kind="ExternalOutput").ap()
            dbg4_skb = nc.dram_tensor("dbg4_skb", [nt, 128, 160], BF16, kind="ExternalOutput").ap()
            dbg4_wn = nc.dram_tensor("dbg4_wn", [nt, 128, KD], F32, kind="ExternalOutput").ap()

        ld = Sem(ctx, nc, "ld")
        skw = Sem(ctx, nc, "skw")
        skr = Sem(ctx, nc, "skr")
        sst = Sem(ctx, nc, "sst")
        v = Sem(ctx, nc, "v")
        a = Sem(ctx, nc, "a")
        t = Sem(ctx, nc, "t")
        g = Sem(ctx, nc, "g")
        gld = Sem(ctx, nc, "gld")
        trp = Sem(ctx, nc, "trp")
        blk = ctx.enter_context(nc.Block())

        tile_done = [None] * (nt + 1)
        PEW = 4

        def pemm(tensor, ins):
            t.inc(ins)

        def vguard(vector, ins):
            v.inc(ins)
            vector.wait_ge(v.h, v.n)

        def pewait(tensor):
            if t.n >= PEW and t.n % 2 == 0:
                tensor.wait_ge(t.h, t.n - 2)

        @blk.sync
        def _(sync):
            sync.dma_start(out=identb[:], in_=id_h[:]).then_inc(ld.h, 16)
            sync.dma_start(out=bbc[:], in_=bb_h[:]).then_inc(ld.h, 16)
            ld.n += 32

        @blk.gpsimd
        def _(gpsimd):
            for d in range(ND):
                gpsimd.dma_start(out=wtb[:, KD * d:KD * (d + 1)],
                                 in_=wt_h[128 * d:128 * (d + 1), :]).then_inc(gld.h, 16)
            gld.n += 16 * ND
        ld_const = ld.n
        gld_const = gld.n

        @blk.vector
        def _(vector):
            vector.memset(c15[:], 1.5)
            vector.memset(ceps[:], EPS)
            vector.memset(ones_c[:], 1.0)
            vector.memset(ones_r[:], 1.0)
            vguard(vector, vector.memset(hbuf[:, 0:STP], 0.0))
            vguard(vector, vector.tensor_copy(ci32[:], c15[:]))
            vguard(vector, vector.tensor_scalar(kap[:], ci32[:], -0.5, 129.0,
                                                ALU.mult, ALU.add))
            vector.wait_ge(ld.h, ld_const)
            vector.wait_ge(gld.h, gld_const)
            v.inc(vector.engine_nop())
        v_const = v.n

        @blk.sync
        def _(sync):
            sync.wait_ge(v.h, v_const)
            sync.dma_start(out=stage_h[0], in_=hbuf[:, 0:STP]).then_inc(skw.h, 16)
            sync.dma_start(out=stage_h[1], in_=hbuf[:, 0:STP]).then_inc(skw.h, 16)
            skw.n += 32
        skw_zero = skw.n

        def _chunks_gu():
            return [(128 * (c // 2), 2048 * (c % 2), 2048,
                     INNER * (c // 2) + 2048 * (c % 2)) for c in range(16)]

        def _chunks_d():
            return [(128 * c, 0, D, D * c) for c in range(NK)]

        WMAT = [(wg_h, wq_g, 16, _chunks_gu(), 0), (wu_h, wq_u, 16, _chunks_gu(), 16),
                (wd_h, wq_d, NK, _chunks_d(), 32)]
        sched = [[] for _ in range(nt)]
        if nt >= 18:
            for c in range(16):
                sched[c // 4].append(("p1", 0, c))
            for c in range(16):
                sched[4 + c // 4].append(("p1", 1, c))
            for c in range(32):
                sched[8 + c // 8].append(("p1", 2, c))
            sched[4].append(("th", 0, 0))
            for c in range(16):
                sched[5 + c // 4].append(("p2", 0, c))
            sched[9].append(("th", 1, 0))
            for c in range(16):
                sched[10 + c // 4].append(("p2", 1, c))
            sched[12].append(("th", 2, 0))
            for c in range(32):
                sched[13 + c // 8].append(("p2", 2, c))
        else:
            for m in range(3):
                nch = WMAT[m][2]
                sched[0] += [("p1", m, c) for c in range(nch)]
                sched[0].append(("th", m, 0))
                sched[0] += [("p2", m, c) for c in range(nch)]

        a_abs = [[0] * 32 for _ in range(3)]
        v_abs = [[0] * 32 for _ in range(3)]
        ld_w = [[0] * 32 for _ in range(3)], [[0] * 32 for _ in range(3)]
        v_lt = [[0] * 32 for _ in range(3)]
        g_stt = [[0] * 32 for _ in range(3)]
        wslot_i = [0]
        wslot_free = [[], []]

        def emit_prep(entry):
            kind, m, c = entry
            wh, wq, nch, chunks, rsoff = WMAT[m]
            if kind in ("p1", "p2"):
                row0, col0, cols, qbase = chunks[c]
                sl = wslot_i[0] % 2
                wslot_i[0] += 1
                p2 = kind == "p2"

                @blk.sync
                def _(sync, m=m, c=c, sl=sl, p2=p2, marks=list(wslot_free[sl]),
                      row0=row0, col0=col0, cols=cols):
                    for s_, n_ in marks:
                        if n_ > 0:
                            sync.wait_ge(s_.h, n_)
                    ld.inc(sync.dma_start(out=wbuf[sl][:, 0:cols],
                                          in_=wh[row0:row0 + 128, col0:col0 + cols]),
                           16)
                    ld_w[1 if p2 else 0][m][c] = ld.n
                if not p2:
                    if c % 2 == 0:
                        @blk.scalar
                        def _(scalar, m=m, c=c, sl=sl, cols=cols):
                            scalar.wait_ge(ld.h, ld_w[0][m][c])
                            if m == 0 and c == 0:
                                scalar.wait_ge(skw.h, skw_zero)
                            a.inc(scalar.activation(
                                hbuf[:, 0:cols], wbuf[sl][:, 0:cols], ACTF.Abs,
                                bias=0.0, scale=1.0,
                                accum_out=rs_abs[:, rsoff + c:rsoff + c + 1]))
                            a_abs[m][c] = a.n
                        wslot_free[sl] = [(a, a_abs[m][c])]
                    else:
                        @blk.vector
                        def _(vector, m=m, c=c, sl=sl, cols=cols):
                            vector.wait_ge(ld.h, ld_w[0][m][c])
                            v.inc(vector.tensor_reduce(
                                rs_abs[:, rsoff + c:rsoff + c + 1],
                                wbuf[sl][:, 0:cols], AX, ALU.add,
                                apply_absolute_value=True))
                            v_abs[m][c] = v.n
                        wslot_free[sl] = [(v, v_abs[m][c])]
                else:
                    @blk.vector
                    def _(vector, m=m, c=c, sl=sl, cols=cols):
                        vector.wait_ge(ld.h, ld_w[1][m][c])
                        if c >= 2:
                            vector.wait_ge(v.h, g_stt[m][c - 2])
                        v.inc(vector.tensor_scalar(tlt[sl][:, 0:cols],
                                                   wbuf[sl][:, 0:cols], negth[m][:],
                                                   None, ALU.is_lt))
                        v_lt[m][c] = v.n

                    @blk.vector
                    def _(vector, m=m, c=c, sl=sl, cols=cols, qbase=qbase):
                        vector.wait_ge(v.h, v_lt[m][c])
                        v.inc(vector.scalar_tensor_tensor(
                            wq[:, qbase:qbase + cols], wbuf[sl][:, 0:cols],
                            thmw[m][:, 0:1], tlt[sl][:, 0:cols],
                            ALU.is_gt, ALU.subtract))
                        g_stt[m][c] = v.n
                    wslot_free[sl] = [(v, g_stt[m][c])]
            else:
                @blk.vector
                def _(vector, m=m):
                    vector.wait_ge(a.h, max(a_abs[m][:nch]))
                    vector.wait_ge(v.h, max(v_abs[m][:nch]))
                    v.inc(vector.tensor_reduce(ST["tred"][:, 0:1],
                                               rs_abs[:, rsoff:rsoff + nch], AX,
                                               ALU.add))

                @blk.tensor
                def _(tensor, m=m, vn=v.n):
                    tensor.wait_ge(v.h, vn)
                    pewait(tensor)
                    t.inc(tensor.matmul(pprep1, ones_c[:],
                                        ST["tred"][:, 0:1], start=True, stop=True))

                @blk.vector
                def _(vector, m=m, tn=t.n):
                    vector.wait_ge(t.h, tn)
                    vguard(vector, vector.tensor_copy(tot1[:], pprep1))
                    vguard(vector, vector.tensor_scalar(tot1[:], tot1[:],
                                                        1.0 / (D * INNER), 1e-5,
                                                        ALU.mult, ALU.max))
                    vector.tensor_copy(vec2[:, 1:2], tot1[:])
                    v.inc(vector.tensor_scalar(vec2[:, 0:1], tot1[:], 0.5, None,
                                               ALU.mult))

                @blk.tensor
                def _(tensor, m=m, vn=v.n):
                    tensor.wait_ge(v.h, vn)
                    pewait(tensor)
                    t.inc(tensor.matmul(pprep2, ones_r[:], vec2[:],
                                        start=True, stop=True))

                @blk.vector
                def _(vector, m=m, tn=t.n):
                    vector.wait_ge(t.h, tn)
                    vguard(vector, vector.tensor_copy(thmw[m][:], pprep2))
                    v.inc(vector.tensor_scalar(negth[m][:], thmw[m][:, 0:1], -1.0,
                                               None, ALU.mult))

        Z = lambda: [0] * (nt + 1)
        ld_x, a_sq, v_var, a_sd, v_ln, a_ybf = Z(), Z(), Z(), Z(), Z(), Z()
        t_ytr, v_yev, a_yev, t_log, v_lb = Z(), Z(), Z(), Z(), Z()
        v_nmx, a_exp, v_rsum, a_wn, skw_n, skr_n = Z(), Z(), Z(), Z(), Z(), Z()
        trp_at, t_mix, v_mst, st_xp = Z(), Z(), Z(), Z()
        mix_pe = [0]
        ld2, a_sq2, v_var2, a_sd2, g_rmin, v_xq = Z(), Z(), Z(), Z(), Z(), Z()
        a_xqbf, t_xtr, v_xev, a_xev, v_hq, a_hqbf = Z(), Z(), Z(), Z(), Z(), Z()
        v_hqbf2 = Z()
        t_down, v_dst, st_out, g_amax = Z(), Z(), Z(), Z()
        trp_y, trp_xq, trp_hq = Z(), Z(), Z()
        t_gu = [[0] * NI for _ in range(nt)]
        a_silu = [[0] * NI for _ in range(nt)]
        v_hstt = [[0] * NI for _ in range(nt)]
        t_hqtr = [[0] * 4 for _ in range(nt)]
        v_hqev = [[0] * 4 for _ in range(nt)]
        a_hqev = [[0] * 4 for _ in range(nt)]

        @blk.sync
        def _(sync):
            for j in (0, 1):
                if j < nt:
                    sync.wait_ge(v.h, v_const)
                    ld.inc(sync.dma_start(out=xb[j][:],
                                          in_=x_h[TT * j:TT * (j + 1), :]), 16)
                    ld_x[j] = ld.n

        for i in range(nt):

            @blk.vector
            def _(vector, i=i):
                vector.wait_ge(ld.h, ld_x[i])
                vector.bn_stats(bn6[:, 0:6], xb[i % 2][:, 0:512])
                vguard(vector, vector.bn_stats(bn6[:, 6:12], xb[i % 2][:, 512:D]))
                v.inc(vector.bn_aggr(bmv(i), bn6[:]))
                v_var[i] = v.n

            @blk.scalar
            def _(scalar, i=i):
                scalar.wait_ge(v.h, v_var[i])
                a.inc(scalar.activation(sv("sd", i), bvar(i), ACTF.Sqrt,
                                        bias=ceps[:], scale=1.0))
                a_sd[i] = a.n

            @blk.vector
            def _(vector, i=i):
                vector.wait_ge(a.h, a_sd[i])
                vguard(vector, vector.reciprocal(sv("r", i), sv("sd", i)))
                v.inc(vector.tensor_scalar(sv("nmr", i), bmean(i), sv("r", i),
                                           -1.0, ALU.mult, ALU.mult))
                v_ln[i] = v.n

            @blk.scalar
            def _(scalar, i=i):
                scalar.wait_ge(v.h, v_ln[i])
                if i >= 2:
                    scalar.wait_ge(t.h, t_mix[i - 2])
                a.inc(scalar.activation(ybf[i % 2][:], xb[i % 2][:], ACTF.Identity,
                                        bias=sv("nmr", i), scale=sv("r", i)))
                a_ybf[i] = a.n

            tile_done[i] = (a, a_ybf[i])
            if astage < 2:
                continue

            @blk.sync
            def _(sync, i=i):
                sync.wait_ge(a.h, a_ybf[i])
                if i >= 2:
                    sync.wait_ge(t.h, t_log[i - 2])
                sync.wait_ge(trp.h, trp.n)
                trp.inc(sync.dma_start_transpose(
                    yT[i % 2][:].rearrange("p (k f) -> p k f", k=8),
                    ybf[i % 2][:].rearrange("p (k f) -> p k f", k=8)), 16)
                trp_y[i] = trp.n
            tile_done[i] = (a, a_yev[i])
            if astage < 3:
                continue

            @blk.tensor
            def _(tensor, i=i):
                tensor.wait_ge(trp.h, trp_y[i])
                if i >= 1:
                    tensor.wait_ge(v.h, v_lb[i - 1])
                for d in range(ND):
                    pewait(tensor)
                    t.inc(tensor.matmul(pl, yT[i % 2][:, 128 * d:128 * (d + 1)],
                                        wtb[:, KD * d:KD * (d + 1)],
                                        start=(d == 0), stop=(d == ND - 1)))
                t_log[i] = t.n

            @blk.vector
            def _(vector, i=i):
                vector.wait_ge(t.h, t_log[i])
                if i >= 2:
                    vector.wait_ge(a.h, a_exp[i - 2])
                vguard(vector, vector.tensor_tensor(lbs[i % 2][:], pl, bbc[:],
                                                    ALU.add))
                v_lb[i] = v.n
                vguard(vector, vector.tensor_reduce(sv("mx", i), lbs[i % 2][:], AX,
                                                    ALU.max))
                v.inc(vector.tensor_scalar(sv("nmx", i), sv("mx", i), -1.0, None,
                                           ALU.mult))
                v_nmx[i] = v.n

            tile_done[i] = (v, v_nmx[i])
            if astage < 4:
                continue

            @blk.scalar
            def _(scalar, i=i):
                scalar.wait_ge(v.h, v_nmx[i])
                a.inc(scalar.activation(ews[i % 2][:], lbs[i % 2][:], ACTF.Exp,
                                        bias=sv("nmx", i), scale=1.0,
                                        accum_out=sv("esum", i)))
                a_exp[i] = a.n

            @blk.vector
            def _(vector, i=i):
                vector.wait_ge(a.h, a_exp[i])
                v.inc(vector.reciprocal(sv("rsum", i), sv("esum", i)))
                v_rsum[i] = v.n

            @blk.scalar
            def _(scalar, i=i):
                scalar.wait_ge(v.h, v_rsum[i])
                if i >= 2:
                    scalar.wait_ge(skw.h, skw_n[i - 2])
                a.inc(scalar.activation(wns[i % 2][:], ews[i % 2][:], ACTF.Copy,
                                        bias=0.0, scale=sv("rsum", i)))
                a_wn[i] = a.n

            tile_done[i] = (a, a_wn[i])
            if astage < 5:
                continue

            @blk.sync
            def _(sync, i=i):
                sync.wait_ge(a.h, a_wn[i])
                if i >= 2:
                    sync.wait_ge(skr.h, skr_n[i - 2])
                skw.inc(sync.dma_start(out=stage_h[i % 2, :, SKB:SKB + KD],
                                       in_=wns[i % 2][:]), 16)
                skw_n[i] = skw.n

            @blk.gpsimd
            def _(gpsimd, i=i):
                gpsimd.wait_ge(skw.h, skw_n[i])
                if i >= 2:
                    gpsimd.wait_ge(trp.h, trp_at[i - 2])
                src = AP(stage_h.tensor, (i % 2) * 128 * STP + SKB,
                         [[STP - 1, 128], [1, 160]])
                skr.inc(gpsimd.dma_start(out=skbb[i % 2][:, 0:160], in_=src), 16)
                skr_n[i] = skr.n

            tile_done[i] = (skr, skr_n[i])
            if astage < 6:
                continue

            @blk.sync
            def _(sync, i=i):
                sync.wait_ge(skr.h, skr_n[i])
                if i >= 2:
                    sync.wait_ge(t.h, t_mix[i - 2])
                sync.wait_ge(trp.h, trp.n)
                trp.inc(sync.dma_start_transpose(
                    ATs[i % 2][:].rearrange("p (k f) -> p k f", k=2),
                    skbb[i % 2][:].rearrange("p (k f) -> p k f", k=2)), 16)
                trp_at[i] = trp.n
            tile_done[i] = (trp, trp_at[i])
            if astage < 7:
                continue

            if i >= 1:
                for half in range(2):
                    @blk.tensor
                    def _(tensor, i=i, half=half, pv=mix_pe[0]):
                        q = i - 1
                        cs = slice(512 * half, 512 * (half + 1))
                        if half == 0:
                            tensor.wait_ge(trp.h, trp_at[q])
                            tensor.wait_ge(a.h, a_ybf[i])
                        if pv:
                            tensor.wait_ge(v.h, pv)
                        pewait(tensor)
                        t.inc(tensor.matmul(PMIXH[:], A1(q), ybf[q % 2][:, cs],
                                            start=True, stop=False))
                        pewait(tensor)
                        t.inc(tensor.matmul(PMIXH[:], A2(q),
                                            ybf[i % 2][0:32, cs], start=False,
                                            stop=True))
                    if half == 1:
                        t_mix[i - 1] = t.n
                    tmark = t.n

                    @blk.vector
                    def _(vector, i=i, half=half, tmark=tmark):
                        q = i - 1
                        cs = slice(512 * half, 512 * (half + 1))
                        vector.wait_ge(t.h, tmark)
                        if half == 0 and q >= 2 and st_xp[q - 2]:
                            vector.wait_ge(sst.h, st_xp[q - 2])
                        v.inc(vector.scalar_tensor_tensor(
                            xpr[q % 2][:, cs], PMIXH[:], 1.0, xb[q % 2][:, cs],
                            ALU.mult, ALU.add))
                        if half == 1:
                            v_mst[q] = v.n
                    mix_pe[0] = v.n

                @blk.scalar
                def _(scalar, i=i):
                    scalar.wait_ge(v.h, v_mst[i - 1])
                    sst.inc(scalar.dma_start(out=xp_h[TT * (i - 1):TT * i, :],
                                             in_=xpr[(i - 1) % 2][:]), 16)
                    st_xp[i - 1] = sst.n

            j = i + 1
            if 2 <= j < nt:
                @blk.sync
                def _(sync, i=i, j=j):
                    if astage >= 7:
                        sync.wait_ge(v.h, v_mst[j - 2])
                    elif tile_done[j - 2] is not None:
                        s_, n_ = tile_done[j - 2]
                        sync.wait_ge(s_.h, n_)
                    ld.inc(sync.dma_start(out=xb[j % 2][:],
                                          in_=x_h[TT * j:TT * (j + 1), :]), 16)
                    ld_x[j] = ld.n

            if prep:
                for entry in sched[i]:
                    emit_prep(entry)

        _ = None
        if astage < 7:
            @blk.vector
            def _(vector):
                s_, n_ = tile_done[nt - 1]
                vector.wait_ge(s_.h, n_)
            return nc

        for half in range(2):
            @blk.tensor
            def _(tensor, half=half, pv=mix_pe[0]):
                q = nt - 1
                cs = slice(512 * half, 512 * (half + 1))
                if half == 0:
                    tensor.wait_ge(trp.h, trp_at[q])
                if pv:
                    tensor.wait_ge(v.h, pv)
                pewait(tensor)
                t.inc(tensor.matmul(PMIXH[:], A1(q), ybf[q % 2][:, cs],
                                    start=True, stop=True))
            if half == 1:
                t_mix[nt - 1] = t.n
            tmark = t.n

            @blk.vector
            def _(vector, half=half, tmark=tmark):
                q = nt - 1
                cs = slice(512 * half, 512 * (half + 1))
                vector.wait_ge(t.h, tmark)
                if half == 0 and q >= 2 and st_xp[q - 2]:
                    vector.wait_ge(sst.h, st_xp[q - 2])
                v.inc(vector.scalar_tensor_tensor(
                    xpr[q % 2][:, cs], PMIXH[:], 1.0, xb[q % 2][:, cs],
                    ALU.mult, ALU.add))
                if half == 1:
                    v_mst[q] = v.n
            mix_pe[0] = v.n

        @blk.scalar
        def _(scalar):
            i = nt - 1
            scalar.wait_ge(v.h, v_mst[i])
            sst.inc(scalar.dma_start(out=xp_h[TT * i:TT * (i + 1), :],
                                     in_=xpr[i % 2][:]), 16)
            st_xp[i] = sst.n

        a_abs_all = a.n

        if dbg == 2:
            dbgs = [("ybf", ybf[0], BF16), ("yT", yT[0], BF16),
                    ("lb", lbs[0], F32), ("ew", ews[0], F32), ("wn", wns[0], F32),
                    ("skb", skbb[0], BF16), ("AT", ATs[0], BF16),
                    ("stats", STbig, F32)]
        elif dbg == 3:
            dbgs = [("thmw0", thmw[0], F32), ("thmw1", thmw[1], F32),
                    ("thmw2", thmw[2], F32), ("kap", kap, F32),
                    ("wqg", wq_g[:, 0:INNER], FP8), ("wqd", wq_d[:, 0:D], FP8),
                    ("xqbf", xqbf[0], BF16), ("xqT", xqT[0], BF16),
                    ("hqbf", hqbf, BF16), ("stats", STbig, F32)]
            dbg_h = {nm: nc.dram_tensor("dbg_" + nm, list(tt.shape), dt,
                                        kind="ExternalOutput").ap()
                     for nm, tt, dt in dbgs}

            if dbg == 2:
                @blk.sync
                def _(sync):
                    sync.wait_ge(sst.h, sst.n)
                    nst = 0
                    for nm, tt, dt in dbgs:
                        sync.dma_start(out=dbg_h[nm], in_=tt[:]).then_inc(skw.h, 16)
                        nst += 16
                    sync.wait_ge(skw.h, skw.n + nst)
                    skw.n += nst

        if not phase_b:
            @blk.scalar
            def _(scalar):
                scalar.wait_ge(sst.h, sst.n)
            return nc
        def hbq_src(grp, k):
            kk = 8 * grp + k
            return hqbf[:, 128 * kk:128 * (kk + 1)]

        def hbq_dst(grp, k):
            kk = 8 * grp + k
            return hqT[:, 128 * kk:128 * (kk + 1)]

        def emit_hq_down(iq):
            @blk.sync
            def _(sync, iq=iq):
                sync.wait_ge(a.h, a_hqbf[iq])
                sync.wait_ge(v.h, v_hqbf2[iq])
                if iq >= 1:
                    sync.wait_ge(t.h, t_down[iq - 1])
                sync.wait_ge(trp.h, trp.n)
                trp.inc(sync.dma_start_transpose(
                    hqT[:].rearrange("p (k f) -> p k f", k=NK),
                    hqbf[:].rearrange("p (k f) -> p k f", k=NK)), 16)
                trp_hq[iq] = trp.n

            for grp in range(4):
                @blk.tensor
                def _(tensor, iq=iq, grp=grp):
                    tensor.wait_ge(trp.h, trp_hq[iq])
                    if grp == 0:
                        if iq == 0:
                            tensor.wait_ge(v.h, g_stt[2][NK - 1])
                        elif v_dst[iq - 1]:
                            tensor.wait_ge(v.h, v_dst[iq - 1])
                    for kk in range(8):
                        k = 8 * grp + kk
                        pewait(tensor)
                        t.inc(tensor.matmul(pd[:, 0:512],
                                            hqT[:, 128 * k:128 * (k + 1)],
                                            wq_d[:, D * k:D * k + 512],
                                            start=(k == 0), stop=(k == NK - 1)))
                        pewait(tensor)
                        t.inc(tensor.matmul(pd[:, 512:1024],
                                            hqT[:, 128 * k:128 * (k + 1)],
                                            wq_d[:, D * k + 512:D * (k + 1)],
                                            start=(k == 0), stop=(k == NK - 1)))
                    if grp == 3:
                        t_down[iq] = t.n

            @blk.vector
            def _(vector, iq=iq):
                vector.wait_ge(t.h, t_down[iq])
                if iq >= 2 and st_out[iq - 2]:
                    vector.wait_ge(sst.h, st_out[iq - 2])
                elif iq < 2:
                    vector.wait_ge(sst.h, st_xp[nt - 1])
                v.inc(vector.scalar_tensor_tensor(outb[iq % 2][:], pd[:],
                                                  sv("dqd", iq), xb[iq % 3][:],
                                                  ALU.mult, ALU.add))
                v_dst[iq] = v.n

            @blk.scalar
            def _(scalar, iq=iq):
                scalar.wait_ge(v.h, v_dst[iq])
                sst.inc(scalar.dma_start(out=out_h[TT * iq:TT * (iq + 1), :],
                                         in_=outb[iq % 2][:]), 16)
                st_out[iq] = sst.n

        def front(f):
            @blk.vector
            def _(vector, f=f):
                vector.wait_ge(ld.h, ld2[f])
                vector.bn_stats(bn6[:, 0:6], xb[f % 3][:, 0:512])
                vector.bn_stats(bn6[:, 6:12], xb[f % 3][:, 512:D])
                vector.tensor_reduce(sv("rmax", f), xb[f % 3][:], AX, ALU.max)
                vguard(vector, vector.tensor_reduce(sv("rmin", f), xb[f % 3][:], AX,
                                                    ALU.min))
                v.inc(vector.bn_aggr(bmv(f), bn6[:]))
                v_var2[f] = v.n

            @blk.scalar
            def _(scalar, f=f):
                scalar.wait_ge(v.h, v_var2[f])
                a.inc(scalar.activation(sv("sd", f), bvar(f), ACTF.Sqrt,
                                        bias=ceps[:], scale=1.0))
                a_sd2[f] = a.n

            @blk.vector
            def _(vector, f=f):
                vector.wait_ge(a.h, a_sd2[f])
                if f >= 1:
                    vector.wait_ge(a.h, a_xqbf[f - 1])
                vguard(vector, vector.reciprocal(sv("r", f), sv("sd", f)))
                vector.tensor_tensor(sv("t1", f), sv("rmax", f), bmean(f),
                                     ALU.subtract)
                vguard(vector, vector.tensor_tensor(sv("t2", f), bmean(f),
                                                    sv("rmin", f), ALU.subtract))
                vguard(vector, vector.tensor_tensor(sv("mxd", f), sv("t1", f),
                                                    sv("t2", f), ALU.max))
                vguard(vector, vector.tensor_scalar(sv("mxy", f), sv("mxd", f),
                                                    sv("r", f), None, ALU.mult))
                vguard(vector, vector.tensor_scalar(sv("mxc", f), sv("mxy", f),
                                                    1e-5, None, ALU.max))
                vguard(vector, vector.reciprocal(sv("sinv", f), sv("mxc", f)))
                vector.tensor_scalar(sv("dqx", f), sv("mxc", f), 1.0 / 127, None,
                                     ALU.mult)
                vguard(vector, vector.tensor_scalar(sv("s", f), sv("sinv", f),
                                                    127.0, None, ALU.mult))
                vguard(vector, vector.tensor_scalar(sv("scomb", f), sv("r", f),
                                                    sv("s", f), None, ALU.mult))
                vguard(vector, vector.tensor_scalar(sv("t3", f), bmean(f),
                                                    sv("scomb", f), -1.0,
                                                    ALU.mult, ALU.mult))
                vguard(vector, vector.tensor_tensor(sv("bcomb", f), sv("t3", f),
                                                    kap[:], ALU.add))
                vector.tensor_scalar(sv("dqg", f), sv("dqx", f), thmw[0][:, 1:2],
                                     None, ALU.mult)
                vector.tensor_scalar(sv("dqu", f), sv("dqx", f), thmw[1][:, 1:2],
                                     None, ALU.mult)
                v.inc(vector.tensor_scalar(xq32[:], xb[f % 3][:], sv("scomb", f),
                                           sv("bcomb", f), ALU.mult, ALU.add))
                v_xq[f] = v.n

            @blk.scalar
            def _(scalar, f=f):
                scalar.wait_ge(v.h, v_xq[f])
                if f >= 2:
                    scalar.wait_ge(trp.h, trp_xq[f - 2])
                a.inc(scalar.activation(xqbf[f % 2][:], xq32[:], ACTF.Copy,
                                        bias=-128.0, scale=1.0))
                a_xqbf[f] = a.n

            @blk.sync
            def _(sync, f=f):
                sync.wait_ge(a.h, a_xqbf[f])
                if f >= 2 and t_gu[f - 2][NI - 1]:
                    sync.wait_ge(t.h, t_gu[f - 2][NI - 1])
                sync.wait_ge(trp.h, trp.n)
                trp.inc(sync.dma_start_transpose(
                    xqT[f % 2][:].rearrange("p (k f) -> p k f", k=8),
                    xqbf[f % 2][:].rearrange("p (k f) -> p k f", k=8)), 16)
                trp_xq[f] = trp.n

        def emit_hqbf(i):
            @blk.scalar
            def _(scalar, i=i):
                scalar.wait_ge(v.h, v_hq[i])
                if i >= 1:
                    scalar.wait_ge(trp.h, trp_hq[i - 1])
                a.inc(scalar.activation(hqbf[:, 0:2048], hq32[:, 0:2048],
                                        ACTF.Copy, bias=-128.0, scale=1.0))
                a_hqbf[i] = a.n

            @blk.vector
            def _(vector, i=i):
                if i >= 1:
                    vector.wait_ge(trp.h, trp_hq[i - 1])
                v.inc(vector.tensor_scalar(hqbf[:, 2048:INNER],
                                           hq32[:, 2048:INNER], -128.0, None,
                                           ALU.add))
                v_hqbf2[i] = v.n

        @blk.sync
        def _(sync):
            for j in (0, 1):
                if j < nt:
                    sync.wait_ge(sst.h, st_xp[j])
                    sync.wait_ge(v.h, v_mst[nt - 1])
                    ld.inc(sync.dma_start(out=xb[j % 3][:],
                                          in_=xp_h[TT * j:TT * (j + 1), :]), 16)
                    ld2[j] = ld.n

        front(0)
        if nt > 1:
            front(1)

        for i in range(nt):

            for j in range(NI):
                @blk.tensor
                def _(tensor, i=i, j=j):
                    tensor.wait_ge(trp.h, trp_xq[i])
                    if i == 0 and j == 0:
                        tensor.wait_ge(v.h, g_stt[1][WMAT[1][2] - 1])
                    pj, qi = (j - 2, i) if j >= 2 else (NI - 2 + j, i - 1)
                    if qi >= 0 and a_silu[qi][pj]:
                        tensor.wait_ge(a.h, a_silu[qi][pj])
                        tensor.wait_ge(v.h, v_hstt[qi][pj])
                    for d in range(8):
                        base = INNER * d + 512 * j
                        pewait(tensor)
                        t.inc(tensor.matmul(P512[j % 2][:],
                                            xqT[i % 2][:, 128 * d:128 * (d + 1)],
                                            wq_g[:, base:base + 512],
                                            start=(d == 0), stop=(d == 7)))
                        pewait(tensor)
                        t.inc(tensor.matmul(P512[2 + j % 2][:],
                                            xqT[i % 2][:, 128 * d:128 * (d + 1)],
                                            wq_u[:, base:base + 512],
                                            start=(d == 0), stop=(d == 7)))
                    t_gu[i][j] = t.n

                @blk.scalar
                def _(scalar, i=i, j=j):
                    scalar.wait_ge(t.h, t_gu[i][j])
                    if j >= 2:
                        scalar.wait_ge(v.h, v_hstt[i][j - 2])
                    elif i >= 1:
                        scalar.wait_ge(v.h, v_hstt[i - 1][NI - 2 + j])
                    a.inc(scalar.activation(gtmp[j % 2][:], P512[j % 2][:],
                                            ACTF.Silu, bias=0.0, scale=sv("dqg", i)))
                    a_silu[i][j] = a.n

                @blk.vector
                def _(vector, i=i, j=j):
                    vector.wait_ge(a.h, a_silu[i][j])
                    if i == 0 and j == 0:
                        vector.wait_ge(a.h, a_abs_all)
                    v.inc(vector.scalar_tensor_tensor(
                        hbuf[:, 512 * j:512 * (j + 1)], P512[2 + j % 2][:],
                        sv("dqu", i), gtmp[j % 2][:], ALU.mult, ALU.mult))
                    v_hstt[i][j] = v.n

            @blk.vector
            def _(vector, i=i):
                vector.wait_ge(v.h, v_hstt[i][NI - 1])
                for j in range(NI):
                    ins = vector.tensor_reduce(mxh8[i % 2][:, j:j + 1],
                                               hbuf[:, 512 * j:512 * (j + 1)], AX,
                                               ALU.max, apply_absolute_value=True)
                    if j == NI - 1:
                        vguard(vector, ins)
                vguard(vector, vector.tensor_reduce(sv("mxh", i), mxh8[i % 2][:], AX,
                                                    ALU.max))
                vguard(vector, vector.tensor_scalar(sv("shy", i), sv("mxh", i),
                                                    1e-5, None, ALU.max))
                vguard(vector, vector.reciprocal(sv("shinv", i), sv("shy", i)))
                vguard(vector, vector.tensor_scalar(sv("sh", i), sv("shinv", i),
                                                    127.0, None, ALU.mult))
                vguard(vector, vector.tensor_scalar(sv("dqh", i), sv("shy", i),
                                                    1.0 / 127, None, ALU.mult))
                vector.tensor_scalar(sv("dqd", i), sv("dqh", i), thmw[2][:, 1:2],
                                     None, ALU.mult)
                if i >= 1:
                    vector.wait_ge(a.h, a_hqbf[i - 1])
                v.inc(vector.tensor_scalar(hq32[:], hbuf[:], sv("sh", i), kap[:],
                                           ALU.mult, ALU.add))
                v_hq[i] = v.n

            if i >= 1:
                emit_hq_down(i - 1)
            emit_hqbf(i)

            j = i + 2
            if 2 <= j < nt:
                @blk.sync
                def _(sync, i=i, j=j):
                    sync.wait_ge(sst.h, st_xp[j])
                    if i >= 1:
                        sync.wait_ge(v.h, v_dst[i - 1])
                    ld.inc(sync.dma_start(out=xb[j % 3][:],
                                          in_=xp_h[TT * j:TT * (j + 1), :]), 16)
                    ld2[j] = ld.n
            if i + 1 < nt:
                front(i + 1)

        emit_hq_down(nt - 1)

        if dbg == 3:
            @blk.sync
            def _(sync):
                sync.wait_ge(sst.h, sst.n)
                nst = 0
                for nm, tt, dt in dbgs:
                    sync.dma_start(out=dbg_h[nm], in_=tt[:]).then_inc(skw.h, 16)
                    nst += 16
                sync.wait_ge(skw.h, skw.n + nst)
                skw.n += nst

        @blk.scalar
        def _(scalar):
            scalar.wait_ge(sst.h, sst.n)

    return nc


def kernel(L0_attn_x, L0_attn_W, L0_attn_b, L0_ffn_w_gate, L0_ffn_w_up,
           L0_ffn_w_down):
    x = np.ascontiguousarray(np.asarray(L0_attn_x, dtype=np.float32))
    Wt = np.ascontiguousarray(np.asarray(L0_attn_W, dtype=np.float32).T)
    b_bc = np.ascontiguousarray(
        np.tile(np.asarray(L0_attn_b, dtype=np.float32)[None, :], (128, 1)))
    wgT = np.ascontiguousarray(np.asarray(L0_ffn_w_gate, dtype=np.float32).T)
    wuT = np.ascontiguousarray(np.asarray(L0_ffn_w_up, dtype=np.float32).T)
    wdT = np.ascontiguousarray(np.asarray(L0_ffn_w_down, dtype=np.float32).T)
    ident = np.eye(128, dtype=ml_dtypes.bfloat16)
    nc = build_nc()
    in_maps = [{"x": x[c], "Wt": Wt, "b_bc": b_bc, "wgT": wgT, "wuT": wuT,
                "wdT": wdT, "ident": ident} for c in range(8)]
    res = run_bass_kernel_spmd(nc, in_maps, core_ids=list(range(8)))
    return np.stack([res.results[c]["out"] for c in range(8)], axis=0)


# version 06
# speedup vs baseline: 1.6112x; 1.0548x over previous
import numpy as np
import ml_dtypes
from contextlib import ExitStack

import concourse.bass as bass
import concourse.mybir as mybir
from concourse.bass_types import AP
from concourse.bass_utils import run_bass_kernel_spmd

F32 = mybir.dt.float32
BF16 = mybir.dt.bfloat16
FP8 = mybir.dt.float8e4
I32 = mybir.dt.int32
I16 = mybir.dt.int16
ALU = mybir.AluOpType
ACTF = mybir.ActivationFunctionType
AX = mybir.AxisListType.X

L, D, KD, INNER, TT = 4096, 1024, 32, 4096, 128
ND = D // 128
NI = INNER // 512
NK = INNER // 128
EPS = 1e-5
STP, SKB = 288, 127


class Sem:
    def __init__(self, ctx, nc, name):
        self.h = ctx.enter_context(nc.semaphore(name))
        self.n = 0

    def inc(self, ins, k=1):
        ins.then_inc(self.h, k)
        self.n += k
        return self.n


def build_nc(nt=L // TT, dbg=False, phase_b=True, prep=True, astage=99, serial=False):
    nc = bass.Bass("TRN2", target_bir_lowering=False, detect_race_conditions=False)
    Ltot = nt * TT

    x_h = nc.dram_tensor("x", [Ltot, D], F32, kind="ExternalInput").ap()
    wt_h = nc.dram_tensor("Wt", [D, KD], F32, kind="ExternalInput").ap()
    bb_h = nc.dram_tensor("b_bc", [128, KD], F32, kind="ExternalInput").ap()
    wg_h = nc.dram_tensor("wgT", [4 * D, INNER // 4], F32, kind="ExternalInput").ap()
    wu_h = nc.dram_tensor("wuT", [4 * D, INNER // 4], F32, kind="ExternalInput").ap()
    wd_h = nc.dram_tensor("wdT", [INNER, D], F32, kind="ExternalInput").ap()
    id_h = nc.dram_tensor("ident", [128, 128], BF16, kind="ExternalInput").ap()
    out_h = nc.dram_tensor("out", [Ltot, D], F32, kind="ExternalOutput").ap()
    xp_h = nc.dram_tensor("xprime", [Ltot, D], F32,
                          kind="ExternalOutput" if dbg else "Internal").ap()
    stage_h = nc.dram_tensor("stage", [2, 128, STP], F32, kind="Internal").ap()

    ctx = ExitStack()
    with ctx:
        sbuf = lambda name, shape, dt: ctx.enter_context(nc.sbuf_tensor(name, shape, dt))
        wq_g = sbuf("wq_g", [128, ND * INNER], FP8)
        wq_u = sbuf("wq_u", [128, ND * INNER], FP8)
        wq_d = sbuf("wq_d", [128, NK * D], FP8)
        wbuf = [sbuf(f"wbuf{k}", [128, 1024], F32) for k in range(2)]
        tlt = [sbuf(f"tlt{k}", [128, 1024], BF16) for k in range(2)]
        rs_abs = sbuf("rs_abs", [128, 96], F32)
        thmw = [sbuf(f"thmw{m}", [128, 2], F32) for m in range(3)]
        negth = [sbuf(f"negth{m}", [128, 1], F32) for m in range(3)]
        vec2 = sbuf("vec2", [1, 2], F32)
        tot1 = sbuf("tot1", [1, 1], F32)
        ones_c = sbuf("ones_c", [128, 1], F32)
        ones_r = sbuf("ones_r", [1, 128], F32)
        identb = sbuf("identb", [128, 128], BF16)
        wtb = sbuf("wtb", [128, ND * KD], BF16)
        bbc = sbuf("bbc", [128, KD], F32)
        kap = sbuf("kap", [128, 1], F32)
        ceps = sbuf("ceps", [128, 1], F32)
        c15 = sbuf("c15", [128, 1], F32)
        ci32 = sbuf("ci32", [128, 1], I32)
        bn6 = sbuf("bn6", [128, 12], F32)
        bnmv = sbuf("bnmv", [128, 8], F32)
        bmv = lambda i: bnmv[:, 2 * (i % 2):2 * (i % 2) + 2]
        bmean = lambda i: bnmv[:, 2 * (i % 2):2 * (i % 2) + 1]
        bvar = lambda i: bnmv[:, 2 * (i % 2) + 1:2 * (i % 2) + 2]
        bmvB = lambda i: bnmv[:, 4 + 2 * (i % 2):4 + 2 * (i % 2) + 2]
        bmeanB = lambda i: bnmv[:, 4 + 2 * (i % 2):4 + 2 * (i % 2) + 1]
        bvarB = lambda i: bnmv[:, 4 + 2 * (i % 2) + 1:4 + 2 * (i % 2) + 2]
        xb = [sbuf(f"xb{k}", [128, D], F32) for k in range(2)]
        xbB = [sbuf(f"xbB{k}", [128, D], F32) for k in range(3)]
        xpr = [sbuf(f"xpr{k}", [128, D], F32) for k in range(2)]
        ybf = [sbuf(f"ybf{k}", [128, D], BF16) for k in range(2)]
        yT = [sbuf(f"yT{k}", [128, D], BF16) for k in range(2)]
        lbs = [sbuf(f"lb{k}", [128, KD], F32) for k in range(2)]
        ews = [sbuf(f"ew{k}", [128, KD], F32) for k in range(2)]
        wns = [sbuf(f"wn{k}", [128, KD], F32) for k in range(2)]
        skbb = [sbuf(f"skb{k}", [128, 256], BF16) for k in range(2)]
        ATs = [sbuf(f"AT{k}", [128, 256], BF16) for k in range(2)]
        A1 = lambda q: ATs[q % 2][:, 0:128]
        A2 = lambda q: ATs[q % 2][0:32, 128:256]
        xq32 = sbuf("xq32", [128, D], I16)
        xqbf = [sbuf(f"xqbf{k}", [128, D], BF16) for k in range(2)]
        xqT = [sbuf(f"xqT{k}", [128, D], BF16) for k in range(2)]
        hbuf = sbuf("hbuf", [128, INNER], F32)
        hq32 = sbuf("hq32", [128, INNER], I16)
        hqbf = sbuf("hqbf", [128, INNER], BF16)
        hqT = sbuf("hqT", [128, INNER], BF16)
        gtmp = [sbuf(f"gtmp{k}", [128, 512], F32) for k in range(2)]
        mxh8 = [sbuf(f"mxh8{k}", [128, 8], F32) for k in range(2)]
        names = ["rs", "m", "var", "sd", "r", "nmr", "mx", "nmx", "ssq", "esum",
                 "rsum", "rmax", "rmin", "t1", "t2", "mxd", "mxy", "mxc", "sinv",
                 "s", "dqx", "scomb", "t3", "bcomb", "dqg", "dqu", "mxh", "shy",
                 "shinv", "sh", "dqh", "dqd", "ss2", "tred", "sdB", "rB"]
        STbig = sbuf("STbig", [128, 2 * len(names)], F32)
        ST = {n: STbig[:, 2 * k:2 * k + 2] for k, n in enumerate(names)}

        def sv(n, i):
            k = names.index(n)
            return STbig[:, 2 * k + (i % 2):2 * k + (i % 2) + 1]

        P512 = [ctx.enter_context(nc.psum_tensor(f"p512_{k}", [128, 512], F32))
                for k in range(4)]
        P1024 = ctx.enter_context(nc.psum_tensor("p1024", [128, 1024], F32))
        PMIXH = ctx.enter_context(nc.psum_tensor("pmixh", [128, 512], F32))
        P32 = ctx.enter_context(nc.psum_tensor("p32", [128, KD], F32))
        pl = P32[:]
        pd = P1024
        pprep1 = P1024[0:1, 992:993]
        pprep2 = P1024[:, 992:994]

        if dbg == 4:
            dbg4_A1 = nc.dram_tensor("dbg4_A1", [nt, 128, 128], BF16, kind="ExternalOutput").ap()
            dbg4_skb = nc.dram_tensor("dbg4_skb", [nt, 128, 160], BF16, kind="ExternalOutput").ap()
            dbg4_wn = nc.dram_tensor("dbg4_wn", [nt, 128, KD], F32, kind="ExternalOutput").ap()

        ld = Sem(ctx, nc, "ld")
        wl = [Sem(ctx, nc, "wl0"), Sem(ctx, nc, "wl1")]
        xl = [Sem(ctx, nc, "xl0"), Sem(ctx, nc, "xl1")]
        bldm = [Sem(ctx, nc, f"bl{k}") for k in range(3)]
        sw = [Sem(ctx, nc, "sw0"), Sem(ctx, nc, "sw1")]
        sr = [Sem(ctx, nc, "sr0"), Sem(ctx, nc, "sr1")]
        sstB = Sem(ctx, nc, "sstB")
        skw = Sem(ctx, nc, "skw")
        skr = Sem(ctx, nc, "skr")
        sst = Sem(ctx, nc, "sst")
        v = Sem(ctx, nc, "v")
        a = Sem(ctx, nc, "a")
        t = Sem(ctx, nc, "t")
        g = Sem(ctx, nc, "g")
        gld = Sem(ctx, nc, "gld")
        trp = Sem(ctx, nc, "trp")
        blk = ctx.enter_context(nc.Block())

        tile_done = [None] * (nt + 1)
        PEW = 4

        def pemm(tensor, ins):
            t.inc(ins)

        def vguard(vector, ins):
            v.inc(ins)
            vector.wait_ge(v.h, v.n)

        def pewait(tensor):
            if t.n >= PEW and t.n % 2 == 0:
                tensor.wait_ge(t.h, t.n - 2)

        @blk.sync
        def _(sync):
            sync.dma_start(out=identb[:], in_=id_h[:]).then_inc(ld.h, 16)
            sync.dma_start(out=bbc[:], in_=bb_h[:]).then_inc(ld.h, 16)
            ld.n += 32

        @blk.gpsimd
        def _(gpsimd):
            for d in range(ND):
                gpsimd.dma_start(out=wtb[:, KD * d:KD * (d + 1)],
                                 in_=wt_h[128 * d:128 * (d + 1), :]).then_inc(gld.h, 16)
            gld.n += 16 * ND
        ld_const = ld.n
        gld_const = gld.n

        @blk.vector
        def _(vector):
            vector.memset(c15[:], 1.5)
            vector.memset(ceps[:], EPS)
            vector.memset(ones_c[:], 1.0)
            vector.memset(ones_r[:], 1.0)
            vguard(vector, vector.memset(hbuf[:, 0:STP], 0.0))
            vguard(vector, vector.tensor_copy(ci32[:], c15[:]))
            vguard(vector, vector.tensor_scalar(kap[:], ci32[:], -0.5, 129.0,
                                                ALU.mult, ALU.add))
            vector.wait_ge(ld.h, ld_const)
            vector.wait_ge(gld.h, gld_const)
            v.inc(vector.engine_nop())
        v_const = v.n

        @blk.sync
        def _(sync):
            sync.wait_ge(v.h, v_const)
            sync.dma_start(out=stage_h[0], in_=hbuf[:, 0:STP]).then_inc(skw.h, 16)
            sync.dma_start(out=stage_h[1], in_=hbuf[:, 0:STP]).then_inc(skw.h, 16)
            skw.n += 32
        skw_zero = skw.n

        def _chunks_gu():
            return [(1024 * (c % 4) + 128 * (c // 4), 0, 1024,
                     INNER * (c // 4) + 1024 * (c % 4)) for c in range(32)]

        def _chunks_d():
            return [(128 * c, 0, D, D * c) for c in range(NK)]

        WMAT = [(wg_h, wq_g, 32, _chunks_gu(), 0), (wu_h, wq_u, 32, _chunks_gu(), 32),
                (wd_h, wq_d, NK, _chunks_d(), 64)]
        sched = [[] for _ in range(nt)]
        if nt >= 18:
            plan = ([("p1", 0, c) for c in range(32)] + [("th", 0, 0)]
                    + [("p1", 1, c) for c in range(32)] + [("th", 1, 0)]
                    + [("p2", 0, c) for c in range(32)]
                    + [("p1", 2, c) for c in range(32)] + [("th", 2, 0)]
                    + [("p2", 1, c) for c in range(32)]
                    + [("p2", 2, c) for c in range(32)])
            budget = [14] * nt
            ti = 0
            for e in plan:
                while budget[ti] <= 0:
                    ti += 1
                sched[ti].append(e)
                if e[0] != "th":
                    budget[ti] -= 1
        else:
            for m in range(3):
                nch = WMAT[m][2]
                sched[0] += [("p1", m, c) for c in range(nch)]
                sched[0].append(("th", m, 0))
                sched[0] += [("p2", m, c) for c in range(nch)]

        a_abs = [[0] * 32 for _ in range(3)]
        v_abs = [[0] * 32 for _ in range(3)]
        a_abs_last = [0]
        ld_w = [[0] * 32 for _ in range(3)], [[0] * 32 for _ in range(3)]
        v_lt = [[0] * 32 for _ in range(3)]
        g_stt = [[0] * 32 for _ in range(3)]
        wslot_i = [0]
        wslot_free = [[], []]
        tlt_user = [0, 0]

        def emit_prep(entry):
            kind, m, c = entry
            wh, wq, nch, chunks, rsoff = WMAT[m]
            if kind in ("p1", "p2"):
                row0, col0, cols, qbase = chunks[c]
                sl = wslot_i[0] % 2
                wslot_i[0] += 1
                p2 = kind == "p2"

                @blk.sync
                def _(sync, m=m, c=c, sl=sl, p2=p2, marks=list(wslot_free[sl]),
                      row0=row0, col0=col0, cols=cols):
                    for s_, n_ in marks:
                        if n_ > 0:
                            sync.wait_ge(s_.h, n_)
                    wl[sl].inc(sync.dma_start(
                        out=wbuf[sl][:, 0:cols],
                        in_=wh[row0:row0 + 128, col0:col0 + cols]), 16)
                    ld_w[1 if p2 else 0][m][c] = wl[sl].n
                if not p2:
                    if c % 2 == 0:
                        @blk.scalar
                        def _(scalar, m=m, c=c, sl=sl, cols=cols):
                            scalar.wait_ge(wl[sl].h, ld_w[0][m][c])
                            if m == 0 and c == 0:
                                scalar.wait_ge(skw.h, skw_zero)
                            a.inc(scalar.activation(
                                hbuf[:, 0:cols], wbuf[sl][:, 0:cols], ACTF.Abs,
                                bias=0.0, scale=1.0,
                                accum_out=rs_abs[:, rsoff + c:rsoff + c + 1]))
                            a_abs[m][c] = a.n
                            a_abs_last[0] = a.n
                        wslot_free[sl] = [(a, a_abs[m][c])]
                    else:
                        @blk.vector
                        def _(vector, m=m, c=c, sl=sl, cols=cols):
                            vector.wait_ge(wl[sl].h, ld_w[0][m][c])
                            v.inc(vector.tensor_reduce(
                                rs_abs[:, rsoff + c:rsoff + c + 1],
                                wbuf[sl][:, 0:cols], AX, ALU.add,
                                apply_absolute_value=True))
                            v_abs[m][c] = v.n
                        wslot_free[sl] = [(v, v_abs[m][c])]
                else:
                    sl2 = sl
                    @blk.vector
                    def _(vector, m=m, c=c, sl=sl, sl2=sl2, cols=cols,
                          tfree=tlt_user[sl]):
                        vector.wait_ge(wl[sl].h, ld_w[1][m][c])
                        if tfree:
                            vector.wait_ge(v.h, tfree)
                        v.inc(vector.tensor_scalar(tlt[sl2][:, 0:cols],
                                                   wbuf[sl][:, 0:cols], negth[m][:],
                                                   None, ALU.is_lt))
                        v_lt[m][c] = v.n

                    @blk.vector
                    def _(vector, m=m, c=c, sl=sl, sl2=sl2, cols=cols, qbase=qbase):
                        vector.wait_ge(v.h, v_lt[m][c])
                        v.inc(vector.scalar_tensor_tensor(
                            wq[:, qbase:qbase + cols], wbuf[sl][:, 0:cols],
                            thmw[m][:, 0:1], tlt[sl2][:, 0:cols],
                            ALU.is_gt, ALU.subtract))
                        g_stt[m][c] = v.n
                    tlt_user[sl2] = g_stt[m][c]
                    wslot_free[sl] = [(v, g_stt[m][c])]
            else:
                @blk.vector
                def _(vector, m=m):
                    vector.wait_ge(a.h, max(a_abs[m][:nch]))
                    vector.wait_ge(v.h, max(v_abs[m][:nch]))
                    v.inc(vector.tensor_reduce(ST["tred"][:, 0:1],
                                               rs_abs[:, rsoff:rsoff + nch], AX,
                                               ALU.add))

                @blk.tensor
                def _(tensor, m=m, vn=v.n):
                    tensor.wait_ge(v.h, vn)
                    pewait(tensor)
                    t.inc(tensor.matmul(pprep1, ones_c[:],
                                        ST["tred"][:, 0:1], start=True, stop=True))

                @blk.vector
                def _(vector, m=m, tn=t.n):
                    vector.wait_ge(t.h, tn)
                    vguard(vector, vector.tensor_copy(tot1[:], pprep1))
                    vguard(vector, vector.tensor_scalar(tot1[:], tot1[:],
                                                        1.0 / (D * INNER), 1e-5,
                                                        ALU.mult, ALU.max))
                    vector.tensor_copy(vec2[:, 1:2], tot1[:])
                    v.inc(vector.tensor_scalar(vec2[:, 0:1], tot1[:], 0.5, None,
                                               ALU.mult))

                @blk.tensor
                def _(tensor, m=m, vn=v.n):
                    tensor.wait_ge(v.h, vn)
                    pewait(tensor)
                    t.inc(tensor.matmul(pprep2, ones_r[:], vec2[:],
                                        start=True, stop=True))

                @blk.vector
                def _(vector, m=m, tn=t.n):
                    vector.wait_ge(t.h, tn)
                    vguard(vector, vector.tensor_copy(thmw[m][:], pprep2))
                    v.inc(vector.tensor_scalar(negth[m][:], thmw[m][:, 0:1], -1.0,
                                               None, ALU.mult))

        Z = lambda: [0] * (nt + 1)
        ld_x, a_sq, v_var, a_sd, v_ln, a_ybf = Z(), Z(), Z(), Z(), Z(), Z()
        t_ytr, v_yev, a_yev, t_log, v_lb = Z(), Z(), Z(), Z(), Z()
        v_nmx, a_exp, v_rsum, a_wn, skw_n, skr_n = Z(), Z(), Z(), Z(), Z(), Z()
        trp_at, t_mix, v_mst, st_xp = Z(), Z(), Z(), Z()
        mix_pe = [0]
        ld2, a_sq2, v_var2, a_sd2, g_rmin, v_xq = Z(), Z(), Z(), Z(), Z(), Z()
        a_xqbf, t_xtr, v_xev, a_xev, v_hq, a_hqbf = Z(), Z(), Z(), Z(), Z(), Z()
        v_hqbf2 = Z()
        t_down, v_dst, st_out, g_amax = Z(), Z(), Z(), Z()
        trp_y, trp_xq, trp_hq = Z(), Z(), Z()
        t_gu = [[0] * NI for _ in range(nt)]
        a_silu = [[0] * NI for _ in range(nt)]
        v_hstt = [[0] * NI for _ in range(nt)]
        t_hqtr = [[0] * 4 for _ in range(nt)]
        v_hqev = [[0] * 4 for _ in range(nt)]
        a_hqev = [[0] * 4 for _ in range(nt)]

        OFF = 14
        OFF_eff = OFF if (phase_b and prep and astage >= 99 and nt >= 18) else nt

        def emit_hq_down(iq):
            @blk.sync
            def _(sync, iq=iq):
                sync.wait_ge(a.h, a_hqbf[iq])
                sync.wait_ge(v.h, v_hqbf2[iq])
                if iq >= 1:
                    sync.wait_ge(t.h, t_down[iq - 1])
                sync.wait_ge(trp.h, trp.n)
                trp.inc(sync.dma_start_transpose(
                    hqT[:].rearrange("p (k f) -> p k f", k=NK),
                    hqbf[:].rearrange("p (k f) -> p k f", k=NK)), 16)
                trp_hq[iq] = trp.n

            for grp in range(4):
                @blk.tensor
                def _(tensor, iq=iq, grp=grp):
                    tensor.wait_ge(trp.h, trp_hq[iq])
                    if grp == 0:
                        if iq == 0:
                            tensor.wait_ge(v.h, g_stt[2][NK - 1])
                        elif v_dst[iq - 1]:
                            tensor.wait_ge(v.h, v_dst[iq - 1])
                    for kk in range(8):
                        k = 8 * grp + kk
                        pewait(tensor)
                        t.inc(tensor.matmul(pd[:, 0:512],
                                            hqT[:, 128 * k:128 * (k + 1)],
                                            wq_d[:, D * k:D * k + 512],
                                            start=(k == 0), stop=(k == NK - 1)))
                        pewait(tensor)
                        t.inc(tensor.matmul(pd[:, 512:1024],
                                            hqT[:, 128 * k:128 * (k + 1)],
                                            wq_d[:, D * k + 512:D * (k + 1)],
                                            start=(k == 0), stop=(k == NK - 1)))
                    if grp == 3:
                        t_down[iq] = t.n

            @blk.vector
            def _(vector, iq=iq):
                vector.wait_ge(t.h, t_down[iq])
                v.inc(vector.scalar_tensor_tensor(xbB[iq % 3][:], pd[:],
                                                  sv("dqd", iq), xbB[iq % 3][:],
                                                  ALU.mult, ALU.add))
                v_dst[iq] = v.n

            @blk.scalar
            def _(scalar, iq=iq):
                scalar.wait_ge(v.h, v_dst[iq])
                sstB.inc(scalar.dma_start(out=out_h[TT * iq:TT * (iq + 1), :],
                                          in_=xbB[iq % 3][:]), 16)
                st_out[iq] = sstB.n

        def front(f):
            @blk.vector
            def _(vector, f=f):
                vector.wait_ge(bldm[f % 3].h, ld2[f])
                vector.bn_stats(bn6[:, 0:6], xbB[f % 3][:, 0:512])
                vector.bn_stats(bn6[:, 6:12], xbB[f % 3][:, 512:D])
                vector.tensor_reduce(sv("rmax", f), xbB[f % 3][:], AX, ALU.max)
                vguard(vector, vector.tensor_reduce(sv("rmin", f), xbB[f % 3][:],
                                                    AX, ALU.min))
                v.inc(vector.bn_aggr(bmvB(f), bn6[:]))
                v_var2[f] = v.n

            @blk.scalar
            def _(scalar, f=f):
                scalar.wait_ge(v.h, v_var2[f])
                a.inc(scalar.activation(sv("sdB", f), bvarB(f), ACTF.Sqrt,
                                        bias=ceps[:], scale=1.0))
                a_sd2[f] = a.n

            @blk.vector
            def _(vector, f=f):
                vector.wait_ge(a.h, a_sd2[f])
                if f >= 1:
                    vector.wait_ge(a.h, a_xqbf[f - 1])
                vguard(vector, vector.reciprocal(sv("rB", f), sv("sdB", f)))
                vector.tensor_tensor(sv("t1", f), sv("rmax", f), bmeanB(f),
                                     ALU.subtract)
                vguard(vector, vector.tensor_tensor(sv("t2", f), bmeanB(f),
                                                    sv("rmin", f), ALU.subtract))
                vguard(vector, vector.tensor_tensor(sv("mxd", f), sv("t1", f),
                                                    sv("t2", f), ALU.max))
                vguard(vector, vector.tensor_scalar(sv("mxy", f), sv("mxd", f),
                                                    sv("rB", f), None, ALU.mult))
                vguard(vector, vector.tensor_scalar(sv("mxc", f), sv("mxy", f),
                                                    1e-5, None, ALU.max))
                vguard(vector, vector.reciprocal(sv("sinv", f), sv("mxc", f)))
                vector.tensor_scalar(sv("dqx", f), sv("mxc", f), 1.0 / 127, None,
                                     ALU.mult)
                vguard(vector, vector.tensor_scalar(sv("s", f), sv("sinv", f),
                                                    127.0, None, ALU.mult))
                vguard(vector, vector.tensor_scalar(sv("scomb", f), sv("rB", f),
                                                    sv("s", f), None, ALU.mult))
                vguard(vector, vector.tensor_scalar(sv("t3", f), bmeanB(f),
                                                    sv("scomb", f), -1.0,
                                                    ALU.mult, ALU.mult))
                vguard(vector, vector.tensor_tensor(sv("bcomb", f), sv("t3", f),
                                                    kap[:], ALU.add))
                vector.tensor_scalar(sv("dqg", f), sv("dqx", f), thmw[0][:, 1:2],
                                     None, ALU.mult)
                vector.tensor_scalar(sv("dqu", f), sv("dqx", f), thmw[1][:, 1:2],
                                     None, ALU.mult)
                v.inc(vector.tensor_scalar(xq32[:], xbB[f % 3][:], sv("scomb", f),
                                           sv("bcomb", f), ALU.mult, ALU.add))
                v_xq[f] = v.n

            @blk.scalar
            def _(scalar, f=f):
                scalar.wait_ge(v.h, v_xq[f])
                if f >= 2:
                    scalar.wait_ge(trp.h, trp_xq[f - 2])
                a.inc(scalar.activation(xqbf[f % 2][:], xq32[:], ACTF.Copy,
                                        bias=-128.0, scale=1.0))
                a_xqbf[f] = a.n

            @blk.sync
            def _(sync, f=f):
                sync.wait_ge(a.h, a_xqbf[f])
                if f >= 2 and t_gu[f - 2][NI - 1]:
                    sync.wait_ge(t.h, t_gu[f - 2][NI - 1])
                sync.wait_ge(trp.h, trp.n)
                trp.inc(sync.dma_start_transpose(
                    xqT[f % 2][:].rearrange("p (k f) -> p k f", k=8),
                    xqbf[f % 2][:].rearrange("p (k f) -> p k f", k=8)), 16)
                trp_xq[f] = trp.n

        def emit_hqbf(k):
            @blk.scalar
            def _(scalar, k=k):
                scalar.wait_ge(v.h, v_hq[k])
                if k >= 1:
                    scalar.wait_ge(trp.h, trp_hq[k - 1])
                a.inc(scalar.activation(hqbf[:, 0:2048], hq32[:, 0:2048],
                                        ACTF.Copy, bias=-128.0, scale=1.0))
                a_hqbf[k] = a.n

            @blk.vector
            def _(vector, k=k):
                if k >= 1:
                    vector.wait_ge(trp.h, trp_hq[k - 1])
                v.inc(vector.tensor_scalar(hqbf[:, 2048:INNER],
                                           hq32[:, 2048:INNER], -128.0, None,
                                           ALU.add))
                v_hqbf2[k] = v.n

        def emit_b_prol():
            @blk.sync
            def _(sync):
                for j in (0, 1):
                    if j < nt:
                        sync.wait_ge(sst.h, st_xp[j])
                        bldm[j % 3].inc(sync.dma_start(
                            out=xbB[j % 3][:],
                            in_=xp_h[TT * j:TT * (j + 1), :]), 16)
                        ld2[j] = bldm[j % 3].n
            front(0)

        def emit_b_iter(k):
            for j in range(NI):
                @blk.tensor
                def _(tensor, k=k, j=j):
                    tensor.wait_ge(trp.h, trp_xq[k])
                    if k == 0 and j == 0:
                        tensor.wait_ge(v.h, g_stt[1][WMAT[1][2] - 1])
                    pj, qi = (j - 2, k) if j >= 2 else (NI - 2 + j, k - 1)
                    if qi >= 0 and a_silu[qi][pj]:
                        tensor.wait_ge(a.h, a_silu[qi][pj])
                        tensor.wait_ge(v.h, v_hstt[qi][pj])
                    for d in range(8):
                        base = INNER * d + 512 * j
                        pewait(tensor)
                        t.inc(tensor.matmul(P512[j % 2][:],
                                            xqT[k % 2][:, 128 * d:128 * (d + 1)],
                                            wq_g[:, base:base + 512],
                                            start=(d == 0), stop=(d == 7)))
                        pewait(tensor)
                        t.inc(tensor.matmul(P512[2 + j % 2][:],
                                            xqT[k % 2][:, 128 * d:128 * (d + 1)],
                                            wq_u[:, base:base + 512],
                                            start=(d == 0), stop=(d == 7)))
                    t_gu[k][j] = t.n

                @blk.scalar
                def _(scalar, k=k, j=j):
                    scalar.wait_ge(t.h, t_gu[k][j])
                    if j >= 2:
                        scalar.wait_ge(v.h, v_hstt[k][j - 2])
                    elif k >= 1:
                        scalar.wait_ge(v.h, v_hstt[k - 1][NI - 2 + j])
                    a.inc(scalar.activation(gtmp[j % 2][:], P512[j % 2][:],
                                            ACTF.Silu, bias=0.0,
                                            scale=sv("dqg", k)))
                    a_silu[k][j] = a.n

                @blk.vector
                def _(vector, k=k, j=j):
                    vector.wait_ge(a.h, a_silu[k][j])
                    if k == 0 and j == 0:
                        vector.wait_ge(a.h, a_abs_last[0])
                    v.inc(vector.scalar_tensor_tensor(
                        hbuf[:, 512 * j:512 * (j + 1)], P512[2 + j % 2][:],
                        sv("dqu", k), gtmp[j % 2][:], ALU.mult, ALU.mult))
                    v_hstt[k][j] = v.n

            @blk.vector
            def _(vector, k=k):
                vector.wait_ge(v.h, v_hstt[k][NI - 1])
                for j in range(NI):
                    ins = vector.tensor_reduce(mxh8[k % 2][:, j:j + 1],
                                               hbuf[:, 512 * j:512 * (j + 1)], AX,
                                               ALU.max, apply_absolute_value=True)
                    if j == NI - 1:
                        vguard(vector, ins)
                vguard(vector, vector.tensor_reduce(sv("mxh", k), mxh8[k % 2][:],
                                                    AX, ALU.max))
                vguard(vector, vector.tensor_scalar(sv("shy", k), sv("mxh", k),
                                                    1e-5, None, ALU.max))
                vguard(vector, vector.reciprocal(sv("shinv", k), sv("shy", k)))
                vguard(vector, vector.tensor_scalar(sv("sh", k), sv("shinv", k),
                                                    127.0, None, ALU.mult))
                vguard(vector, vector.tensor_scalar(sv("dqh", k), sv("shy", k),
                                                    1.0 / 127, None, ALU.mult))
                vector.tensor_scalar(sv("dqd", k), sv("dqh", k), thmw[2][:, 1:2],
                                     None, ALU.mult)
                if k >= 1:
                    vector.wait_ge(a.h, a_hqbf[k - 1])
                v.inc(vector.tensor_scalar(hq32[:], hbuf[:], sv("sh", k), kap[:],
                                           ALU.mult, ALU.add))
                v_hq[k] = v.n

            if k >= 1:
                emit_hq_down(k - 1)
            emit_hqbf(k)

            j = k + 2
            if 2 <= j < nt:
                @blk.sync
                def _(sync, k=k, j=j):
                    sync.wait_ge(sst.h, st_xp[j])
                    if k >= 1:
                        sync.wait_ge(sstB.h, st_out[k - 1])
                    bldm[j % 3].inc(sync.dma_start(
                        out=xbB[j % 3][:],
                        in_=xp_h[TT * j:TT * (j + 1), :]), 16)
                    ld2[j] = bldm[j % 3].n
            if k + 1 < nt:
                front(k + 1)

        @blk.sync
        def _(sync):
            for j in (0, 1):
                if j < nt:
                    sync.wait_ge(v.h, v_const)
                    xl[j % 2].inc(sync.dma_start(
                        out=xb[j][:], in_=x_h[TT * j:TT * (j + 1), :]), 16)
                    ld_x[j] = xl[j % 2].n

        for i in range(nt):
            if phase_b and OFF_eff < nt:
                if i == OFF_eff - 2:
                    emit_b_prol()
                if i - OFF_eff >= 0:
                    emit_b_iter(i - OFF_eff)

            @blk.vector
            def _(vector, i=i):
                vector.wait_ge(xl[i % 2].h, ld_x[i])
                vector.bn_stats(bn6[:, 0:6], xb[i % 2][:, 0:512])
                vguard(vector, vector.bn_stats(bn6[:, 6:12], xb[i % 2][:, 512:D]))
                v.inc(vector.bn_aggr(bmv(i), bn6[:]))
                v_var[i] = v.n

            @blk.scalar
            def _(scalar, i=i):
                scalar.wait_ge(v.h, v_var[i])
                a.inc(scalar.activation(sv("sd", i), bvar(i), ACTF.Sqrt,
                                        bias=ceps[:], scale=1.0))
                a_sd[i] = a.n

            @blk.vector
            def _(vector, i=i):
                vector.wait_ge(a.h, a_sd[i])
                vguard(vector, vector.reciprocal(sv("r", i), sv("sd", i)))
                v.inc(vector.tensor_scalar(sv("nmr", i), bmean(i), sv("r", i),
                                           -1.0, ALU.mult, ALU.mult))
                v_ln[i] = v.n

            @blk.scalar
            def _(scalar, i=i):
                scalar.wait_ge(v.h, v_ln[i])
                if i >= 2:
                    scalar.wait_ge(t.h, t_mix[i - 2])
                a.inc(scalar.activation(ybf[i % 2][:], xb[i % 2][:], ACTF.Identity,
                                        bias=sv("nmr", i), scale=sv("r", i)))
                a_ybf[i] = a.n

            tile_done[i] = (a, a_ybf[i])
            if astage < 2:
                continue

            @blk.sync
            def _(sync, i=i):
                sync.wait_ge(a.h, a_ybf[i])
                if i >= 2:
                    sync.wait_ge(t.h, t_log[i - 2])
                sync.wait_ge(trp.h, trp.n)
                trp.inc(sync.dma_start_transpose(
                    yT[i % 2][:].rearrange("p (k f) -> p k f", k=8),
                    ybf[i % 2][:].rearrange("p (k f) -> p k f", k=8)), 16)
                trp_y[i] = trp.n
            tile_done[i] = (a, a_yev[i])
            if astage < 3:
                continue

            @blk.tensor
            def _(tensor, i=i):
                tensor.wait_ge(trp.h, trp_y[i])
                if i >= 1:
                    tensor.wait_ge(v.h, v_lb[i - 1])
                for d in range(ND):
                    pewait(tensor)
                    t.inc(tensor.matmul(pl, yT[i % 2][:, 128 * d:128 * (d + 1)],
                                        wtb[:, KD * d:KD * (d + 1)],
                                        start=(d == 0), stop=(d == ND - 1)))
                t_log[i] = t.n

            @blk.vector
            def _(vector, i=i):
                vector.wait_ge(t.h, t_log[i])
                if i >= 2:
                    vector.wait_ge(a.h, a_exp[i - 2])
                vguard(vector, vector.tensor_tensor(lbs[i % 2][:], pl, bbc[:],
                                                    ALU.add))
                v_lb[i] = v.n
                vguard(vector, vector.tensor_reduce(sv("mx", i), lbs[i % 2][:], AX,
                                                    ALU.max))
                v.inc(vector.tensor_scalar(sv("nmx", i), sv("mx", i), -1.0, None,
                                           ALU.mult))
                v_nmx[i] = v.n

            tile_done[i] = (v, v_nmx[i])
            if astage < 4:
                continue

            @blk.scalar
            def _(scalar, i=i):
                scalar.wait_ge(v.h, v_nmx[i])
                a.inc(scalar.activation(ews[i % 2][:], lbs[i % 2][:], ACTF.Exp,
                                        bias=sv("nmx", i), scale=1.0,
                                        accum_out=sv("esum", i)))
                a_exp[i] = a.n

            @blk.vector
            def _(vector, i=i):
                vector.wait_ge(a.h, a_exp[i])
                v.inc(vector.reciprocal(sv("rsum", i), sv("esum", i)))
                v_rsum[i] = v.n

            @blk.scalar
            def _(scalar, i=i):
                scalar.wait_ge(v.h, v_rsum[i])
                if i >= 2:
                    scalar.wait_ge(sw[i % 2].h, skw_n[i - 2])
                a.inc(scalar.activation(wns[i % 2][:], ews[i % 2][:], ACTF.Copy,
                                        bias=0.0, scale=sv("rsum", i)))
                a_wn[i] = a.n

            tile_done[i] = (a, a_wn[i])
            if astage < 5:
                continue

            @blk.sync
            def _(sync, i=i):
                sync.wait_ge(a.h, a_wn[i])
                if i >= 2:
                    sync.wait_ge(sr[i % 2].h, skr_n[i - 2])
                sw[i % 2].inc(sync.dma_start(
                    out=stage_h[i % 2, :, SKB:SKB + KD], in_=wns[i % 2][:]), 16)
                skw_n[i] = sw[i % 2].n

            @blk.gpsimd
            def _(gpsimd, i=i):
                gpsimd.wait_ge(sw[i % 2].h, skw_n[i])
                if i >= 2:
                    gpsimd.wait_ge(trp.h, trp_at[i - 2])
                src = AP(stage_h.tensor, (i % 2) * 128 * STP + SKB,
                         [[STP - 1, 128], [1, 160]])
                sr[i % 2].inc(gpsimd.dma_start(out=skbb[i % 2][:, 0:160],
                                               in_=src), 16)
                skr_n[i] = sr[i % 2].n

            tile_done[i] = (sr[i % 2], skr_n[i])
            if astage < 6:
                continue

            @blk.sync
            def _(sync, i=i):
                sync.wait_ge(sr[i % 2].h, skr_n[i])
                if i >= 2:
                    sync.wait_ge(t.h, t_mix[i - 2])
                sync.wait_ge(trp.h, trp.n)
                trp.inc(sync.dma_start_transpose(
                    ATs[i % 2][:].rearrange("p (k f) -> p k f", k=2),
                    skbb[i % 2][:].rearrange("p (k f) -> p k f", k=2)), 16)
                trp_at[i] = trp.n
            tile_done[i] = (trp, trp_at[i])
            if astage < 7:
                continue

            if i >= 1:
                for half in range(2):
                    @blk.tensor
                    def _(tensor, i=i, half=half, pv=mix_pe[0]):
                        q = i - 1
                        cs = slice(512 * half, 512 * (half + 1))
                        if half == 0:
                            tensor.wait_ge(trp.h, trp_at[q])
                            tensor.wait_ge(a.h, a_ybf[i])
                        if pv:
                            tensor.wait_ge(v.h, pv)
                        pewait(tensor)
                        t.inc(tensor.matmul(PMIXH[:], A1(q), ybf[q % 2][:, cs],
                                            start=True, stop=False))
                        pewait(tensor)
                        t.inc(tensor.matmul(PMIXH[:], A2(q),
                                            ybf[i % 2][0:32, cs], start=False,
                                            stop=True))
                    if half == 1:
                        t_mix[i - 1] = t.n
                    tmark = t.n

                    @blk.vector
                    def _(vector, i=i, half=half, tmark=tmark):
                        q = i - 1
                        cs = slice(512 * half, 512 * (half + 1))
                        vector.wait_ge(t.h, tmark)
                        if half == 0 and q >= 2 and st_xp[q - 2]:
                            vector.wait_ge(sst.h, st_xp[q - 2])
                        v.inc(vector.scalar_tensor_tensor(
                            xpr[q % 2][:, cs], PMIXH[:], 1.0, xb[q % 2][:, cs],
                            ALU.mult, ALU.add))
                        if half == 1:
                            v_mst[q] = v.n
                    mix_pe[0] = v.n

                @blk.scalar
                def _(scalar, i=i):
                    scalar.wait_ge(v.h, v_mst[i - 1])
                    sst.inc(scalar.dma_start(out=xp_h[TT * (i - 1):TT * i, :],
                                             in_=xpr[(i - 1) % 2][:]), 16)
                    st_xp[i - 1] = sst.n

            j = i + 1
            if 2 <= j < nt:
                @blk.sync
                def _(sync, i=i, j=j):
                    if astage >= 7:
                        sync.wait_ge(v.h, v_mst[j - 2])
                    elif tile_done[j - 2] is not None:
                        s_, n_ = tile_done[j - 2]
                        sync.wait_ge(s_.h, n_)
                    xl[j % 2].inc(sync.dma_start(
                        out=xb[j % 2][:], in_=x_h[TT * j:TT * (j + 1), :]), 16)
                    ld_x[j] = xl[j % 2].n

            if prep:
                for entry in sched[i]:
                    emit_prep(entry)

        _ = None
        if astage < 7:
            @blk.vector
            def _(vector):
                s_, n_ = tile_done[nt - 1]
                vector.wait_ge(s_.h, n_)
            return nc

        for half in range(2):
            @blk.tensor
            def _(tensor, half=half, pv=mix_pe[0]):
                q = nt - 1
                cs = slice(512 * half, 512 * (half + 1))
                if half == 0:
                    tensor.wait_ge(trp.h, trp_at[q])
                if pv:
                    tensor.wait_ge(v.h, pv)
                pewait(tensor)
                t.inc(tensor.matmul(PMIXH[:], A1(q), ybf[q % 2][:, cs],
                                    start=True, stop=True))
            if half == 1:
                t_mix[nt - 1] = t.n
            tmark = t.n

            @blk.vector
            def _(vector, half=half, tmark=tmark):
                q = nt - 1
                cs = slice(512 * half, 512 * (half + 1))
                vector.wait_ge(t.h, tmark)
                if half == 0 and q >= 2 and st_xp[q - 2]:
                    vector.wait_ge(sst.h, st_xp[q - 2])
                v.inc(vector.scalar_tensor_tensor(
                    xpr[q % 2][:, cs], PMIXH[:], 1.0, xb[q % 2][:, cs],
                    ALU.mult, ALU.add))
                if half == 1:
                    v_mst[q] = v.n
            mix_pe[0] = v.n

        @blk.scalar
        def _(scalar):
            i = nt - 1
            scalar.wait_ge(v.h, v_mst[i])
            sst.inc(scalar.dma_start(out=xp_h[TT * i:TT * (i + 1), :],
                                     in_=xpr[i % 2][:]), 16)
            st_xp[i] = sst.n


        if dbg == 2:
            dbgs = [("ybf", ybf[0], BF16), ("yT", yT[0], BF16),
                    ("lb", lbs[0], F32), ("ew", ews[0], F32), ("wn", wns[0], F32),
                    ("skb", skbb[0], BF16), ("AT", ATs[0], BF16),
                    ("stats", STbig, F32)]
        elif dbg == 3:
            dbgs = [("thmw0", thmw[0], F32), ("thmw1", thmw[1], F32),
                    ("thmw2", thmw[2], F32), ("kap", kap, F32),
                    ("wqg", wq_g[:, 0:INNER], FP8), ("wqd", wq_d[:, 0:D], FP8),
                    ("xqbf", xqbf[0], BF16), ("xqT", xqT[0], BF16),
                    ("hqbf", hqbf, BF16), ("stats", STbig, F32)]
            dbg_h = {nm: nc.dram_tensor("dbg_" + nm, list(tt.shape), dt,
                                        kind="ExternalOutput").ap()
                     for nm, tt, dt in dbgs}

            if dbg == 2:
                @blk.sync
                def _(sync):
                    sync.wait_ge(sst.h, sst.n)
                    nst = 0
                    for nm, tt, dt in dbgs:
                        sync.dma_start(out=dbg_h[nm], in_=tt[:]).then_inc(skw.h, 16)
                        nst += 16
                    sync.wait_ge(skw.h, skw.n + nst)
                    skw.n += nst

        if not phase_b:
            @blk.scalar
            def _(scalar):
                scalar.wait_ge(sst.h, sst.n)
            return nc
        if OFF_eff >= nt:
            emit_b_prol()
        for k in range(max(0, nt - OFF_eff), nt):
            emit_b_iter(k)
        emit_hq_down(nt - 1)

        if dbg == 3:
            @blk.sync
            def _(sync):
                sync.wait_ge(sst.h, sst.n)
                nst = 0
                for nm, tt, dt in dbgs:
                    sync.dma_start(out=dbg_h[nm], in_=tt[:]).then_inc(skw.h, 16)
                    nst += 16
                sync.wait_ge(skw.h, skw.n + nst)
                skw.n += nst

        @blk.scalar
        def _(scalar):
            scalar.wait_ge(sst.h, sst.n)
            scalar.wait_ge(sstB.h, sstB.n)

    return nc


def kernel(L0_attn_x, L0_attn_W, L0_attn_b, L0_ffn_w_gate, L0_ffn_w_up,
           L0_ffn_w_down):
    x = np.ascontiguousarray(np.asarray(L0_attn_x, dtype=np.float32))
    Wt = np.ascontiguousarray(np.asarray(L0_attn_W, dtype=np.float32).T)
    b_bc = np.ascontiguousarray(
        np.tile(np.asarray(L0_attn_b, dtype=np.float32)[None, :], (128, 1)))
    def retile(wT):
        return np.ascontiguousarray(
            wT.reshape(1024, 4, 1024).swapaxes(0, 1).reshape(4096, 1024))
    wgT = retile(np.asarray(L0_ffn_w_gate, dtype=np.float32).T)
    wuT = retile(np.asarray(L0_ffn_w_up, dtype=np.float32).T)
    wdT = np.ascontiguousarray(np.asarray(L0_ffn_w_down, dtype=np.float32).T)
    ident = np.eye(128, dtype=ml_dtypes.bfloat16)
    nc = build_nc()
    in_maps = [{"x": x[c], "Wt": Wt, "b_bc": b_bc, "wgT": wgT, "wuT": wuT,
                "wdT": wdT, "ident": ident} for c in range(8)]
    res = run_bass_kernel_spmd(nc, in_maps, core_ids=list(range(8)))
    return np.stack([res.results[c]["out"] for c in range(8)], axis=0)


# version 08
# speedup vs baseline: 1.6360x; 1.0154x over previous
import numpy as np
import ml_dtypes
from contextlib import ExitStack

import concourse.bass as bass
import concourse.mybir as mybir
from concourse.bass_types import AP
from concourse.bass_utils import run_bass_kernel_spmd

F32 = mybir.dt.float32
BF16 = mybir.dt.bfloat16
FP8 = mybir.dt.float8e4
I32 = mybir.dt.int32
I16 = mybir.dt.int16
ALU = mybir.AluOpType
ACTF = mybir.ActivationFunctionType
AX = mybir.AxisListType.X

L, D, KD, INNER, TT = 4096, 1024, 32, 4096, 128
ND = D // 128
NI = INNER // 512
NK = INNER // 128
EPS = 1e-5
STP, SKB = 288, 127


class Sem:
    def __init__(self, ctx, nc, name):
        self.h = ctx.enter_context(nc.semaphore(name))
        self.n = 0

    def inc(self, ins, k=1):
        ins.then_inc(self.h, k)
        self.n += k
        return self.n


def build_nc(nt=L // TT, dbg=False, phase_b=True, prep=True, astage=99, serial=False):
    nc = bass.Bass("TRN2", target_bir_lowering=False, detect_race_conditions=False)
    Ltot = nt * TT

    x_h = nc.dram_tensor("x", [Ltot, D], F32, kind="ExternalInput").ap()
    wt_h = nc.dram_tensor("Wt", [D, KD], F32, kind="ExternalInput").ap()
    bb_h = nc.dram_tensor("b_bc", [128, KD], F32, kind="ExternalInput").ap()
    wg_h = nc.dram_tensor("wgT", [4 * D, INNER // 4], F32, kind="ExternalInput").ap()
    wu_h = nc.dram_tensor("wuT", [4 * D, INNER // 4], F32, kind="ExternalInput").ap()
    wd_h = nc.dram_tensor("wdT", [INNER, D], F32, kind="ExternalInput").ap()
    id_h = nc.dram_tensor("ident", [128, 128], BF16, kind="ExternalInput").ap()
    out_h = nc.dram_tensor("out", [Ltot, D], F32, kind="ExternalOutput").ap()
    xp_h = nc.dram_tensor("xprime", [Ltot, D], F32,
                          kind="ExternalOutput" if dbg else "Internal").ap()
    stage_h = nc.dram_tensor("stage", [2, 128, STP], F32, kind="Internal").ap()

    ctx = ExitStack()
    with ctx:
        sbuf = lambda name, shape, dt: ctx.enter_context(nc.sbuf_tensor(name, shape, dt))
        wq_g = sbuf("wq_g", [128, ND * INNER], FP8)
        wq_u = sbuf("wq_u", [128, ND * INNER], FP8)
        wq_d = sbuf("wq_d", [128, NK * D], FP8)
        wbuf = [sbuf(f"wbuf{k}", [128, 1024], F32) for k in range(2)]
        tlt = [sbuf(f"tlt{k}", [128, 1024], BF16) for k in range(2)]
        rs_abs = sbuf("rs_abs", [128, 96], F32)
        thmw = [sbuf(f"thmw{m}", [128, 2], F32) for m in range(3)]
        negth = [sbuf(f"negth{m}", [128, 1], F32) for m in range(3)]
        vec2 = sbuf("vec2", [1, 2], F32)
        tot1 = sbuf("tot1", [1, 1], F32)
        ones_c = sbuf("ones_c", [128, 1], F32)
        ones_r = sbuf("ones_r", [1, 128], F32)
        identb = sbuf("identb", [128, 128], BF16)
        wtb = sbuf("wtb", [128, ND * KD], BF16)
        bbc = sbuf("bbc", [128, KD], F32)
        kap = sbuf("kap", [128, 1], F32)
        ceps = sbuf("ceps", [128, 1], F32)
        c15 = sbuf("c15", [128, 1], F32)
        ci32 = sbuf("ci32", [128, 1], I32)
        bn6 = sbuf("bn6", [128, 12], F32)
        bnmv = sbuf("bnmv", [128, 8], F32)
        bmv = lambda i: bnmv[:, 2 * (i % 2):2 * (i % 2) + 2]
        bmean = lambda i: bnmv[:, 2 * (i % 2):2 * (i % 2) + 1]
        bvar = lambda i: bnmv[:, 2 * (i % 2) + 1:2 * (i % 2) + 2]
        bmvB = lambda i: bnmv[:, 4 + 2 * (i % 2):4 + 2 * (i % 2) + 2]
        bmeanB = lambda i: bnmv[:, 4 + 2 * (i % 2):4 + 2 * (i % 2) + 1]
        bvarB = lambda i: bnmv[:, 4 + 2 * (i % 2) + 1:4 + 2 * (i % 2) + 2]
        xb = [sbuf(f"xb{k}", [128, D], F32) for k in range(2)]
        xbB = [sbuf(f"xbB{k}", [128, D], F32) for k in range(3)]
        xpr = [sbuf(f"xpr{k}", [128, D], F32) for k in range(2)]
        ybf = [sbuf(f"ybf{k}", [128, D], BF16) for k in range(2)]
        yT = [sbuf(f"yT{k}", [128, D], BF16) for k in range(2)]
        lbs = [sbuf(f"lb{k}", [128, KD], F32) for k in range(2)]
        ews = [sbuf(f"ew{k}", [128, KD], F32) for k in range(2)]
        wns = [sbuf(f"wn{k}", [128, KD], F32) for k in range(2)]
        skbb = [sbuf(f"skb{k}", [128, 256], BF16) for k in range(2)]
        ATs = [sbuf(f"AT{k}", [128, 256], BF16) for k in range(2)]
        A1 = lambda q: ATs[q % 2][:, 0:128]
        A2 = lambda q: ATs[q % 2][0:32, 128:256]
        xq32 = sbuf("xq32", [128, D], I16)
        xqbf = [sbuf(f"xqbf{k}", [128, D], BF16) for k in range(2)]
        xqT = [sbuf(f"xqT{k}", [128, D], BF16) for k in range(2)]
        hbuf = sbuf("hbuf", [128, INNER], F32)
        hq32 = sbuf("hq32", [128, INNER], I16)
        hqbf = sbuf("hqbf", [128, INNER], BF16)
        hqT = sbuf("hqT", [128, INNER], BF16)
        gtmp = [sbuf(f"gtmp{k}", [128, 512], F32) for k in range(2)]
        mxh8 = [sbuf(f"mxh8{k}", [128, 8], F32) for k in range(2)]
        names = ["rs", "m", "var", "sd", "r", "nmr", "mx", "nmx", "ssq", "esum",
                 "rsum", "rmax", "rmin", "t1", "t2", "mxd", "mxy", "mxc", "sinv",
                 "s", "dqx", "scomb", "t3", "bcomb", "dqg", "dqu", "mxh", "shy",
                 "shinv", "sh", "dqh", "dqd", "ss2", "tred", "sdB", "rB"]
        STbig = sbuf("STbig", [128, 2 * len(names)], F32)
        ST = {n: STbig[:, 2 * k:2 * k + 2] for k, n in enumerate(names)}

        def sv(n, i):
            k = names.index(n)
            return STbig[:, 2 * k + (i % 2):2 * k + (i % 2) + 1]

        P512 = [ctx.enter_context(nc.psum_tensor(f"p512_{k}", [128, 512], F32))
                for k in range(4)]
        P1024 = ctx.enter_context(nc.psum_tensor("p1024", [128, 1024], F32))
        PMIXH = ctx.enter_context(nc.psum_tensor("pmixh", [128, 512], F32))
        P32 = ctx.enter_context(nc.psum_tensor("p32", [128, KD], F32))
        pl = P32[:]
        pd = P1024
        pprep1 = P1024[0:1, 992:993]
        pprep2 = P1024[:, 992:994]

        if dbg == 4:
            dbg4_A1 = nc.dram_tensor("dbg4_A1", [nt, 128, 128], BF16, kind="ExternalOutput").ap()
            dbg4_skb = nc.dram_tensor("dbg4_skb", [nt, 128, 160], BF16, kind="ExternalOutput").ap()
            dbg4_wn = nc.dram_tensor("dbg4_wn", [nt, 128, KD], F32, kind="ExternalOutput").ap()

        ld = Sem(ctx, nc, "ld")
        wl = [Sem(ctx, nc, "wl0"), Sem(ctx, nc, "wl1")]
        xl = [Sem(ctx, nc, "xl0"), Sem(ctx, nc, "xl1")]
        bldm = [Sem(ctx, nc, f"bl{k}") for k in range(3)]
        sw = [Sem(ctx, nc, "sw0"), Sem(ctx, nc, "sw1")]
        sr = [Sem(ctx, nc, "sr0"), Sem(ctx, nc, "sr1")]
        sstB = Sem(ctx, nc, "sstB")
        skw = Sem(ctx, nc, "skw")
        skr = Sem(ctx, nc, "skr")
        sst = Sem(ctx, nc, "sst")
        v = Sem(ctx, nc, "v")
        a = Sem(ctx, nc, "a")
        t = Sem(ctx, nc, "t")
        g = Sem(ctx, nc, "g")
        gld = Sem(ctx, nc, "gld")
        trp = Sem(ctx, nc, "trp")
        blk = ctx.enter_context(nc.Block())

        tile_done = [None] * (nt + 1)
        PEW = 4

        def pemm(tensor, ins):
            t.inc(ins)

        def vguard(vector, ins):
            v.inc(ins)
            vector.wait_ge(v.h, v.n)

        def pewait(tensor):
            if t.n >= PEW and t.n % 2 == 0:
                tensor.wait_ge(t.h, t.n - 2)

        @blk.sync
        def _(sync):
            sync.dma_start(out=identb[:], in_=id_h[:]).then_inc(ld.h, 16)
            sync.dma_start(out=bbc[:], in_=bb_h[:]).then_inc(ld.h, 16)
            ld.n += 32

        @blk.gpsimd
        def _(gpsimd):
            for d in range(ND):
                gpsimd.dma_start(out=wtb[:, KD * d:KD * (d + 1)],
                                 in_=wt_h[128 * d:128 * (d + 1), :]).then_inc(gld.h, 16)
            gld.n += 16 * ND
        ld_const = ld.n
        gld_const = gld.n

        @blk.vector
        def _(vector):
            vector.memset(c15[:], 1.5)
            vector.memset(ceps[:], EPS)
            vector.memset(ones_c[:], 1.0)
            vector.memset(ones_r[:], 1.0)
            vguard(vector, vector.memset(hbuf[:, 0:STP], 0.0))
            vguard(vector, vector.tensor_copy(ci32[:], c15[:]))
            vguard(vector, vector.tensor_scalar(kap[:], ci32[:], -0.5, 129.0,
                                                ALU.mult, ALU.add))
            vector.wait_ge(ld.h, ld_const)
            vector.wait_ge(gld.h, gld_const)
            v.inc(vector.engine_nop())
        v_const = v.n

        @blk.sync
        def _(sync):
            sync.wait_ge(v.h, v_const)
            sync.dma_start(out=stage_h[0], in_=hbuf[:, 0:STP]).then_inc(skw.h, 16)
            sync.dma_start(out=stage_h[1], in_=hbuf[:, 0:STP]).then_inc(skw.h, 16)
            skw.n += 32
        skw_zero = skw.n

        def _chunks_gu():
            return [(1024 * (c % 4) + 128 * (c // 4), 0, 1024,
                     INNER * (c // 4) + 1024 * (c % 4)) for c in range(32)]

        def _chunks_d():
            return [(128 * c, 0, D, D * c) for c in range(NK)]

        WMAT = [(wg_h, wq_g, 32, _chunks_gu(), 0), (wu_h, wq_u, 32, _chunks_gu(), 32),
                (wd_h, wq_d, NK, _chunks_d(), 64)]
        sched = [[] for _ in range(nt)]
        if nt >= 18:
            plan = ([("p1", 0, c) for c in range(32)] + [("th", 0, 0)]
                    + [("p1", 1, c) for c in range(32)] + [("th", 1, 0)]
                    + [("p2", 0, c) for c in range(32)]
                    + [("p1", 2, c) for c in range(32)] + [("th", 2, 0)]
                    + [("p2", 1, c) for c in range(32)]
                    + [("p2", 2, c) for c in range(32)])
            budget = [14] * nt
            ti = 0
            for e in plan:
                while budget[ti] <= 0:
                    ti += 1
                sched[ti].append(e)
                if e[0] != "th":
                    budget[ti] -= 1
        else:
            for m in range(3):
                nch = WMAT[m][2]
                sched[0] += [("p1", m, c) for c in range(nch)]
                sched[0].append(("th", m, 0))
                sched[0] += [("p2", m, c) for c in range(nch)]

        a_abs = [[0] * 32 for _ in range(3)]
        v_abs = [[0] * 32 for _ in range(3)]
        a_abs_last = [0]
        ld_w = [[0] * 32 for _ in range(3)], [[0] * 32 for _ in range(3)]
        v_lt = [[0] * 32 for _ in range(3)]
        g_stt = [[0] * 32 for _ in range(3)]
        wslot_i = [0]
        wslot_free = [[], []]
        tlt_user = [0, 0]

        def emit_prep(entry):
            kind, m, c = entry
            wh, wq, nch, chunks, rsoff = WMAT[m]
            if kind in ("p1", "p2"):
                row0, col0, cols, qbase = chunks[c]
                sl = wslot_i[0] % 2
                wslot_i[0] += 1
                p2 = kind == "p2"

                @blk.scalar
                def _(scalar, m=m, c=c, sl=sl, p2=p2, marks=list(wslot_free[sl]),
                      row0=row0, col0=col0, cols=cols):
                    for s_, n_ in marks:
                        if n_ > 0:
                            scalar.wait_ge(s_.h, n_)
                    wl[sl].inc(scalar.dma_start(
                        out=wbuf[sl][:, 0:cols],
                        in_=wh[row0:row0 + 128, col0:col0 + cols]), 16)
                    ld_w[1 if p2 else 0][m][c] = wl[sl].n
                if not p2:
                    if c % 2 == 0:
                        @blk.scalar
                        def _(scalar, m=m, c=c, sl=sl, cols=cols):
                            scalar.wait_ge(wl[sl].h, ld_w[0][m][c])
                            if m == 0 and c == 0:
                                scalar.wait_ge(skw.h, skw_zero)
                            a.inc(scalar.activation(
                                hbuf[:, 0:cols], wbuf[sl][:, 0:cols], ACTF.Abs,
                                bias=0.0, scale=1.0,
                                accum_out=rs_abs[:, rsoff + c:rsoff + c + 1]))
                            a_abs[m][c] = a.n
                            a_abs_last[0] = a.n
                        wslot_free[sl] = [(a, a_abs[m][c])]
                    else:
                        @blk.vector
                        def _(vector, m=m, c=c, sl=sl, cols=cols):
                            vector.wait_ge(wl[sl].h, ld_w[0][m][c])
                            v.inc(vector.tensor_reduce(
                                rs_abs[:, rsoff + c:rsoff + c + 1],
                                wbuf[sl][:, 0:cols], AX, ALU.add,
                                apply_absolute_value=True))
                            v_abs[m][c] = v.n
                        wslot_free[sl] = [(v, v_abs[m][c])]
                else:
                    sl2 = sl
                    @blk.vector
                    def _(vector, m=m, c=c, sl=sl, sl2=sl2, cols=cols,
                          tfree=tlt_user[sl]):
                        vector.wait_ge(wl[sl].h, ld_w[1][m][c])
                        if tfree:
                            vector.wait_ge(v.h, tfree)
                        v.inc(vector.tensor_scalar(tlt[sl2][:, 0:cols],
                                                   wbuf[sl][:, 0:cols], negth[m][:],
                                                   None, ALU.is_lt))
                        v_lt[m][c] = v.n

                    @blk.vector
                    def _(vector, m=m, c=c, sl=sl, sl2=sl2, cols=cols, qbase=qbase):
                        vector.wait_ge(v.h, v_lt[m][c])
                        v.inc(vector.scalar_tensor_tensor(
                            wq[:, qbase:qbase + cols], wbuf[sl][:, 0:cols],
                            thmw[m][:, 0:1], tlt[sl2][:, 0:cols],
                            ALU.is_gt, ALU.subtract))
                        g_stt[m][c] = v.n
                    tlt_user[sl2] = g_stt[m][c]
                    wslot_free[sl] = [(v, g_stt[m][c])]
            else:
                @blk.vector
                def _(vector, m=m):
                    vector.wait_ge(a.h, max(a_abs[m][:nch]))
                    vector.wait_ge(v.h, max(v_abs[m][:nch]))
                    v.inc(vector.tensor_reduce(ST["tred"][:, 0:1],
                                               rs_abs[:, rsoff:rsoff + nch], AX,
                                               ALU.add))

                @blk.tensor
                def _(tensor, m=m, vn=v.n):
                    tensor.wait_ge(v.h, vn)
                    pewait(tensor)
                    t.inc(tensor.matmul(pprep1, ones_c[:],
                                        ST["tred"][:, 0:1], start=True, stop=True))

                @blk.vector
                def _(vector, m=m, tn=t.n):
                    vector.wait_ge(t.h, tn)
                    vguard(vector, vector.tensor_copy(tot1[:], pprep1))
                    vguard(vector, vector.tensor_scalar(tot1[:], tot1[:],
                                                        1.0 / (D * INNER), 1e-5,
                                                        ALU.mult, ALU.max))
                    vector.tensor_copy(vec2[:, 1:2], tot1[:])
                    v.inc(vector.tensor_scalar(vec2[:, 0:1], tot1[:], 0.5, None,
                                               ALU.mult))

                @blk.tensor
                def _(tensor, m=m, vn=v.n):
                    tensor.wait_ge(v.h, vn)
                    pewait(tensor)
                    t.inc(tensor.matmul(pprep2, ones_r[:], vec2[:],
                                        start=True, stop=True))

                @blk.vector
                def _(vector, m=m, tn=t.n):
                    vector.wait_ge(t.h, tn)
                    vguard(vector, vector.tensor_copy(thmw[m][:], pprep2))
                    v.inc(vector.tensor_scalar(negth[m][:], thmw[m][:, 0:1], -1.0,
                                               None, ALU.mult))

        Z = lambda: [0] * (nt + 1)
        ld_x, a_sq, v_var, a_sd, v_ln, a_ybf = Z(), Z(), Z(), Z(), Z(), Z()
        t_ytr, v_yev, a_yev, t_log, v_lb = Z(), Z(), Z(), Z(), Z()
        v_nmx, a_exp, v_rsum, a_wn, skw_n, skr_n = Z(), Z(), Z(), Z(), Z(), Z()
        trp_at, t_mix, v_mst, st_xp = Z(), Z(), Z(), Z()
        mix_pe = [0]
        ld2, a_sq2, v_var2, a_sd2, g_rmin, v_xq = Z(), Z(), Z(), Z(), Z(), Z()
        a_xqbf, t_xtr, v_xev, a_xev, v_hq, a_hqbf = Z(), Z(), Z(), Z(), Z(), Z()
        v_hqbf2 = Z()
        t_down, v_dst, st_out, g_amax = Z(), Z(), Z(), Z()
        trp_y, trp_xq, trp_hq = Z(), Z(), Z()
        t_gu = [[0] * NI for _ in range(nt)]
        a_silu = [[0] * NI for _ in range(nt)]
        v_hstt = [[0] * NI for _ in range(nt)]
        t_hqtr = [[0] * 4 for _ in range(nt)]
        v_hqev = [[0] * 4 for _ in range(nt)]
        a_hqev = [[0] * 4 for _ in range(nt)]

        OFF = 14
        OFF_eff = OFF if (phase_b and prep and astage >= 99 and nt >= 18) else nt

        def emit_hq_down(iq):
            @blk.sync
            def _(sync, iq=iq):
                sync.wait_ge(a.h, a_hqbf[iq])
                sync.wait_ge(v.h, v_hqbf2[iq])
                if iq >= 1:
                    sync.wait_ge(t.h, t_down[iq - 1])
                sync.wait_ge(trp.h, trp.n)
                trp.inc(sync.dma_start_transpose(
                    hqT[:].rearrange("p (k f) -> p k f", k=NK),
                    hqbf[:].rearrange("p (k f) -> p k f", k=NK)), 16)
                trp_hq[iq] = trp.n

            for grp in range(4):
                @blk.tensor
                def _(tensor, iq=iq, grp=grp):
                    tensor.wait_ge(trp.h, trp_hq[iq])
                    if grp == 0:
                        if iq == 0:
                            tensor.wait_ge(v.h, g_stt[2][NK - 1])
                        elif v_dst[iq - 1]:
                            tensor.wait_ge(v.h, v_dst[iq - 1])
                    for kk in range(8):
                        k = 8 * grp + kk
                        pewait(tensor)
                        t.inc(tensor.matmul(pd[:, 0:512],
                                            hqT[:, 128 * k:128 * (k + 1)],
                                            wq_d[:, D * k:D * k + 512],
                                            start=(k == 0), stop=(k == NK - 1)))
                        pewait(tensor)
                        t.inc(tensor.matmul(pd[:, 512:1024],
                                            hqT[:, 128 * k:128 * (k + 1)],
                                            wq_d[:, D * k + 512:D * (k + 1)],
                                            start=(k == 0), stop=(k == NK - 1)))
                    if grp == 3:
                        t_down[iq] = t.n

            @blk.vector
            def _(vector, iq=iq):
                vector.wait_ge(t.h, t_down[iq])
                v.inc(vector.scalar_tensor_tensor(xbB[iq % 3][:], pd[:],
                                                  sv("dqd", iq), xbB[iq % 3][:],
                                                  ALU.mult, ALU.add))
                v_dst[iq] = v.n

            @blk.scalar
            def _(scalar, iq=iq):
                scalar.wait_ge(v.h, v_dst[iq])
                sstB.inc(scalar.dma_start(out=out_h[TT * iq:TT * (iq + 1), :],
                                          in_=xbB[iq % 3][:]), 16)
                st_out[iq] = sstB.n

        def front(f):
            @blk.vector
            def _(vector, f=f):
                vector.wait_ge(bldm[f % 3].h, ld2[f])
                vector.bn_stats(bn6[:, 0:6], xbB[f % 3][:, 0:512])
                vector.bn_stats(bn6[:, 6:12], xbB[f % 3][:, 512:D])
                vector.tensor_reduce(sv("rmax", f), xbB[f % 3][:], AX, ALU.max)
                vguard(vector, vector.tensor_reduce(sv("rmin", f), xbB[f % 3][:],
                                                    AX, ALU.min))
                v.inc(vector.bn_aggr(bmvB(f), bn6[:]))
                v_var2[f] = v.n

            @blk.scalar
            def _(scalar, f=f):
                scalar.wait_ge(v.h, v_var2[f])
                a.inc(scalar.activation(sv("sdB", f), bvarB(f), ACTF.Sqrt,
                                        bias=ceps[:], scale=1.0))
                a_sd2[f] = a.n

            @blk.vector
            def _(vector, f=f):
                vector.wait_ge(a.h, a_sd2[f])
                if f >= 1:
                    vector.wait_ge(a.h, a_xqbf[f - 1])
                vguard(vector, vector.reciprocal(sv("rB", f), sv("sdB", f)))
                vector.tensor_tensor(sv("t1", f), sv("rmax", f), bmeanB(f),
                                     ALU.subtract)
                vguard(vector, vector.tensor_tensor(sv("t2", f), bmeanB(f),
                                                    sv("rmin", f), ALU.subtract))
                vguard(vector, vector.tensor_tensor(sv("mxd", f), sv("t1", f),
                                                    sv("t2", f), ALU.max))
                vguard(vector, vector.tensor_scalar(sv("mxy", f), sv("mxd", f),
                                                    sv("rB", f), None, ALU.mult))
                vguard(vector, vector.tensor_scalar(sv("mxc", f), sv("mxy", f),
                                                    1e-5, None, ALU.max))
                vguard(vector, vector.reciprocal(sv("sinv", f), sv("mxc", f)))
                vector.tensor_scalar(sv("dqx", f), sv("mxc", f), 1.0 / 127, None,
                                     ALU.mult)
                vguard(vector, vector.tensor_scalar(sv("s", f), sv("sinv", f),
                                                    127.0, None, ALU.mult))
                vguard(vector, vector.tensor_scalar(sv("scomb", f), sv("rB", f),
                                                    sv("s", f), None, ALU.mult))
                vguard(vector, vector.tensor_scalar(sv("t3", f), bmeanB(f),
                                                    sv("scomb", f), -1.0,
                                                    ALU.mult, ALU.mult))
                vguard(vector, vector.tensor_tensor(sv("bcomb", f), sv("t3", f),
                                                    kap[:], ALU.add))
                vector.tensor_scalar(sv("dqg", f), sv("dqx", f), thmw[0][:, 1:2],
                                     None, ALU.mult)
                vector.tensor_scalar(sv("dqu", f), sv("dqx", f), thmw[1][:, 1:2],
                                     None, ALU.mult)
                v.inc(vector.tensor_scalar(xq32[:], xbB[f % 3][:], sv("scomb", f),
                                           sv("bcomb", f), ALU.mult, ALU.add))
                v_xq[f] = v.n

            @blk.scalar
            def _(scalar, f=f):
                scalar.wait_ge(v.h, v_xq[f])
                if f >= 2:
                    scalar.wait_ge(trp.h, trp_xq[f - 2])
                a.inc(scalar.activation(xqbf[f % 2][:], xq32[:], ACTF.Copy,
                                        bias=-128.0, scale=1.0))
                a_xqbf[f] = a.n

            @blk.sync
            def _(sync, f=f):
                sync.wait_ge(a.h, a_xqbf[f])
                if f >= 2 and t_gu[f - 2][NI - 1]:
                    sync.wait_ge(t.h, t_gu[f - 2][NI - 1])
                sync.wait_ge(trp.h, trp.n)
                trp.inc(sync.dma_start_transpose(
                    xqT[f % 2][:].rearrange("p (k f) -> p k f", k=8),
                    xqbf[f % 2][:].rearrange("p (k f) -> p k f", k=8)), 16)
                trp_xq[f] = trp.n

        def emit_hqbf(k):
            @blk.scalar
            def _(scalar, k=k):
                scalar.wait_ge(v.h, v_hq[k])
                if k >= 1:
                    scalar.wait_ge(trp.h, trp_hq[k - 1])
                a.inc(scalar.activation(hqbf[:, 0:2048], hq32[:, 0:2048],
                                        ACTF.Copy, bias=-128.0, scale=1.0))
                a_hqbf[k] = a.n

            @blk.vector
            def _(vector, k=k):
                if k >= 1:
                    vector.wait_ge(trp.h, trp_hq[k - 1])
                v.inc(vector.tensor_scalar(hqbf[:, 2048:INNER],
                                           hq32[:, 2048:INNER], -128.0, None,
                                           ALU.add))
                v_hqbf2[k] = v.n

        def emit_b_prol():
            @blk.sync
            def _(sync):
                for j in (0, 1):
                    if j < nt:
                        sync.wait_ge(sst.h, st_xp[j])
                        bldm[j % 3].inc(sync.dma_start(
                            out=xbB[j % 3][:],
                            in_=xp_h[TT * j:TT * (j + 1), :]), 16)
                        ld2[j] = bldm[j % 3].n
            front(0)

        def emit_b_iter(k):
            for j in range(NI):
                @blk.tensor
                def _(tensor, k=k, j=j):
                    tensor.wait_ge(trp.h, trp_xq[k])
                    if k == 0 and j == 0:
                        tensor.wait_ge(v.h, g_stt[1][WMAT[1][2] - 1])
                    pj, qi = (j - 2, k) if j >= 2 else (NI - 2 + j, k - 1)
                    if qi >= 0 and a_silu[qi][pj]:
                        tensor.wait_ge(a.h, a_silu[qi][pj])
                        tensor.wait_ge(v.h, v_hstt[qi][pj])
                    for d in range(8):
                        base = INNER * d + 512 * j
                        pewait(tensor)
                        t.inc(tensor.matmul(P512[j % 2][:],
                                            xqT[k % 2][:, 128 * d:128 * (d + 1)],
                                            wq_g[:, base:base + 512],
                                            start=(d == 0), stop=(d == 7)))
                        pewait(tensor)
                        t.inc(tensor.matmul(P512[2 + j % 2][:],
                                            xqT[k % 2][:, 128 * d:128 * (d + 1)],
                                            wq_u[:, base:base + 512],
                                            start=(d == 0), stop=(d == 7)))
                    t_gu[k][j] = t.n

                @blk.scalar
                def _(scalar, k=k, j=j):
                    scalar.wait_ge(t.h, t_gu[k][j])
                    if j >= 2:
                        scalar.wait_ge(v.h, v_hstt[k][j - 2])
                    elif k >= 1:
                        scalar.wait_ge(v.h, v_hstt[k - 1][NI - 2 + j])
                    a.inc(scalar.activation(gtmp[j % 2][:], P512[j % 2][:],
                                            ACTF.Silu, bias=0.0,
                                            scale=sv("dqg", k)))
                    a_silu[k][j] = a.n

                @blk.vector
                def _(vector, k=k, j=j):
                    vector.wait_ge(a.h, a_silu[k][j])
                    if k == 0 and j == 0:
                        vector.wait_ge(a.h, a_abs_last[0])
                    v.inc(vector.scalar_tensor_tensor(
                        hbuf[:, 512 * j:512 * (j + 1)], P512[2 + j % 2][:],
                        sv("dqu", k), gtmp[j % 2][:], ALU.mult, ALU.mult))
                    v_hstt[k][j] = v.n

            @blk.vector
            def _(vector, k=k):
                vector.wait_ge(v.h, v_hstt[k][NI - 1])
                for j in range(NI):
                    ins = vector.tensor_reduce(mxh8[k % 2][:, j:j + 1],
                                               hbuf[:, 512 * j:512 * (j + 1)], AX,
                                               ALU.max, apply_absolute_value=True)
                    if j == NI - 1:
                        vguard(vector, ins)
                vguard(vector, vector.tensor_reduce(sv("mxh", k), mxh8[k % 2][:],
                                                    AX, ALU.max))
                vguard(vector, vector.tensor_scalar(sv("shy", k), sv("mxh", k),
                                                    1e-5, None, ALU.max))
                vguard(vector, vector.reciprocal(sv("shinv", k), sv("shy", k)))
                vguard(vector, vector.tensor_scalar(sv("sh", k), sv("shinv", k),
                                                    127.0, None, ALU.mult))
                vguard(vector, vector.tensor_scalar(sv("dqh", k), sv("shy", k),
                                                    1.0 / 127, None, ALU.mult))
                vector.tensor_scalar(sv("dqd", k), sv("dqh", k), thmw[2][:, 1:2],
                                     None, ALU.mult)
                if k >= 1:
                    vector.wait_ge(a.h, a_hqbf[k - 1])
                v.inc(vector.tensor_scalar(hq32[:], hbuf[:], sv("sh", k), kap[:],
                                           ALU.mult, ALU.add))
                v_hq[k] = v.n

            if k >= 1:
                emit_hq_down(k - 1)
            emit_hqbf(k)

            j = k + 2
            if 2 <= j < nt:
                @blk.sync
                def _(sync, k=k, j=j):
                    sync.wait_ge(sst.h, st_xp[j])
                    if k >= 1:
                        sync.wait_ge(sstB.h, st_out[k - 1])
                    bldm[j % 3].inc(sync.dma_start(
                        out=xbB[j % 3][:],
                        in_=xp_h[TT * j:TT * (j + 1), :]), 16)
                    ld2[j] = bldm[j % 3].n
            if k + 1 < nt:
                front(k + 1)

        @blk.sync
        def _(sync):
            for j in (0, 1):
                if j < nt:
                    sync.wait_ge(v.h, v_const)
                    xl[j % 2].inc(sync.dma_start(
                        out=xb[j][:], in_=x_h[TT * j:TT * (j + 1), :]), 16)
                    ld_x[j] = xl[j % 2].n

        for i in range(nt):
            if phase_b and OFF_eff < nt:
                if i == OFF_eff - 2:
                    emit_b_prol()
                if i - OFF_eff >= 0:
                    emit_b_iter(i - OFF_eff)

            @blk.vector
            def _(vector, i=i):
                vector.wait_ge(xl[i % 2].h, ld_x[i])
                vector.bn_stats(bn6[:, 0:6], xb[i % 2][:, 0:512])
                vguard(vector, vector.bn_stats(bn6[:, 6:12], xb[i % 2][:, 512:D]))
                v.inc(vector.bn_aggr(bmv(i), bn6[:]))
                v_var[i] = v.n

            @blk.scalar
            def _(scalar, i=i):
                scalar.wait_ge(v.h, v_var[i])
                a.inc(scalar.activation(sv("sd", i), bvar(i), ACTF.Sqrt,
                                        bias=ceps[:], scale=1.0))
                a_sd[i] = a.n

            @blk.vector
            def _(vector, i=i):
                vector.wait_ge(a.h, a_sd[i])
                vguard(vector, vector.reciprocal(sv("r", i), sv("sd", i)))
                v.inc(vector.tensor_scalar(sv("nmr", i), bmean(i), sv("r", i),
                                           -1.0, ALU.mult, ALU.mult))
                v_ln[i] = v.n

            @blk.scalar
            def _(scalar, i=i):
                scalar.wait_ge(v.h, v_ln[i])
                if i >= 2:
                    scalar.wait_ge(t.h, t_mix[i - 2])
                a.inc(scalar.activation(ybf[i % 2][:], xb[i % 2][:], ACTF.Identity,
                                        bias=sv("nmr", i), scale=sv("r", i)))
                a_ybf[i] = a.n

            tile_done[i] = (a, a_ybf[i])
            if astage < 2:
                continue

            @blk.sync
            def _(sync, i=i):
                sync.wait_ge(a.h, a_ybf[i])
                if i >= 2:
                    sync.wait_ge(t.h, t_log[i - 2])
                sync.wait_ge(trp.h, trp.n)
                trp.inc(sync.dma_start_transpose(
                    yT[i % 2][:].rearrange("p (k f) -> p k f", k=8),
                    ybf[i % 2][:].rearrange("p (k f) -> p k f", k=8)), 16)
                trp_y[i] = trp.n
            tile_done[i] = (a, a_yev[i])
            if astage < 3:
                continue

            @blk.tensor
            def _(tensor, i=i):
                tensor.wait_ge(trp.h, trp_y[i])
                if i >= 1:
                    tensor.wait_ge(v.h, v_lb[i - 1])
                for d in range(ND):
                    pewait(tensor)
                    t.inc(tensor.matmul(pl, yT[i % 2][:, 128 * d:128 * (d + 1)],
                                        wtb[:, KD * d:KD * (d + 1)],
                                        start=(d == 0), stop=(d == ND - 1)))
                t_log[i] = t.n

            @blk.vector
            def _(vector, i=i):
                vector.wait_ge(t.h, t_log[i])
                if i >= 2:
                    vector.wait_ge(a.h, a_exp[i - 2])
                vguard(vector, vector.tensor_tensor(lbs[i % 2][:], pl, bbc[:],
                                                    ALU.add))
                v_lb[i] = v.n
                vguard(vector, vector.tensor_reduce(sv("mx", i), lbs[i % 2][:], AX,
                                                    ALU.max))
                v.inc(vector.tensor_scalar(sv("nmx", i), sv("mx", i), -1.0, None,
                                           ALU.mult))
                v_nmx[i] = v.n

            tile_done[i] = (v, v_nmx[i])
            if astage < 4:
                continue

            @blk.scalar
            def _(scalar, i=i):
                scalar.wait_ge(v.h, v_nmx[i])
                a.inc(scalar.activation(ews[i % 2][:], lbs[i % 2][:], ACTF.Exp,
                                        bias=sv("nmx", i), scale=1.0,
                                        accum_out=sv("esum", i)))
                a_exp[i] = a.n

            @blk.vector
            def _(vector, i=i):
                vector.wait_ge(a.h, a_exp[i])
                v.inc(vector.reciprocal(sv("rsum", i), sv("esum", i)))
                v_rsum[i] = v.n

            @blk.scalar
            def _(scalar, i=i):
                scalar.wait_ge(v.h, v_rsum[i])
                if i >= 2:
                    scalar.wait_ge(sw[i % 2].h, skw_n[i - 2])
                a.inc(scalar.activation(wns[i % 2][:], ews[i % 2][:], ACTF.Copy,
                                        bias=0.0, scale=sv("rsum", i)))
                a_wn[i] = a.n

            tile_done[i] = (a, a_wn[i])
            if astage < 5:
                continue

            @blk.sync
            def _(sync, i=i):
                sync.wait_ge(a.h, a_wn[i])
                if i >= 2:
                    sync.wait_ge(sr[i % 2].h, skr_n[i - 2])
                sw[i % 2].inc(sync.dma_start(
                    out=stage_h[i % 2, :, SKB:SKB + KD], in_=wns[i % 2][:]), 16)
                skw_n[i] = sw[i % 2].n

            @blk.gpsimd
            def _(gpsimd, i=i):
                gpsimd.wait_ge(sw[i % 2].h, skw_n[i])
                if i >= 2:
                    gpsimd.wait_ge(trp.h, trp_at[i - 2])
                src = AP(stage_h.tensor, (i % 2) * 128 * STP + SKB,
                         [[STP - 1, 128], [1, 160]])
                sr[i % 2].inc(gpsimd.dma_start(out=skbb[i % 2][:, 0:160],
                                               in_=src), 16)
                skr_n[i] = sr[i % 2].n

            tile_done[i] = (sr[i % 2], skr_n[i])
            if astage < 6:
                continue

            @blk.sync
            def _(sync, i=i):
                sync.wait_ge(sr[i % 2].h, skr_n[i])
                if i >= 2:
                    sync.wait_ge(t.h, t_mix[i - 2])
                sync.wait_ge(trp.h, trp.n)
                trp.inc(sync.dma_start_transpose(
                    ATs[i % 2][:].rearrange("p (k f) -> p k f", k=2),
                    skbb[i % 2][:].rearrange("p (k f) -> p k f", k=2)), 16)
                trp_at[i] = trp.n
            tile_done[i] = (trp, trp_at[i])
            if astage < 7:
                continue

            if i >= 1:
                for half in range(2):
                    @blk.tensor
                    def _(tensor, i=i, half=half, pv=mix_pe[0]):
                        q = i - 1
                        cs = slice(512 * half, 512 * (half + 1))
                        if half == 0:
                            tensor.wait_ge(trp.h, trp_at[q])
                            tensor.wait_ge(a.h, a_ybf[i])
                        if pv:
                            tensor.wait_ge(v.h, pv)
                        pewait(tensor)
                        t.inc(tensor.matmul(PMIXH[:], A1(q), ybf[q % 2][:, cs],
                                            start=True, stop=False))
                        pewait(tensor)
                        t.inc(tensor.matmul(PMIXH[:], A2(q),
                                            ybf[i % 2][0:32, cs], start=False,
                                            stop=True))
                    if half == 1:
                        t_mix[i - 1] = t.n
                    tmark = t.n

                    @blk.vector
                    def _(vector, i=i, half=half, tmark=tmark):
                        q = i - 1
                        cs = slice(512 * half, 512 * (half + 1))
                        vector.wait_ge(t.h, tmark)
                        if half == 0 and q >= 2 and st_xp[q - 2]:
                            vector.wait_ge(sst.h, st_xp[q - 2])
                        v.inc(vector.scalar_tensor_tensor(
                            xpr[q % 2][:, cs], PMIXH[:], 1.0, xb[q % 2][:, cs],
                            ALU.mult, ALU.add))
                        if half == 1:
                            v_mst[q] = v.n
                    mix_pe[0] = v.n

                @blk.scalar
                def _(scalar, i=i):
                    scalar.wait_ge(v.h, v_mst[i - 1])
                    sst.inc(scalar.dma_start(out=xp_h[TT * (i - 1):TT * i, :],
                                             in_=xpr[(i - 1) % 2][:]), 16)
                    st_xp[i - 1] = sst.n

            j = i + 1
            if 2 <= j < nt:
                @blk.sync
                def _(sync, i=i, j=j):
                    if astage >= 7:
                        sync.wait_ge(v.h, v_mst[j - 2])
                    elif tile_done[j - 2] is not None:
                        s_, n_ = tile_done[j - 2]
                        sync.wait_ge(s_.h, n_)
                    xl[j % 2].inc(sync.dma_start(
                        out=xb[j % 2][:], in_=x_h[TT * j:TT * (j + 1), :]), 16)
                    ld_x[j] = xl[j % 2].n

            if prep:
                for entry in sched[i]:
                    emit_prep(entry)

        _ = None
        if astage < 7:
            @blk.vector
            def _(vector):
                s_, n_ = tile_done[nt - 1]
                vector.wait_ge(s_.h, n_)
            return nc

        for half in range(2):
            @blk.tensor
            def _(tensor, half=half, pv=mix_pe[0]):
                q = nt - 1
                cs = slice(512 * half, 512 * (half + 1))
                if half == 0:
                    tensor.wait_ge(trp.h, trp_at[q])
                if pv:
                    tensor.wait_ge(v.h, pv)
                pewait(tensor)
                t.inc(tensor.matmul(PMIXH[:], A1(q), ybf[q % 2][:, cs],
                                    start=True, stop=True))
            if half == 1:
                t_mix[nt - 1] = t.n
            tmark = t.n

            @blk.vector
            def _(vector, half=half, tmark=tmark):
                q = nt - 1
                cs = slice(512 * half, 512 * (half + 1))
                vector.wait_ge(t.h, tmark)
                if half == 0 and q >= 2 and st_xp[q - 2]:
                    vector.wait_ge(sst.h, st_xp[q - 2])
                v.inc(vector.scalar_tensor_tensor(
                    xpr[q % 2][:, cs], PMIXH[:], 1.0, xb[q % 2][:, cs],
                    ALU.mult, ALU.add))
                if half == 1:
                    v_mst[q] = v.n
            mix_pe[0] = v.n

        @blk.scalar
        def _(scalar):
            i = nt - 1
            scalar.wait_ge(v.h, v_mst[i])
            sst.inc(scalar.dma_start(out=xp_h[TT * i:TT * (i + 1), :],
                                     in_=xpr[i % 2][:]), 16)
            st_xp[i] = sst.n


        if dbg == 2:
            dbgs = [("ybf", ybf[0], BF16), ("yT", yT[0], BF16),
                    ("lb", lbs[0], F32), ("ew", ews[0], F32), ("wn", wns[0], F32),
                    ("skb", skbb[0], BF16), ("AT", ATs[0], BF16),
                    ("stats", STbig, F32)]
        elif dbg == 3:
            dbgs = [("thmw0", thmw[0], F32), ("thmw1", thmw[1], F32),
                    ("thmw2", thmw[2], F32), ("kap", kap, F32),
                    ("wqg", wq_g[:, 0:INNER], FP8), ("wqd", wq_d[:, 0:D], FP8),
                    ("xqbf", xqbf[0], BF16), ("xqT", xqT[0], BF16),
                    ("hqbf", hqbf, BF16), ("stats", STbig, F32)]
            dbg_h = {nm: nc.dram_tensor("dbg_" + nm, list(tt.shape), dt,
                                        kind="ExternalOutput").ap()
                     for nm, tt, dt in dbgs}

            if dbg == 2:
                @blk.sync
                def _(sync):
                    sync.wait_ge(sst.h, sst.n)
                    nst = 0
                    for nm, tt, dt in dbgs:
                        sync.dma_start(out=dbg_h[nm], in_=tt[:]).then_inc(skw.h, 16)
                        nst += 16
                    sync.wait_ge(skw.h, skw.n + nst)
                    skw.n += nst

        if not phase_b:
            @blk.scalar
            def _(scalar):
                scalar.wait_ge(sst.h, sst.n)
            return nc
        if OFF_eff >= nt:
            emit_b_prol()
        for k in range(max(0, nt - OFF_eff), nt):
            emit_b_iter(k)
        emit_hq_down(nt - 1)

        if dbg == 3:
            @blk.sync
            def _(sync):
                sync.wait_ge(sst.h, sst.n)
                nst = 0
                for nm, tt, dt in dbgs:
                    sync.dma_start(out=dbg_h[nm], in_=tt[:]).then_inc(skw.h, 16)
                    nst += 16
                sync.wait_ge(skw.h, skw.n + nst)
                skw.n += nst

        @blk.scalar
        def _(scalar):
            scalar.wait_ge(sst.h, sst.n)
            scalar.wait_ge(sstB.h, sstB.n)

    return nc


def kernel(L0_attn_x, L0_attn_W, L0_attn_b, L0_ffn_w_gate, L0_ffn_w_up,
           L0_ffn_w_down):
    x = np.ascontiguousarray(np.asarray(L0_attn_x, dtype=np.float32))
    Wt = np.ascontiguousarray(np.asarray(L0_attn_W, dtype=np.float32).T)
    b_bc = np.ascontiguousarray(
        np.tile(np.asarray(L0_attn_b, dtype=np.float32)[None, :], (128, 1)))
    def retile(wT):
        return np.ascontiguousarray(
            wT.reshape(1024, 4, 1024).swapaxes(0, 1).reshape(4096, 1024))
    wgT = retile(np.asarray(L0_ffn_w_gate, dtype=np.float32).T)
    wuT = retile(np.asarray(L0_ffn_w_up, dtype=np.float32).T)
    wdT = np.ascontiguousarray(np.asarray(L0_ffn_w_down, dtype=np.float32).T)
    ident = np.eye(128, dtype=ml_dtypes.bfloat16)
    nc = build_nc()
    in_maps = [{"x": x[c], "Wt": Wt, "b_bc": b_bc, "wgT": wgT, "wuT": wuT,
                "wdT": wdT, "ident": ident} for c in range(8)]
    res = run_bass_kernel_spmd(nc, in_maps, core_ids=list(range(8)))
    return np.stack([res.results[c]["out"] for c in range(8)], axis=0)
